# version 19
# baseline (speedup 1.0000x reference)
from contextlib import ExitStack

import numpy as np
import ml_dtypes

import concourse.bass as bass
import concourse.mybir as mybir
from concourse.bass_utils import run_bass_kernel_spmd

F32 = mybir.dt.float32
BF16 = mybir.dt.bfloat16
AF = mybir.ActivationFunctionType
ALU = mybir.AluOpType
AX = mybir.AxisListType

DM = 1024
DH = 4096
NE = 8
DC = DM // 128
HC = DH // 128
NEG = -1.0e30
EPS = 1e-5
SCL = 1.0 / 32.0


class Ctx:

    def __init__(self, nc):
        self.nc = nc
        self.sems = {}
        self.counts = {}
        for n in ("t", "v", "s"):
            self.new(n)

    def new(self, n):
        if n not in self.sems:
            self.sems[n] = self.nc.alloc_semaphore(n)
            self.counts[n] = 0

    def inc(self, ins, name, n=1):
        ins.then_inc(self.sems[name], n)
        self.counts[name] += n
        return self.counts[name]

    def wait(self, eng, name, val=None):
        v = self.counts[name] if val is None else val
        if v > 0:
            eng.wait_ge(self.sems[name], v)
        return v

    def join(self, engines):
        for eng in engines:
            for n in self.sems:
                self.wait(eng, n)


def build_nc(S=4096, n_rounds=4, taps=()):
    Q = S // n_rounds
    QT = Q // 128
    TB = min(512, Q)
    NTB = Q // TB
    KG = (Q + 511) // 512
    JT = max(1, -(-3 * Q // (8 * 128)))
    CAP = JT * 128
    SCB = min(256, Q)
    NSC = Q // SCB

    nc = bass.Bass("TRN2", target_bir_lowering=False)
    c = Ctx(nc)
    for _sn in ("dc", "dcg", "dct", "dwa", "dwq", "dwk", "dwv", "dx0", "dx1",
                "dx2", "dx3", "dx32", "dw1_0", "dw1_1", "dw1_2", "dw1s_0", "dw1s_1", "dw1s_2",
                "dwbW_0", "dwbW_1", "dwbW_2",
                "dw2_0", "dw2_1", "dw2_2", "dw2s_0", "dw2s_1", "dw2s_2",
                "dwb_0", "dwb_1", "dwb_2", "dy0", "dy1", "dkc", "dkw"):
        c.new(_sn)

    x_h = nc.dram_tensor("x", [S, DM], F32, kind="ExternalInput").ap()
    wq_h = nc.dram_tensor("wq", [DM, DM], F32, kind="ExternalInput").ap()
    wk_h = nc.dram_tensor("wk", [DM, DM], F32, kind="ExternalInput").ap()
    wv_h = nc.dram_tensor("wv", [DM, DM], F32, kind="ExternalInput").ap()
    wo_h = nc.dram_tensor("wo", [DM, DM], F32, kind="ExternalInput").ap()
    wr_h = nc.dram_tensor("wr", [DM, NE], F32, kind="ExternalInput").ap()
    w1_h = nc.dram_tensor("w1", [NE, DM, DH], F32, kind="ExternalInput").ap()
    b1_h = nc.dram_tensor("b1", [NE, DH], F32, kind="ExternalInput").ap()
    w2_h = nc.dram_tensor("w2", [NE, DH, DM], F32, kind="ExternalInput").ap()
    b2_h = nc.dram_tensor("b2", [NE, DM], F32, kind="ExternalInput").ap()
    y_h = nc.dram_tensor("y", [S, DM], F32, kind="ExternalOutput").ap()
    w1b_h = nc.dram_tensor("w1b", [NE, DM, DH], BF16, kind="Internal").ap()
    w2b_h = nc.dram_tensor("w2b", [NE, DH, DM], BF16, kind="Internal").ap()
    wvob_h = nc.dram_tensor("wvob", [DM, DM], BF16, kind="Internal").ap()
    kc_h = nc.dram_tensor("kcache", [n_rounds, 128, DC, Q], BF16, kind="Internal").ap()
    vc_h = nc.dram_tensor("vcache", [n_rounds, 128, QT, DM], BF16, kind="Internal").ap()
    tap_h = {}
    for tn, shp in (("t_r", [DM, S]), ("t_x1", [DM, S]), ("t_w", [NE, S]),
                    ("t_acc", [DM, S])):
        if tn in taps:
            tap_h[tn] = nc.dram_tensor(tn, shp, F32, kind="ExternalOutput").ap()

    eye16_c = nc.inline_tensor(np.eye(128, dtype=ml_dtypes.bfloat16), "eye16c")
    eye32_c = nc.inline_tensor(np.eye(128, dtype=np.float32), "eye32c")
    tri = np.where(np.arange(128)[None, :] <= np.arange(128)[:, None], 0.0, NEG)
    tri_c = nc.inline_tensor(tri.astype(np.float32), "tric")
    ones_col_c = nc.inline_tensor(np.ones((128, 1), dtype=ml_dtypes.bfloat16), "onescolc")
    ones_row_c = nc.inline_tensor(np.ones((1, 128), dtype=np.float32), "onesrowc")
    sel = np.zeros((NE, NE * 128), dtype=np.float32)
    for e in range(NE):
        sel[e, e * 128:(e + 1) * 128] = 1.0
    sel_c = nc.inline_tensor(sel, "selc")
    jrow_c = nc.inline_tensor(
        (np.arange(CAP, dtype=np.float32) + 1.0)[None, :].repeat(128, 0), "jrowc")
    jcol_c = nc.inline_tensor(
        (np.arange(JT * 128, dtype=np.float32) + 1.0).reshape(JT, 128).T.copy(),
        "jcolc")
    eps_c = nc.inline_tensor(np.full((1, 1), EPS, dtype=np.float32), "epsc")
    eps128_c = nc.inline_tensor(np.full((128, 1), EPS, dtype=np.float32), "eps128c")
    triu_c = nc.inline_tensor(
        np.triu(np.ones((128, 128))).astype(ml_dtypes.bfloat16), "triuc")

    ALL_ENG = (nc.tensor, nc.vector, nc.scalar, nc.gpsimd, nc.sync)

    with ExitStack() as P:
        acc = P.enter_context(nc.sbuf_tensor("acc", [128, DC, Q], F32))
        x1q = P.enter_context(nc.sbuf_tensor("x1q", [128, DC, Q], BF16))
        eye16 = P.enter_context(nc.sbuf_tensor("eye16", [128, 128], BF16))
        eye32 = P.enter_context(nc.sbuf_tensor("eye32", [128, 128], F32))
        trim = P.enter_context(nc.sbuf_tensor("trim", [128, 128], F32))
        ones_col = P.enter_context(nc.sbuf_tensor("ones_col", [128, 1], BF16))
        ones_row = P.enter_context(nc.sbuf_tensor("ones_row", [1, 128], F32))
        sel_sb = P.enter_context(nc.sbuf_tensor("sel_sb", [NE, NE * 128], F32))
        b1T = P.enter_context(nc.sbuf_tensor("b1T", [128, HC, NE], F32))
        b2T = P.enter_context(nc.sbuf_tensor("b2T", [128, DC, NE], F32))
        wr_sb = P.enter_context(nc.sbuf_tensor("wr_sb", [128, DC, NE], BF16))
        m_run = P.enter_context(nc.sbuf_tensor("m_run", [128, QT], F32))
        l_run = P.enter_context(nc.sbuf_tensor("l_run", [128, QT], F32))
        sm0 = P.enter_context(nc.sbuf_tensor("sm0", [128, 8], F32))
        sm1 = P.enter_context(nc.sbuf_tensor("sm1", [128, 8], F32))
        epsb = P.enter_context(nc.sbuf_tensor("epsb", [1, 1], F32))
        eps128 = P.enter_context(nc.sbuf_tensor("eps128", [128, 1], F32))
        bst = P.enter_context(nc.sbuf_tensor("bst", [128, 12], F32))
        mv = P.enter_context(nc.sbuf_tensor("mv", [128, 2], F32))
        triu_sb = P.enter_context(nc.sbuf_tensor("triu_sb", [128, 128], BF16))
        jrow = P.enter_context(nc.sbuf_tensor("jrow", [128, CAP], F32))
        jcol = P.enter_context(nc.sbuf_tensor("jcol", [128, JT], F32))
        x1_tm = P.enter_context(nc.sbuf_tensor("x1_tm", [128, QT, DM], BF16))

        with ExitStack() as PH:
            b1raw = PH.enter_context(nc.sbuf_tensor("b1raw", [NE, DH], F32))
            b2raw = PH.enter_context(nc.sbuf_tensor("b2raw", [NE, DM], F32))
            wv_tmp = PH.enter_context(nc.sbuf_tensor("wv_tmp", [128, DC, DM], BF16))
            wvT_tmp = PH.enter_context(nc.sbuf_tensor("wvT_tmp", [128, DC, DM], BF16))
            wvo_pre = PH.enter_context(nc.sbuf_tensor("wvo_pre", [128, DC, DM], BF16))
            psP = PH.enter_context(nc.psum_tensor("psP", [128, 512], F32))
            psP_T = PH.enter_context(nc.psum_tensor("psP_T", [128, 1024], BF16))

            for dst, src in ((eye16, eye16_c), (eye32, eye32_c), (trim, tri_c),
                             (ones_col, ones_col_c), (ones_row, ones_row_c),
                             (sel_sb, sel_c), (epsb, eps_c), (triu_sb, triu_c),
                             (jrow, jrow_c), (jcol, jcol_c), (eps128, eps128_c)):
                c.inc(nc.sync.dma_start(out=dst[:], in_=src.ap()), "dc", 16)
            c.inc(nc.sync.dma_start(out=b1raw[:], in_=b1_h), "dc", 16)
            c.inc(nc.sync.dma_start(out=b2raw[:], in_=b2_h), "dc", 16)
            c.inc(nc.gpsimd.dma_start(
                out=wr_sb[:], in_=wr_h.rearrange("(dc p) e -> p dc e", p=128)), "dwa", 16)
            c.wait(nc.tensor, "dc")
            for hc in range(HC):
                tok_t = c.inc(nc.tensor.transpose(
                    psP[:, 0:NE], b1raw[0:NE, hc * 128:(hc + 1) * 128],
                    eye32[0:NE, 0:NE]), "t")
                c.wait(nc.vector, "t", tok_t)
                c.inc(nc.vector.tensor_copy(out=b1T[:, hc, :], in_=psP[:, 0:NE]), "v")
                c.wait(nc.tensor, "v")
            for dc in range(DC):
                tok_t = c.inc(nc.tensor.transpose(
                    psP[:, 8:8 + NE], b2raw[0:NE, dc * 128:(dc + 1) * 128],
                    eye32[0:NE, 0:NE]), "t")
                c.wait(nc.vector, "t", tok_t)
                c.inc(nc.vector.tensor_copy(out=b2T[:, dc, :], in_=psP[:, 8:8 + NE]), "v")
                c.wait(nc.tensor, "v")

            tok_wv = c.inc(nc.gpsimd.dma_start(
                out=wv_tmp[:], in_=wv_h.rearrange("(dc p) n -> p dc n", p=128)), "dwa", 16)
            c.wait(nc.tensor, "dwa", tok_wv)
            for dc in range(DC):
                for dvc in range(DC):
                    c.inc(nc.tensor.transpose(
                        psP_T[:, dvc * 128:(dvc + 1) * 128],
                        wv_tmp[:, dc, dvc * 128:(dvc + 1) * 128], eye16[:]), "t")
                c.wait(nc.vector, "t")
                c.inc(nc.vector.tensor_copy(
                    out=wvT_tmp[:, dc, :], in_=psP_T[:, 0:DM]), "v")
                c.wait(nc.tensor, "v")
            c.wait(nc.gpsimd, "t")
            tok_wo = c.inc(nc.gpsimd.dma_start(
                out=wv_tmp[:], in_=wo_h.rearrange("(dc p) n -> p dc n", p=128)), "dwa", 16)
            c.wait(nc.tensor, "dwa", tok_wo)
            for dc in range(DC):
                for half in range(DM // 512):
                    for dvc in range(DC):
                        ins = nc.tensor.matmul(
                            psP[:, 0:512],
                            wvT_tmp[:, dc, dvc * 128:(dvc + 1) * 128],
                            wv_tmp[:, dvc, half * 512:half * 512 + 512],
                            start=(dvc == 0), stop=(dvc == DC - 1))
                    tok_t = c.inc(ins, "t")
                    c.wait(nc.vector, "t", tok_t)
                    c.inc(nc.vector.tensor_copy(
                        out=wvo_pre[:, dc, half * 512:half * 512 + 512],
                        in_=psP[:, 0:512]), "v")
                    c.wait(nc.tensor, "v")
            c.wait(nc.sync, "v")
            c.inc(nc.sync.dma_start(
                out=wvob_h.rearrange("(dc p) n -> p dc n", p=128),
                in_=wvo_pre[:]), "dc", 16)
            c.join(ALL_ENG)

        for r in range(n_rounds):
            q0 = r * Q

            with ExitStack() as PA:
                sb = lambda nm, shp, dt: PA.enter_context(
                    nc.sbuf_tensor(f"{nm}_r{r}", shp, dt))
                psb = lambda nm, shp, dt: PA.enter_context(
                    nc.psum_tensor(f"{nm}_r{r}", shp, dt))
                q_fm = sb("q_fm", [128, DC, Q], BF16)
                k_blk = sb("k_blk", [128, DC, Q], BF16)
                v_blk = sb("v_blk", [128, QT, DM], BF16)
                wq_sb = sb("wq_sb", [128, DC, DM], BF16)
                wk_sb = sb("wk_sb", [128, DC, DM], BF16)
                wvo_sb = sb("wvo_sb", [128, DC, DM], BF16)
                xt16 = [sb(f"xt16_{i}", [128, DM], BF16) for i in range(3)]
                xf16 = sb("xf16", [128, DC, 512], BF16)
                xt32 = sb("xt32", [128, DM], F32)
                rtm = sb("rtm", [128, DM], F32)
                rowst = sb("rowst", [1, 128], F32)
                s_row = sb("s_row", [128, Q], BF16)
                p16 = sb("p16", [128, Q], BF16)
                attnT = sb("attnT", [128, QT, 128], BF16)

                psA_s = psb("psA_s", [128, 512], F32)
                psA_s2 = psb("psA_s2", [128, 512], F32)
                psA_pv = [psb(f"psA_pv{i}", [128, 1024], F32) for i in range(2)]
                psA_T = psb("psA_T", [128, 1024], BF16)
                psA_m = psb("psA_m", [128, 512], F32)

                def load_w_fm(dst, src, sem):
                    return c.inc(nc.gpsimd.dma_start(
                        out=dst[:], in_=src.rearrange("(dc p) n -> p dc n", p=128)),
                        sem, 16)

                tok_wq = load_w_fm(wq_sb, wq_h, "dwq")
                tok_wk = load_w_fm(wk_sb, wk_h, "dwk")
                tok_wv = c.inc(nc.sync.dma_start(
                    out=wvo_sb[:],
                    in_=wvob_h.rearrange("(dc p) n -> p dc n", p=128)), "dwv", 16)

                c.wait(nc.tensor, "dwq", tok_wq)
                xb_tok = [0, 0, 0, 0]
                sbank_tok = [0, 0]
                for g in range(KG):
                    gn = min(512, Q - g * 512)
                    gt = gn // 128
                    for i in range(gt):
                        t0 = q0 + g * 512 + i * 128
                        buf = xt16[i % 3]
                        dsem = f"dx{i % 3}"
                        c.wait(nc.gpsimd, "t", xb_tok[i % 3])
                        tok_d = c.inc(nc.gpsimd.dma_start(
                            out=buf[:], in_=x_h[t0:t0 + 128, :]), dsem, 16)
                        c.wait(nc.tensor, dsem, tok_d)
                        for dcc in range(DC):
                            c.inc(nc.tensor.transpose(
                                psA_T[:, dcc * 128:(dcc + 1) * 128],
                                buf[:, dcc * 128:(dcc + 1) * 128], eye16[:]), "t")
                        xb_tok[i % 3] = c.counts["t"]
                        c.wait(nc.vector, "t")
                        c.inc(nc.vector.tensor_copy(
                            out=xf16[:, :, i * 128:(i + 1) * 128],
                            in_=psA_T[:, 0:DM].rearrange("p (a b) -> p a b", b=128)), "v")
                        c.wait(nc.tensor, "v")
                    for kc in range(DC):
                        for dcc in range(DC):
                            ins = nc.tensor.matmul(
                                psA_s[:, 0:gn], wq_sb[:, dcc, kc * 128:(kc + 1) * 128],
                                xf16[:, dcc, 0:gn],
                                start=(dcc == 0), stop=(dcc == DC - 1))
                        tok_t = c.inc(ins, "t")
                        c.wait(nc.vector, "t", tok_t)
                        c.inc(nc.vector.tensor_copy(
                            out=q_fm[:, kc, g * 512:g * 512 + gn], in_=psA_s[:, 0:gn]), "v")
                        c.wait(nc.tensor, "v")

                for kb in range(r + 1):
                    kb0 = kb * Q
                    if kb < r:
                        c.wait(nc.sync, "t")
                        tok_kc = c.inc(nc.sync.dma_start(
                            out=k_blk[:], in_=kc_h[kb]), "dkc", 16)
                        tok_vc = c.inc(nc.sync.dma_start(
                            out=v_blk[:], in_=vc_h[kb]), "dkc", 16)
                        c.wait(nc.tensor, "dkc", tok_vc)
                    if kb == r:
                        c.wait(nc.tensor, "dwk", tok_wk)
                        for g in range(KG):
                            gn = min(512, Q - g * 512)
                            gt = gn // 128
                            for i in range(gt):
                                t0 = kb0 + g * 512 + i * 128
                                buf = xt16[i % 2]
                                dsem = f"dx{i % 2}"
                                c.wait(nc.gpsimd, "t", xb_tok[i % 2])
                                tok_d = c.inc(nc.gpsimd.dma_start(
                                    out=buf[:], in_=x_h[t0:t0 + 128, :]), dsem, 16)
                                c.wait(nc.tensor, dsem, tok_d)
                                for dcc in range(DC):
                                    c.inc(nc.tensor.transpose(
                                        psA_T[:, dcc * 128:(dcc + 1) * 128],
                                        buf[:, dcc * 128:(dcc + 1) * 128], eye16[:]), "t")
                                xb_tok[i % 3] = c.counts["t"]
                                c.wait(nc.vector, "t")
                                c.inc(nc.vector.tensor_copy(
                                    out=xf16[:, :, i * 128:(i + 1) * 128],
                                    in_=psA_T[:, 0:DM].rearrange("p (a b) -> p a b", b=128)), "v")
                                c.wait(nc.tensor, "v")
                                c.wait(nc.tensor, "dwv", tok_wv)
                                for half in range(DM // 512):
                                    for dcc in range(DC):
                                        ins = nc.tensor.matmul(
                                            psA_pv[0][:, half * 512:half * 512 + 512],
                                            xf16[:, dcc, i * 128:(i + 1) * 128],
                                            wvo_sb[:, dcc, half * 512:half * 512 + 512],
                                            start=(dcc == 0), stop=(dcc == DC - 1))
                                tok_t = c.inc(ins, "t")
                                c.wait(nc.vector, "t", tok_t)
                                c.inc(nc.vector.tensor_copy(
                                    out=v_blk[:, g * 4 + i, :], in_=psA_pv[0][:, 0:DM]), "v")
                                c.wait(nc.tensor, "v")
                            for kc in range(DC):
                                for dcc in range(DC):
                                    ins = nc.tensor.matmul(
                                        psA_s[:, 0:gn], wk_sb[:, dcc, kc * 128:(kc + 1) * 128],
                                        xf16[:, dcc, 0:gn],
                                        start=(dcc == 0), stop=(dcc == DC - 1))
                                tok_t = c.inc(ins, "t")
                                c.wait(nc.vector, "t", tok_t)
                                c.inc(nc.vector.tensor_copy(
                                    out=k_blk[:, kc, g * 512:g * 512 + gn],
                                    in_=psA_s[:, 0:gn]), "v")
                                c.wait(nc.tensor, "v")

                    if kb == r and r + 1 < n_rounds:
                        c.wait(nc.sync, "v")
                        c.inc(nc.sync.dma_start(out=kc_h[r], in_=k_blk[:]), "dkw", 16)
                        c.inc(nc.sync.dma_start(out=vc_h[r], in_=v_blk[:]), "dkw", 16)
                    diag = (kb == r)
                    for qt in range(QT):
                        tk = (qt + 1) * 128 if diag else Q
                        ng = (tk + 511) // 512
                        for g in range(ng):
                            n0 = g * 512
                            nn = min(512, tk - n0)
                            ps = psA_s if g % 2 == 0 else psA_s2
                            c.wait(nc.tensor, "v", sbank_tok[g % 2])
                            for kc in range(DC):
                                ins = nc.tensor.matmul(
                                    ps[:, 0:nn], q_fm[:, kc, qt * 128:(qt + 1) * 128],
                                    k_blk[:, kc, n0:n0 + nn],
                                    start=(kc == 0), stop=(kc == DC - 1))
                            tok_t = c.inc(ins, "t")
                            c.wait(nc.vector, "t", tok_t)
                            nmask = 128 if (diag and g == ng - 1) else 0
                            if nn - nmask > 0:
                                c.inc(nc.vector.tensor_copy(
                                    out=s_row[:, n0:n0 + nn - nmask],
                                    in_=ps[:, 0:nn - nmask]), "v")
                            if nmask:
                                c.inc(nc.vector.tensor_add(
                                    s_row[:, n0 + nn - 128:n0 + nn],
                                    ps[:, nn - 128:nn], trim[:]), "v")
                            sbank_tok[g % 2] = c.counts["v"]
                        c.wait(nc.vector, "v")
                        c.inc(nc.vector.reduce_max(
                            sm0[:, 0:1], s_row[:, 0:tk], axis=AX.X), "v")
                        mcur = m_run[:, qt:qt + 1]
                        c.wait(nc.vector, "v")
                        if kb == 0:
                            c.inc(nc.vector.tensor_copy(out=mcur, in_=sm0[:, 0:1]), "v")
                        else:
                            c.inc(nc.vector.tensor_tensor(
                                sm1[:, 0:1], mcur, sm0[:, 0:1], op=ALU.max), "v")
                            c.wait(nc.vector, "v")
                            c.inc(nc.vector.tensor_sub(
                                sm0[:, 1:2], mcur, sm1[:, 0:1]), "v")
                            c.wait(nc.vector, "v")
                            c.inc(nc.vector.tensor_copy(out=mcur, in_=sm1[:, 0:1]), "v")
                            c.wait(nc.scalar, "v")
                            c.inc(nc.scalar.activation(
                                out=sm0[:, 2:3], in_=sm0[:, 1:2], func=AF.Exp,
                                scale=SCL), "s")
                        c.wait(nc.vector, "v")
                        c.inc(nc.vector.tensor_scalar_mul(sm0[:, 3:4], mcur, -SCL), "v")
                        c.wait(nc.scalar, "v")
                        c.inc(nc.scalar.activation(
                            out=p16[:, 0:tk], in_=s_row[:, 0:tk], func=AF.Exp,
                            bias=sm0[:, 3:4], scale=SCL, accum_out=sm0[:, 4:5]), "s")
                        c.wait(nc.vector, "s")
                        if kb == 0:
                            c.inc(nc.vector.tensor_copy(
                                out=l_run[:, qt:qt + 1], in_=sm0[:, 4:5]), "v")
                        else:
                            c.inc(nc.vector.scalar_tensor_tensor(
                                out=l_run[:, qt:qt + 1], in0=l_run[:, qt:qt + 1],
                                scalar=sm0[:, 2:3], in1=sm0[:, 4:5],
                                op0=ALU.mult, op1=ALU.add), "v")
                        ntk = tk // 128
                        c.wait(nc.tensor, "s")
                        c.wait(nc.tensor, "v")
                        for tkc in range(ntk):
                            c.inc(nc.tensor.transpose(
                                psA_T[:, tkc * 128:(tkc + 1) * 128],
                                p16[:, tkc * 128:(tkc + 1) * 128], eye16[:]), "t")
                        c.wait(nc.vector, "t")
                        c.inc(nc.vector.tensor_copy(
                            out=attnT[:, 0:ntk, :],
                            in_=psA_T[:, 0:ntk * 128].rearrange(
                                "p (a b) -> p a b", b=128)), "v")
                        pvb = psA_pv[qt % 2]
                        c.wait(nc.tensor, "v")
                        for dvc in range(DC):
                            for tkc in range(ntk):
                                ins = nc.tensor.matmul(
                                    pvb[:, dvc * 128:(dvc + 1) * 128],
                                    v_blk[:, tkc, dvc * 128:(dvc + 1) * 128],
                                    attnT[:, tkc, :],
                                    start=(tkc == 0), stop=(tkc == ntk - 1))
                        tok_pv = c.inc(ins, "t")
                        accs = acc[:, :, qt * 128:(qt + 1) * 128]
                        c.wait(nc.vector, "t", tok_pv)
                        if kb == 0:
                            c.inc(nc.vector.tensor_copy(
                                out=accs,
                                in_=pvb[:, 0:DM].rearrange("p (a b) -> p a b", b=128)), "v")
                        else:
                            c.wait(nc.tensor, "s")
                            tok_t = c.inc(nc.tensor.transpose(
                                psA_m[0:1, 0:128], sm0[:, 2:3], eye32[:]), "t")
                            c.wait(nc.vector, "t", tok_t)
                            c.inc(nc.vector.tensor_copy(
                                out=rowst[0:1, 0:128], in_=psA_m[0:1, 0:128]), "v")
                            c.wait(nc.tensor, "v")
                            tok_t = c.inc(nc.tensor.matmul(
                                psA_m[:, 128:256], ones_row[:], rowst[0:1, 0:128],
                                start=True, stop=True), "t")
                            c.wait(nc.vector, "t", tok_t)
                            for dcc in range(DC):
                                c.inc(nc.vector.tensor_mul(
                                    accs[:, dcc, :], accs[:, dcc, :],
                                    psA_m[:, 128:256]), "v")
                            c.wait(nc.vector, "v")
                            c.inc(nc.vector.tensor_add(
                                accs, accs,
                                pvb[:, 0:DM].rearrange("p (a b) -> p a b", b=128)), "v")
                        c.wait(nc.tensor, "v")

                x1tm_tok = [0] * QT
                for qt in range(QT):
                    t0 = q0 + qt * 128
                    c.wait(nc.sync, "v")
                    tok_d = c.inc(nc.sync.dma_start(
                        out=xt32[:], in_=x_h[t0:t0 + 128, :]), "dx32", 16)
                    c.wait(nc.vector, "v")
                    c.inc(nc.vector.reciprocal(sm0[:, 5:6], l_run[:, qt:qt + 1]), "v")
                    pp = psA_pv[qt % 2]
                    c.wait(nc.tensor, "v")
                    for dcc in range(DC):
                        ins = nc.tensor.transpose(
                            pp[:, dcc * 128:(dcc + 1) * 128],
                            acc[:, dcc, qt * 128:(qt + 1) * 128], eye32[:])
                    tok_t = c.inc(ins, "t")
                    c.wait(nc.vector, "t", tok_t)
                    c.wait(nc.vector, "dx32", tok_d)
                    c.wait(nc.vector, "v")
                    c.inc(nc.vector.scalar_tensor_tensor(
                        out=rtm[:], in0=pp[:, 0:DM], scalar=sm0[:, 5:6],
                        in1=xt32[:], op0=ALU.mult, op1=ALU.add), "v")
                    c.wait(nc.vector, "v")
                    c.inc(nc.vector.bn_stats(bst[:, 0:6], rtm[:, 0:512]), "v")
                    c.inc(nc.vector.bn_stats(bst[:, 6:12], rtm[:, 512:1024]), "v")
                    c.wait(nc.vector, "v")
                    c.inc(nc.vector.bn_aggr(mv[:], bst[:]), "v")
                    c.wait(nc.scalar, "v")
                    c.inc(nc.scalar.activation(
                        out=sm0[:, 6:7], in_=mv[:, 1:2], func=AF.Ln,
                        bias=eps128[:]), "s")
                    c.wait(nc.scalar, "s")
                    c.inc(nc.scalar.activation(
                        out=sm0[:, 7:8], in_=sm0[:, 6:7], func=AF.Exp,
                        scale=-0.5), "s")
                    c.wait(nc.vector, "s")
                    x1tm_tok[qt] = c.inc(nc.vector.tensor_scalar(
                        out=x1_tm[:, qt, :], in0=rtm[:], scalar1=mv[:, 0:1],
                        scalar2=sm0[:, 7:8], op0=ALU.subtract, op1=ALU.mult), "v")
                trev_tok = 0
                for qt in range(QT):
                    c.wait(nc.tensor, "v", max(x1tm_tok[qt], trev_tok))
                    for dcc in range(DC):
                        ins = nc.tensor.transpose(
                            psA_T[:, dcc * 128:(dcc + 1) * 128],
                            x1_tm[:, qt, dcc * 128:(dcc + 1) * 128], eye16[:])
                    tok_t = c.inc(ins, "t")
                    c.wait(nc.vector, "t", tok_t)
                    trev_tok = c.inc(nc.vector.tensor_copy(
                        out=x1q[:, :, qt * 128:(qt + 1) * 128],
                        in_=psA_T[:, 0:DM].rearrange("p (a b) -> p a b", b=128)), "v")
                c.wait(nc.tensor, "v")

                if "t_x1" in tap_h:
                    c.wait(nc.gpsimd, "v")
                    c.inc(nc.gpsimd.dma_start(
                        out=tap_h["t_x1"][:, q0:q0 + Q].rearrange(
                            "(dc p) t -> p dc t", p=128),
                        in_=x1q[:]), "dcg", 16)
                if "t_acc" in tap_h:
                    c.wait(nc.gpsimd, "v")
                    c.inc(nc.gpsimd.dma_start(
                        out=tap_h["t_acc"][:, q0:q0 + Q].rearrange(
                            "(dc p) t -> p dc t", p=128),
                        in_=acc[:]), "dcg", 16)
                c.join(ALL_ENG)

            with ExitStack() as PF:
                sb = lambda nm, shp, dt: PF.enter_context(
                    nc.sbuf_tensor(f"{nm}_r{r}", shp, dt))
                psb = lambda nm, shp, dt: PF.enter_context(
                    nc.psum_tensor(f"{nm}_r{r}", shp, dt))
                w1sl = [sb(f"w1sl{i}", [128, DC, 512], BF16) for i in range(3)]
                w2sl = [sb(f"w2sl{i}", [128, 4, DM], BF16) for i in range(3)]
                h_sb = sb("h_sb", [128, HC, CAP], BF16)
                x1_g = sb("x1_g", [128, DC, CAP], BF16)
                out_g = sb("out_g", [128, JT, DM], BF16)
                P_sb = [sb(f"P_sb{i}", [128, QT, CAP], BF16) for i in range(2)]
                PwT = [sb(f"PwT{i}", [128, JT, Q], BF16) for i in range(2)]
                selm = sb("selm", [128, QT, NE], BF16)
                selm32 = sb("selm32", [128, QT, NE], F32)
                cumg = sb("cumg", [128, QT, NE], F32)
                offr = sb("offr", [1, QT, NE], F32)
                totr = sb("totr", [1, QT * NE], F32)
                cumgT = sb("cumgT", [NE, Q], F32)
                wbcf = [sb(f"wbcf{i}", [128, Q], BF16) for i in range(2)]
                tmpP = sb("tmpP", [128, 128], F32)
                ytile = [sb("yt0", [128, DM], F32)]
                wT8 = sb("wT8", [NE, Q], F32)
                wT8b = sb("wT8b", [NE, Q], BF16)
                prb = sb("prb", [128, NE], F32)
                b2row = sb("b2row", [1, DM], BF16)
                wrow = sb("wrow", [1, Q], BF16)

                psO = [psb(f"psO{i}", [128, 1024], F32) for i in range(3)]
                psU = psb("psU", [128, 512], F32)
                psU2 = psb("psU2", [128, 512], F32)

                for tt in range(QT):
                    c.wait(nc.tensor, "v")
                    for dcc in range(DC):
                        ins = nc.tensor.matmul(
                            psU[:, 0:NE], x1q[:, dcc, tt * 128:(tt + 1) * 128],
                            wr_sb[:, dcc, :], start=(dcc == 0), stop=(dcc == DC - 1))
                    tok_t = c.inc(ins, "t")
                    c.wait(nc.vector, "t", tok_t)
                    c.inc(nc.vector.reduce_max(
                        sm0[:, 0:1], psU[:, 0:NE], axis=AX.X), "v")
                    c.wait(nc.vector, "v")
                    c.inc(nc.vector.tensor_scalar_mul(sm0[:, 1:2], sm0[:, 0:1], -1.0), "v")
                    c.wait(nc.scalar, "v")
                    c.inc(nc.scalar.activation(
                        out=prb[:], in_=psU[:, 0:NE], func=AF.Exp,
                        bias=sm0[:, 1:2], accum_out=sm0[:, 2:3]), "s")
                    c.wait(nc.vector, "s")
                    c.inc(nc.vector.reciprocal(sm0[:, 3:4], sm0[:, 2:3]), "v")
                    c.wait(nc.vector, "v")
                    c.inc(nc.vector.tensor_scalar_mul(prb[:], prb[:], sm0[:, 3:4]), "v")
                    c.wait(nc.vector, "v")
                    c.inc(nc.vector.reduce_max(sm0[:, 4:5], prb[:], axis=AX.X), "v")
                    c.wait(nc.vector, "v")
                    c.inc(nc.vector.tensor_scalar(
                        out=sm1[:, 0:NE], in0=prb[:], scalar1=sm0[:, 4:5],
                        scalar2=None, op0=ALU.is_ge), "v")
                    c.wait(nc.vector, "v")
                    c.inc(nc.vector.scalar_tensor_tensor(
                        out=sm1[:, 0:NE], in0=sm1[:, 0:NE], scalar=NEG,
                        in1=prb[:], op0=ALU.mult, op1=ALU.add), "v")
                    c.wait(nc.vector, "v")
                    c.inc(nc.vector.reduce_max(sm0[:, 5:6], sm1[:, 0:NE], axis=AX.X), "v")
                    c.wait(nc.vector, "v")
                    c.inc(nc.vector.tensor_scalar(
                        out=sm1[:, 0:NE], in0=prb[:], scalar1=sm0[:, 5:6],
                        scalar2=None, op0=ALU.is_ge), "v")
                    c.wait(nc.vector, "v")
                    c.inc(nc.vector.tensor_copy(out=selm[:, tt, :], in_=sm1[:, 0:NE]), "v")
                    c.inc(nc.vector.tensor_copy(out=selm32[:, tt, :], in_=sm1[:, 0:NE]), "v")
                    c.wait(nc.vector, "v")
                    c.inc(nc.vector.tensor_mul(prb[:], prb[:], sm1[:, 0:NE]), "v")
                    c.wait(nc.tensor, "s")
                    c.wait(nc.tensor, "v")
                    tok_t = c.inc(nc.tensor.transpose(
                        psU[0:NE, 128:256], prb[:], eye32[:]), "t")
                    c.wait(nc.vector, "t", tok_t)
                    c.inc(nc.vector.tensor_copy(
                        out=wT8[:, tt * 128:(tt + 1) * 128], in_=psU[0:NE, 128:256]), "v")
                    c.wait(nc.tensor, "v")
                c.wait(nc.vector, "v")
                c.inc(nc.vector.tensor_copy(out=wT8b[:], in_=wT8[:]), "v")

                c.wait(nc.tensor, "v")
                for tt in range(QT):
                    ins = nc.tensor.matmul(
                        psU[0:1, 256:256 + NE].rearrange("a b -> a b") if False
                        else psU[0:1, 256 + tt * NE:256 + (tt + 1) * NE],
                        ones_col[:], selm[:, tt, :], start=True, stop=True)
                tok_t = c.inc(ins, "t")
                c.wait(nc.vector, "t", tok_t)
                c.inc(nc.vector.tensor_copy(
                    out=totr[0:1, :], in_=psU[0:1, 256:256 + QT * NE]), "v")
                c.wait(nc.vector, "v")
                c.inc(nc.vector.memset(offr[0:1, 0, :], 0.0), "v")
                for tt in range(1, QT):
                    c.wait(nc.vector, "v")
                    c.inc(nc.vector.tensor_add(
                        offr[0:1, tt, :], offr[0:1, tt - 1, :],
                        totr[0:1, (tt - 1) * NE:tt * NE]), "v")
                c.wait(nc.tensor, "v")
                for tt in range(QT):
                    ins = nc.tensor.matmul(
                        psU[:, 0:NE], triu_sb[:], selm[:, tt, :],
                        start=True, stop=True)
                    tok_t = c.inc(ins, "t")
                    ins = nc.tensor.matmul(
                        psU[:, NE:2 * NE], ones_row[:], offr[0:1, tt, :],
                        start=True, stop=True)
                    tok_t2 = c.inc(ins, "t")
                    c.wait(nc.vector, "t", tok_t2)
                    c.inc(nc.vector.tensor_copy(
                        out=cumg[:, tt, :], in_=psU[:, 0:NE]), "v")
                    c.wait(nc.vector, "v")
                    c.inc(nc.vector.tensor_add(
                        cumg[:, tt, :], cumg[:, tt, :], psU[:, NE:2 * NE]), "v")
                    c.wait(nc.tensor, "v")

                c.wait(nc.tensor, "v")
                for tt in range(QT):
                    tok_t = c.inc(nc.tensor.transpose(
                        psU[0:NE, 0:128], cumg[:, tt, :], eye32[:]), "t")
                    c.wait(nc.vector, "t", tok_t)
                    c.inc(nc.vector.tensor_copy(
                        out=cumgT[:, tt * 128:(tt + 1) * 128], in_=psU[0:NE, 0:128]), "v")
                    c.wait(nc.tensor, "v")

                c.wait(nc.vector, "v")
                for dcc in range(DC):
                    c.inc(nc.vector.tensor_copy(
                        out=acc[:, dcc, :], in_=x1q[:, dcc, :]), "v")

                w1_buf = [0, 0, 0]
                w1_wb = [0, 0, 0]
                w2_buf = [0, 0, 0]
                w2_wb = [0, 0, 0]
                gath_tok = [0]
                pso2_tok = [0]
                b2r_tok = [0]
                tok_P = [0, 0]
                psu_tok = [0, 0, 0, 0]

                def bslot(i):
                    if i == 0:
                        return psU[:, 0:256]
                    if i == 1:
                        return psU2[:, 0:256]
                    if i == 2:
                        return psO[2][:, 0:256]
                    return psO[2][:, 512:768]

                def build_P(e_):
                    par = e_ % 2
                    for tt in range(QT):
                        sl = tt % 4
                        pu = bslot(sl)
                        c.wait(nc.tensor, "v", max(psu_tok[sl], gath_tok[0]))
                        c.inc(nc.tensor.matmul(
                            pu[:, 0:128], sel_sb[:, e_ * 128:(e_ + 1) * 128],
                            wT8[:, tt * 128:(tt + 1) * 128], start=True, stop=True), "t")
                        tok_t = c.inc(nc.tensor.matmul(
                            pu[:, 128:256],
                            sel_sb[:, e_ * 128:(e_ + 1) * 128],
                            cumgT[:, tt * 128:(tt + 1) * 128], start=True, stop=True), "t")
                        c.wait(nc.vector, "t", tok_t)
                        c.wait(nc.vector, "v")
                        c.inc(nc.vector.tensor_copy(
                            out=wbcf[par][:, tt * 128:(tt + 1) * 128],
                            in_=pu[:, 0:128]), "v")
                        c.inc(nc.vector.tensor_scalar(
                            out=P_sb[par][:, tt, 0:CAP], in0=jrow[:, 0:CAP],
                            scalar1=cumg[:, tt, e_:e_ + 1],
                            scalar2=selm32[:, tt, e_:e_ + 1],
                            op0=ALU.is_equal, op1=ALU.mult), "v")
                        for jt in range(JT):
                            c.wait(nc.vector, "v")
                            c.inc(nc.vector.tensor_scalar(
                                out=tmpP[:], in0=pu[:, 128:256],
                                scalar1=jcol[:, jt:jt + 1], scalar2=None,
                                op0=ALU.is_equal), "v")
                            c.wait(nc.vector, "v")
                            c.inc(nc.vector.tensor_mul(
                                PwT[par][:, jt, tt * 128:(tt + 1) * 128],
                                tmpP[:], wbcf[par][:, tt * 128:(tt + 1) * 128]), "v")
                        psu_tok[sl] = c.counts["v"]
                    tok_P[par] = c.counts["v"]
                    pso2_tok[0] = max(pso2_tok[0], c.counts["v"])

                def issue_w1(e_, hs):
                    bi = hs % 3
                    if r == 0:
                        c.wait(nc.gpsimd, "t", w1_buf[bi])
                        if w1_wb[bi]:
                            c.wait(nc.gpsimd, f"dwbW_{bi}", w1_wb[bi])
                        dsem = f"dw1_{bi}"
                        tok_d = c.inc(nc.gpsimd.dma_start(
                            out=w1sl[bi][:],
                            in_=w1_h[e_][:, hs * 512:(hs + 1) * 512]
                            .rearrange("(dc p) h -> p dc h", p=128)), dsem, 16)
                        c.wait(nc.sync, dsem, tok_d)
                        w1_wb[bi] = c.inc(nc.sync.dma_start(
                            out=w1b_h[e_][:, hs * 512:(hs + 1) * 512]
                            .rearrange("(dc p) h -> p dc h", p=128),
                            in_=w1sl[bi][:]), f"dwbW_{bi}", 16)
                    else:
                        c.wait(nc.sync, "t", w1_buf[bi])
                        dsem = f"dw1s_{bi}"
                        tok_d = c.inc(nc.sync.dma_start(
                            out=w1sl[bi][:],
                            in_=w1b_h[e_][:, hs * 512:(hs + 1) * 512]
                            .rearrange("(dc p) h -> p dc h", p=128)), dsem, 16)
                    return dsem, tok_d

                def issue_w2(e_, hs):
                    bi = hs % 3
                    if r == 0:
                        c.wait(nc.gpsimd, "t", w2_buf[bi])
                        if w2_wb[bi]:
                            c.wait(nc.gpsimd, f"dwb_{bi}", w2_wb[bi])
                        dsem = f"dw2_{bi}"
                        tok_d = c.inc(nc.gpsimd.dma_start(
                            out=w2sl[bi][:],
                            in_=w2_h[e_, hs * 512:(hs + 1) * 512, :]
                            .rearrange("(hcc p) d -> p hcc d", p=128)), dsem, 16)
                        c.wait(nc.scalar, dsem, tok_d)
                        w2_wb[bi] = c.inc(nc.scalar.dma_start(
                            out=w2b_h[e_, hs * 512:(hs + 1) * 512, :]
                            .rearrange("(hcc p) d -> p hcc d", p=128),
                            in_=w2sl[bi][:]), f"dwb_{bi}", 16)
                    else:
                        c.wait(nc.scalar, "t", w2_buf[bi])
                        dsem = f"dw2s_{bi}"
                        tok_d = c.inc(nc.scalar.dma_start(
                            out=w2sl[bi][:],
                            in_=w2b_h[e_, hs * 512:(hs + 1) * 512, :]
                            .rearrange("(hcc p) d -> p hcc d", p=128)), dsem, 16)
                    return dsem, tok_d

                build_P(0)
                for e in range(NE):
                    par = e % 2
                    c.wait(nc.tensor, "v", max(tok_P[par], pso2_tok[0]))
                    for dcc in range(DC):
                        for tch in range(QT):
                            ins = nc.tensor.matmul(
                                psO[2][:, 0:CAP],
                                x1_tm[:, tch, dcc * 128:(dcc + 1) * 128],
                                P_sb[par][:, tch, 0:CAP],
                                start=(tch == 0), stop=(tch == QT - 1))
                        tok_t = c.inc(ins, "t")
                        c.wait(nc.vector, "t", tok_t)
                        c.inc(nc.vector.tensor_copy(
                            out=x1_g[:, dcc, :], in_=psO[2][:, 0:CAP]), "v")
                        c.wait(nc.tensor, "v", c.counts["v"])
                    gath_tok[0] = c.counts["v"]
                    pso2_tok[0] = c.counts["v"]
                    if e + 1 < NE:
                        build_P(e + 1)
                    c.wait(nc.gpsimd, "t", b2r_tok[0])
                    tok_b2 = c.inc(nc.gpsimd.dma_start(
                        out=b2row[0:1, :], in_=b2_h[e:e + 1, :]), "dwa", 16)
                    c.wait(nc.vector, "v", tok_P[par])
                    c.inc(nc.vector.tensor_copy(
                        out=wrow[0:1, :], in_=wbcf[par][0:1, :]), "v")
                    w1_pend = {i: issue_w1(e, i) for i in range(3)}
                    w2_pend = {i: issue_w2(e, i) for i in range(3)}
                    gel_toks = [0] * HC
                    for hs in range(8):
                        bi = hs % 3
                        dsem, tok_d = w1_pend.pop(hs)
                        c.wait(nc.tensor, dsem, tok_d)
                        for hh in range(4):
                            hc = hs * 4 + hh
                            ph = psO[(hc % 4) // 2][:, 512 * (hc % 2):512 * (hc % 2) + CAP]
                            if hc >= 4:
                                c.wait(nc.tensor, "s", gel_toks[hc - 4])
                            for dcc in range(DC):
                                ins = nc.tensor.matmul(
                                    ph,
                                    w1sl[bi][:, dcc, hh * 128:(hh + 1) * 128],
                                    x1_g[:, dcc, :],
                                    start=(dcc == 0), stop=(dcc == DC - 1))
                            tok_t = c.inc(ins, "t")
                            if hh == 3:
                                w1_buf[bi] = tok_t
                            c.wait(nc.scalar, "t", tok_t)
                            gel_toks[hc] = c.inc(nc.scalar.activation(
                                out=h_sb[:, hc, :], in_=ph, func=AF.Gelu,
                                bias=b1T[:, hc, e:e + 1]), "s")
                        if hs + 3 < 8:
                            w1_pend[hs + 3] = issue_w1(e, hs + 3)
                    for hs in range(8):
                        bi = hs % 3
                        dsem, tok_d = w2_pend.pop(hs)
                        if hs == 0:
                            c.wait(nc.tensor, "s", gel_toks[HC - 1])
                            c.wait(nc.tensor, "v")
                        c.wait(nc.tensor, dsem, tok_d)
                        for hh in range(4):
                            hc = hs * 4 + hh
                            for jt in range(JT):
                                for dh2 in range(2):
                                    ins = nc.tensor.matmul(
                                        psO[jt][:, dh2 * 512:(dh2 + 1) * 512],
                                        h_sb[:, hc, jt * 128:(jt + 1) * 128],
                                        w2sl[bi][:, hh, dh2 * 512:(dh2 + 1) * 512],
                                        start=(hc == 0), stop=(hc == HC - 1))
                        tok_pe = c.inc(ins, "t")
                        w2_buf[bi] = tok_pe
                        if hs + 3 < 8:
                            w2_pend[hs + 3] = issue_w2(e, hs + 3)
                    c.wait(nc.vector, "t")
                    c.wait(nc.vector, "v")
                    for jt in range(JT):
                        c.inc(nc.vector.tensor_copy(
                            out=out_g[:, jt, :], in_=psO[jt][:, 0:DM]), "v")
                    def sc_slot(sc_, dcc_):
                        if sc_ % 2 == 0:
                            return psO[dcc_ // 4][
                                :, (dcc_ % 4) * SCB:(dcc_ % 4) * SCB + SCB]
                        if dcc_ < 4:
                            return psO[2][:, dcc_ * SCB:(dcc_ + 1) * SCB] \
                                if SCB * 4 <= 1024 else psO[2][:, (dcc_ % 4) * SCB:]
                        u = (psU, psU2)[(dcc_ - 4) // 2]
                        return u[:, ((dcc_ - 4) % 2) * SCB:((dcc_ - 4) % 2) * SCB + SCB]
                    c.wait(nc.tensor, "v")
                    c.wait(nc.tensor, "dwa", tok_b2)
                    sc_add_tok = [0] * NSC
                    for sc in range(NSC):
                        if sc >= 2:
                            c.wait(nc.tensor, "v", sc_add_tok[sc - 2])
                        for dcc in range(DC):
                            nc.tensor.matmul(
                                sc_slot(sc, dcc),
                                b2row[0:1, dcc * 128:(dcc + 1) * 128],
                                wrow[0:1, sc * SCB:sc * SCB + SCB],
                                start=True, stop=False)
                            for jc in range(JT):
                                ins = nc.tensor.matmul(
                                    sc_slot(sc, dcc),
                                    out_g[:, jc, dcc * 128:(dcc + 1) * 128],
                                    PwT[par][:, jc, sc * SCB:sc * SCB + SCB],
                                    start=False, stop=(jc == JT - 1))
                        tok_t = c.inc(ins, "t")
                        b2r_tok[0] = tok_t
                        c.wait(nc.vector, "t", tok_t)
                        c.wait(nc.vector, "v")
                        for dcc in range(DC):
                            c.inc(nc.vector.tensor_add(
                                acc[:, dcc, sc * SCB:sc * SCB + SCB],
                                acc[:, dcc, sc * SCB:sc * SCB + SCB],
                                sc_slot(sc, dcc)), "v")
                        sc_add_tok[sc] = c.counts["v"]
                    for _i in range(4):
                        psu_tok[_i] = max(psu_tok[_i], c.counts["v"])
                    pso2_tok[0] = max(pso2_tok[0], c.counts["v"])
                    c.wait(nc.tensor, "v")

                for tt2 in range(Q // 128):
                    pb = psO[tt2 % 2]
                    c.wait(nc.tensor, "v")
                    for dcc in range(DC):
                        ins = nc.tensor.transpose(
                            pb[:, dcc * 128:(dcc + 1) * 128],
                            acc[:, dcc, tt2 * 128:(tt2 + 1) * 128], eye32[:])
                    tok_t = c.inc(ins, "t")
                    c.wait(nc.vector, "t", tok_t)
                    c.wait(nc.vector, "v")
                    c.inc(nc.vector.bn_stats(bst[:, 0:6], pb[:, 0:512]), "v")
                    c.inc(nc.vector.bn_stats(bst[:, 6:12], pb[:, 512:1024]), "v")
                    c.wait(nc.vector, "v")
                    c.inc(nc.vector.bn_aggr(mv[:], bst[:]), "v")
                    c.wait(nc.scalar, "v")
                    c.inc(nc.scalar.activation(
                        out=sm0[:, 6:7], in_=mv[:, 1:2], func=AF.Ln,
                        bias=eps128[:]), "s")
                    c.wait(nc.scalar, "s")
                    c.inc(nc.scalar.activation(
                        out=sm0[:, 7:8], in_=sm0[:, 6:7], func=AF.Exp,
                        scale=-0.5), "s")
                    yb = ytile[0]
                    c.wait(nc.vector, "s")
                    c.wait(nc.vector, "dy0")
                    c.inc(nc.vector.tensor_scalar(
                        out=yb[:], in0=pb[:, 0:DM], scalar1=mv[:, 0:1],
                        scalar2=sm0[:, 7:8], op0=ALU.subtract, op1=ALU.mult), "v")
                    tok_v = c.counts["v"]
                    c.wait(nc.tensor, "v")
                    c.wait(nc.sync, "v", tok_v)
                    t0 = q0 + tt2 * 128
                    c.inc(nc.sync.dma_start(
                        out=y_h[t0:t0 + 128, :], in_=yb[:]), "dy0", 16)

                if "t_w" in tap_h:
                    c.wait(nc.sync, "v")
                    c.inc(nc.sync.dma_start(
                        out=tap_h["t_w"][:, q0:q0 + Q], in_=wT8[:]), "dct", 16)
                c.join(ALL_ENG)

        c.wait(nc.sync, "dy0")
        c.wait(nc.sync, "dy1")
    return nc


def kernel(L0_attn_x, L0_attn_Wq, L0_attn_Wk, L0_attn_Wv, L0_attn_Wo,
           L0_ffn_w_router, L0_ffn_w1, L0_ffn_b1, L0_ffn_w2, L0_ffn_b2):
    B, S, D = L0_attn_x.shape
    nc = build_nc(S=S)
    common = {
        "wq": np.ascontiguousarray(L0_attn_Wq, dtype=np.float32),
        "wk": np.ascontiguousarray(L0_attn_Wk, dtype=np.float32),
        "wv": np.ascontiguousarray(L0_attn_Wv, dtype=np.float32),
        "wo": np.ascontiguousarray(L0_attn_Wo, dtype=np.float32),
        "wr": np.ascontiguousarray(L0_ffn_w_router, dtype=np.float32),
        "w1": np.ascontiguousarray(L0_ffn_w1, dtype=np.float32),
        "b1": np.ascontiguousarray(L0_ffn_b1, dtype=np.float32),
        "w2": np.ascontiguousarray(L0_ffn_w2, dtype=np.float32),
        "b2": np.ascontiguousarray(L0_ffn_b2, dtype=np.float32),
    }
    in_maps = []
    for b in range(B):
        m = dict(common)
        m["x"] = np.ascontiguousarray(L0_attn_x[b], dtype=np.float32)
        in_maps.append(m)
    res = run_bass_kernel_spmd(nc, in_maps, core_ids=list(range(B)))
    return np.stack([res.results[b]["y"] for b in range(B)], axis=0)


# version 20
# speedup vs baseline: 1.2526x; 1.2526x over previous
from contextlib import ExitStack

import numpy as np
import ml_dtypes

import concourse.bass as bass
import concourse.mybir as mybir
from concourse.bass_utils import run_bass_kernel_spmd

F32 = mybir.dt.float32
BF16 = mybir.dt.bfloat16
AF = mybir.ActivationFunctionType
ALU = mybir.AluOpType
AX = mybir.AxisListType

DM = 1024
DH = 4096
NE = 8
DC = DM // 128
HC = DH // 128
NEG = -1.0e30
EPS = 1e-5
SCL = 1.0 / 32.0


class Ctx:

    def __init__(self, nc):
        self.nc = nc
        self.sems = {}
        self.counts = {}
        for n in ("t", "v", "s"):
            self.new(n)

    def new(self, n):
        if n not in self.sems:
            self.sems[n] = self.nc.alloc_semaphore(n)
            self.counts[n] = 0

    def inc(self, ins, name, n=1):
        ins.then_inc(self.sems[name], n)
        self.counts[name] += n
        return self.counts[name]

    def wait(self, eng, name, val=None):
        v = self.counts[name] if val is None else val
        if v > 0:
            eng.wait_ge(self.sems[name], v)
        return v

    def join(self, engines):
        for eng in engines:
            for n in self.sems:
                self.wait(eng, n)


def build_nc(S=4096, n_rounds=4, taps=()):
    Q = S // n_rounds
    QT = Q // 128
    TB = min(512, Q)
    NTB = Q // TB
    KG = (Q + 511) // 512
    JT = max(1, -(-3 * Q // (8 * 128)))
    CAP = JT * 128
    SCB = min(256, Q)
    NSC = Q // SCB

    nc = bass.Bass("TRN2", target_bir_lowering=False)
    c = Ctx(nc)
    for _sn in ("dc", "dcg", "dct", "dwa", "dwq", "dwk", "dwv", "dx0", "dx1",
                "dx2", "dx3", "dx32", "dw1_0", "dw1_1", "dw1_2", "dw1s_0", "dw1s_1", "dw1s_2",
                "dwbW_0", "dwbW_1", "dwbW_2",
                "dw2_0", "dw2_1", "dw2_2", "dw2s_0", "dw2s_1", "dw2s_2",
                "dwb_0", "dwb_1", "dwb_2", "dy0", "dy1", "dkc", "dkw"):
        c.new(_sn)

    x_h = nc.dram_tensor("x", [S, DM], F32, kind="ExternalInput").ap()
    wq_h = nc.dram_tensor("wq", [DM, DM], F32, kind="ExternalInput").ap()
    wk_h = nc.dram_tensor("wk", [DM, DM], F32, kind="ExternalInput").ap()
    wv_h = nc.dram_tensor("wv", [DM, DM], F32, kind="ExternalInput").ap()
    wo_h = nc.dram_tensor("wo", [DM, DM], F32, kind="ExternalInput").ap()
    wr_h = nc.dram_tensor("wr", [DM, NE], F32, kind="ExternalInput").ap()
    w1_h = nc.dram_tensor("w1", [NE, DM, DH], F32, kind="ExternalInput").ap()
    b1_h = nc.dram_tensor("b1", [NE, DH], F32, kind="ExternalInput").ap()
    w2_h = nc.dram_tensor("w2", [NE, DH, DM], F32, kind="ExternalInput").ap()
    b2_h = nc.dram_tensor("b2", [NE, DM], F32, kind="ExternalInput").ap()
    y_h = nc.dram_tensor("y", [S, DM], F32, kind="ExternalOutput").ap()
    w1b_h = nc.dram_tensor("w1b", [NE, DM, DH], BF16, kind="Internal").ap()
    w2b_h = nc.dram_tensor("w2b", [NE, DH, DM], BF16, kind="Internal").ap()
    wvob_h = nc.dram_tensor("wvob", [DM, DM], BF16, kind="Internal").ap()
    kc_h = nc.dram_tensor("kcache", [n_rounds, 128, DC, Q], BF16, kind="Internal").ap()
    vc_h = nc.dram_tensor("vcache", [n_rounds, 128, QT, DM], BF16, kind="Internal").ap()
    tap_h = {}
    for tn, shp in (("t_r", [DM, S]), ("t_x1", [DM, S]), ("t_w", [NE, S]),
                    ("t_acc", [DM, S])):
        if tn in taps:
            tap_h[tn] = nc.dram_tensor(tn, shp, F32, kind="ExternalOutput").ap()

    eye16_c = nc.inline_tensor(np.eye(128, dtype=ml_dtypes.bfloat16), "eye16c")
    eye32_c = nc.inline_tensor(np.eye(128, dtype=np.float32), "eye32c")
    tri = np.where(np.arange(128)[None, :] <= np.arange(128)[:, None], 0.0, NEG)
    tri_c = nc.inline_tensor(tri.astype(np.float32), "tric")
    ones_col_c = nc.inline_tensor(np.ones((128, 1), dtype=ml_dtypes.bfloat16), "onescolc")
    ones_row_c = nc.inline_tensor(np.ones((1, 128), dtype=np.float32), "onesrowc")
    sel = np.zeros((NE, NE * 128), dtype=np.float32)
    for e in range(NE):
        sel[e, e * 128:(e + 1) * 128] = 1.0
    sel_c = nc.inline_tensor(sel, "selc")
    jrow_c = nc.inline_tensor(
        (np.arange(CAP, dtype=np.float32) + 1.0)[None, :].repeat(128, 0), "jrowc")
    jcol_c = nc.inline_tensor(
        (np.arange(JT * 128, dtype=np.float32) + 1.0).reshape(JT, 128).T.copy(),
        "jcolc")
    eps_c = nc.inline_tensor(np.full((1, 1), EPS, dtype=np.float32), "epsc")
    eps128_c = nc.inline_tensor(np.full((128, 1), EPS, dtype=np.float32), "eps128c")
    triu_c = nc.inline_tensor(
        np.triu(np.ones((128, 128))).astype(ml_dtypes.bfloat16), "triuc")

    ALL_ENG = (nc.tensor, nc.vector, nc.scalar, nc.gpsimd, nc.sync)

    with ExitStack() as P:
        acc = P.enter_context(nc.sbuf_tensor("acc", [128, DC, Q], F32))
        x1q = P.enter_context(nc.sbuf_tensor("x1q", [128, DC, Q], BF16))
        eye16 = P.enter_context(nc.sbuf_tensor("eye16", [128, 128], BF16))
        eye32 = P.enter_context(nc.sbuf_tensor("eye32", [128, 128], F32))
        trim = P.enter_context(nc.sbuf_tensor("trim", [128, 128], F32))
        ones_col = P.enter_context(nc.sbuf_tensor("ones_col", [128, 1], BF16))
        ones_row = P.enter_context(nc.sbuf_tensor("ones_row", [1, 128], F32))
        sel_sb = P.enter_context(nc.sbuf_tensor("sel_sb", [NE, NE * 128], F32))
        b1T = P.enter_context(nc.sbuf_tensor("b1T", [128, HC, NE], F32))
        b2T = P.enter_context(nc.sbuf_tensor("b2T", [128, DC, NE], F32))
        wr_sb = P.enter_context(nc.sbuf_tensor("wr_sb", [128, DC, NE], BF16))
        m_run = P.enter_context(nc.sbuf_tensor("m_run", [128, QT], F32))
        l_run = P.enter_context(nc.sbuf_tensor("l_run", [128, QT], F32))
        sm0 = P.enter_context(nc.sbuf_tensor("sm0", [128, 8], F32))
        sm1 = P.enter_context(nc.sbuf_tensor("sm1", [128, 8], F32))
        epsb = P.enter_context(nc.sbuf_tensor("epsb", [1, 1], F32))
        eps128 = P.enter_context(nc.sbuf_tensor("eps128", [128, 1], F32))
        bst = P.enter_context(nc.sbuf_tensor("bst", [128, 12], F32))
        mv = P.enter_context(nc.sbuf_tensor("mv", [128, 2], F32))
        triu_sb = P.enter_context(nc.sbuf_tensor("triu_sb", [128, 128], BF16))
        jrow = P.enter_context(nc.sbuf_tensor("jrow", [128, CAP], F32))
        jcol = P.enter_context(nc.sbuf_tensor("jcol", [128, JT], F32))
        x1_tm = P.enter_context(nc.sbuf_tensor("x1_tm", [128, QT, DM], BF16))

        with ExitStack() as PH:
            b1raw = PH.enter_context(nc.sbuf_tensor("b1raw", [NE, DH], F32))
            b2raw = PH.enter_context(nc.sbuf_tensor("b2raw", [NE, DM], F32))
            wv_tmp = PH.enter_context(nc.sbuf_tensor("wv_tmp", [128, DC, DM], BF16))
            wvT_tmp = PH.enter_context(nc.sbuf_tensor("wvT_tmp", [128, DC, DM], BF16))
            wvo_pre = PH.enter_context(nc.sbuf_tensor("wvo_pre", [128, DC, DM], BF16))
            psP = PH.enter_context(nc.psum_tensor("psP", [128, 512], F32))
            psP_T = PH.enter_context(nc.psum_tensor("psP_T", [128, 1024], BF16))

            for dst, src in ((eye16, eye16_c), (eye32, eye32_c), (trim, tri_c),
                             (ones_col, ones_col_c), (ones_row, ones_row_c),
                             (sel_sb, sel_c), (epsb, eps_c), (triu_sb, triu_c),
                             (jrow, jrow_c), (jcol, jcol_c), (eps128, eps128_c)):
                c.inc(nc.sync.dma_start(out=dst[:], in_=src.ap()), "dc", 16)
            c.inc(nc.sync.dma_start(out=b1raw[:], in_=b1_h), "dc", 16)
            c.inc(nc.sync.dma_start(out=b2raw[:], in_=b2_h), "dc", 16)
            c.inc(nc.gpsimd.dma_start(
                out=wr_sb[:], in_=wr_h.rearrange("(dc p) e -> p dc e", p=128)), "dwa", 16)
            c.wait(nc.tensor, "dc")
            for hc in range(HC):
                tok_t = c.inc(nc.tensor.transpose(
                    psP[:, 0:NE], b1raw[0:NE, hc * 128:(hc + 1) * 128],
                    eye32[0:NE, 0:NE]), "t")
                c.wait(nc.vector, "t", tok_t)
                c.inc(nc.vector.tensor_copy(out=b1T[:, hc, :], in_=psP[:, 0:NE]), "v")
                c.wait(nc.tensor, "v")
            for dc in range(DC):
                tok_t = c.inc(nc.tensor.transpose(
                    psP[:, 8:8 + NE], b2raw[0:NE, dc * 128:(dc + 1) * 128],
                    eye32[0:NE, 0:NE]), "t")
                c.wait(nc.vector, "t", tok_t)
                c.inc(nc.vector.tensor_copy(out=b2T[:, dc, :], in_=psP[:, 8:8 + NE]), "v")
                c.wait(nc.tensor, "v")

            tok_wv = c.inc(nc.gpsimd.dma_start(
                out=wv_tmp[:], in_=wv_h.rearrange("(dc p) n -> p dc n", p=128)), "dwa", 16)
            c.wait(nc.tensor, "dwa", tok_wv)
            for dc in range(DC):
                for dvc in range(DC):
                    c.inc(nc.tensor.transpose(
                        psP_T[:, dvc * 128:(dvc + 1) * 128],
                        wv_tmp[:, dc, dvc * 128:(dvc + 1) * 128], eye16[:]), "t")
                c.wait(nc.vector, "t")
                c.inc(nc.vector.tensor_copy(
                    out=wvT_tmp[:, dc, :], in_=psP_T[:, 0:DM]), "v")
                c.wait(nc.tensor, "v")
            c.wait(nc.gpsimd, "t")
            tok_wo = c.inc(nc.gpsimd.dma_start(
                out=wv_tmp[:], in_=wo_h.rearrange("(dc p) n -> p dc n", p=128)), "dwa", 16)
            c.wait(nc.tensor, "dwa", tok_wo)
            for dc in range(DC):
                for half in range(DM // 512):
                    for dvc in range(DC):
                        ins = nc.tensor.matmul(
                            psP[:, 0:512],
                            wvT_tmp[:, dc, dvc * 128:(dvc + 1) * 128],
                            wv_tmp[:, dvc, half * 512:half * 512 + 512],
                            start=(dvc == 0), stop=(dvc == DC - 1))
                    tok_t = c.inc(ins, "t")
                    c.wait(nc.vector, "t", tok_t)
                    c.inc(nc.vector.tensor_copy(
                        out=wvo_pre[:, dc, half * 512:half * 512 + 512],
                        in_=psP[:, 0:512]), "v")
                    c.wait(nc.tensor, "v")
            c.wait(nc.sync, "v")
            c.inc(nc.sync.dma_start(
                out=wvob_h.rearrange("(dc p) n -> p dc n", p=128),
                in_=wvo_pre[:]), "dc", 16)
            c.join(ALL_ENG)

        for r in range(n_rounds):
            q0 = r * Q

            with ExitStack() as PA:
                sb = lambda nm, shp, dt: PA.enter_context(
                    nc.sbuf_tensor(f"{nm}_r{r}", shp, dt))
                psb = lambda nm, shp, dt: PA.enter_context(
                    nc.psum_tensor(f"{nm}_r{r}", shp, dt))
                q_fm = sb("q_fm", [128, DC, Q], BF16)
                k_blk = sb("k_blk", [128, DC, Q], BF16)
                v_blk = sb("v_blk", [128, QT, DM], BF16)
                wq_sb = sb("wq_sb", [128, DC, DM], BF16)
                wk_sb = sb("wk_sb", [128, DC, DM], BF16)
                wvo_sb = sb("wvo_sb", [128, DC, DM], BF16)
                xt16 = [sb(f"xt16_{i}", [128, DM], BF16) for i in range(3)]
                xf16 = sb("xf16", [128, DC, 512], BF16)
                xt32 = sb("xt32", [128, DM], F32)
                rtm = sb("rtm", [128, DM], F32)
                rowst = sb("rowst", [1, 128], F32)
                s_row = sb("s_row", [128, Q], BF16)
                p16 = sb("p16", [128, Q], BF16)
                attnT = sb("attnT", [128, QT, 128], BF16)

                psA_s = psb("psA_s", [128, 512], F32)
                psA_s2 = psb("psA_s2", [128, 512], F32)
                psA_pv = [psb(f"psA_pv{i}", [128, 1024], F32) for i in range(2)]
                psA_T = psb("psA_T", [128, 1024], BF16)
                psA_m = psb("psA_m", [128, 512], F32)

                def load_w_fm(dst, src, sem):
                    return c.inc(nc.gpsimd.dma_start(
                        out=dst[:], in_=src.rearrange("(dc p) n -> p dc n", p=128)),
                        sem, 16)

                tok_wq = load_w_fm(wq_sb, wq_h, "dwq")
                tok_wk = load_w_fm(wk_sb, wk_h, "dwk")
                tok_wv = c.inc(nc.sync.dma_start(
                    out=wvo_sb[:],
                    in_=wvob_h.rearrange("(dc p) n -> p dc n", p=128)), "dwv", 16)

                c.wait(nc.tensor, "dwq", tok_wq)
                xb_tok = [0, 0, 0, 0]
                sbank_tok = [0, 0]
                for g in range(KG):
                    gn = min(512, Q - g * 512)
                    gt = gn // 128
                    for i in range(gt):
                        t0 = q0 + g * 512 + i * 128
                        buf = xt16[i % 3]
                        dsem = f"dx{i % 3}"
                        c.wait(nc.gpsimd, "t", xb_tok[i % 3])
                        tok_d = c.inc(nc.gpsimd.dma_start(
                            out=buf[:], in_=x_h[t0:t0 + 128, :]), dsem, 16)
                        c.wait(nc.tensor, dsem, tok_d)
                        for dcc in range(DC):
                            c.inc(nc.tensor.transpose(
                                psA_T[:, dcc * 128:(dcc + 1) * 128],
                                buf[:, dcc * 128:(dcc + 1) * 128], eye16[:]), "t")
                        xb_tok[i % 3] = c.counts["t"]
                        c.wait(nc.vector, "t")
                        c.inc(nc.vector.tensor_copy(
                            out=xf16[:, :, i * 128:(i + 1) * 128],
                            in_=psA_T[:, 0:DM].rearrange("p (a b) -> p a b", b=128)), "v")
                        c.wait(nc.tensor, "v")
                    for kc in range(DC):
                        for dcc in range(DC):
                            ins = nc.tensor.matmul(
                                psA_s[:, 0:gn], wq_sb[:, dcc, kc * 128:(kc + 1) * 128],
                                xf16[:, dcc, 0:gn],
                                start=(dcc == 0), stop=(dcc == DC - 1))
                        tok_t = c.inc(ins, "t")
                        c.wait(nc.vector, "t", tok_t)
                        c.inc(nc.vector.tensor_copy(
                            out=q_fm[:, kc, g * 512:g * 512 + gn], in_=psA_s[:, 0:gn]), "v")
                        c.wait(nc.tensor, "v")

                for kb in range(r + 1):
                    kb0 = kb * Q
                    if kb < r:
                        c.wait(nc.sync, "t")
                        tok_kc = c.inc(nc.sync.dma_start(
                            out=k_blk[:], in_=kc_h[kb]), "dkc", 16)
                        tok_vc = c.inc(nc.sync.dma_start(
                            out=v_blk[:], in_=vc_h[kb]), "dkc", 16)
                        c.wait(nc.tensor, "dkc", tok_vc)
                    if kb == r:
                        c.wait(nc.tensor, "dwk", tok_wk)
                        for g in range(KG):
                            gn = min(512, Q - g * 512)
                            gt = gn // 128
                            for i in range(gt):
                                t0 = kb0 + g * 512 + i * 128
                                buf = xt16[i % 2]
                                dsem = f"dx{i % 2}"
                                c.wait(nc.gpsimd, "t", xb_tok[i % 2])
                                tok_d = c.inc(nc.gpsimd.dma_start(
                                    out=buf[:], in_=x_h[t0:t0 + 128, :]), dsem, 16)
                                c.wait(nc.tensor, dsem, tok_d)
                                for dcc in range(DC):
                                    c.inc(nc.tensor.transpose(
                                        psA_T[:, dcc * 128:(dcc + 1) * 128],
                                        buf[:, dcc * 128:(dcc + 1) * 128], eye16[:]), "t")
                                xb_tok[i % 3] = c.counts["t"]
                                c.wait(nc.vector, "t")
                                c.inc(nc.vector.tensor_copy(
                                    out=xf16[:, :, i * 128:(i + 1) * 128],
                                    in_=psA_T[:, 0:DM].rearrange("p (a b) -> p a b", b=128)), "v")
                                c.wait(nc.tensor, "v")
                                c.wait(nc.tensor, "dwv", tok_wv)
                                for half in range(DM // 512):
                                    for dcc in range(DC):
                                        ins = nc.tensor.matmul(
                                            psA_pv[0][:, half * 512:half * 512 + 512],
                                            xf16[:, dcc, i * 128:(i + 1) * 128],
                                            wvo_sb[:, dcc, half * 512:half * 512 + 512],
                                            start=(dcc == 0), stop=(dcc == DC - 1))
                                tok_t = c.inc(ins, "t")
                                c.wait(nc.vector, "t", tok_t)
                                c.inc(nc.vector.tensor_copy(
                                    out=v_blk[:, g * 4 + i, :], in_=psA_pv[0][:, 0:DM]), "v")
                                c.wait(nc.tensor, "v")
                            for kc in range(DC):
                                for dcc in range(DC):
                                    ins = nc.tensor.matmul(
                                        psA_s[:, 0:gn], wk_sb[:, dcc, kc * 128:(kc + 1) * 128],
                                        xf16[:, dcc, 0:gn],
                                        start=(dcc == 0), stop=(dcc == DC - 1))
                                tok_t = c.inc(ins, "t")
                                c.wait(nc.vector, "t", tok_t)
                                c.inc(nc.vector.tensor_copy(
                                    out=k_blk[:, kc, g * 512:g * 512 + gn],
                                    in_=psA_s[:, 0:gn]), "v")
                                c.wait(nc.tensor, "v")

                    if kb == r and r + 1 < n_rounds:
                        c.wait(nc.sync, "v")
                        c.inc(nc.sync.dma_start(out=kc_h[r], in_=k_blk[:]), "dkw", 16)
                        c.inc(nc.sync.dma_start(out=vc_h[r], in_=v_blk[:]), "dkw", 16)
                    diag = (kb == r)
                    for qt in range(QT):
                        tk = (qt + 1) * 128 if diag else Q
                        ng = (tk + 511) // 512
                        for g in range(ng):
                            n0 = g * 512
                            nn = min(512, tk - n0)
                            ps = psA_s if g % 2 == 0 else psA_s2
                            c.wait(nc.tensor, "v", sbank_tok[g % 2])
                            for kc in range(DC):
                                ins = nc.tensor.matmul(
                                    ps[:, 0:nn], q_fm[:, kc, qt * 128:(qt + 1) * 128],
                                    k_blk[:, kc, n0:n0 + nn],
                                    start=(kc == 0), stop=(kc == DC - 1))
                            tok_t = c.inc(ins, "t")
                            c.wait(nc.vector, "t", tok_t)
                            nmask = 128 if (diag and g == ng - 1) else 0
                            if nn - nmask > 0:
                                c.inc(nc.vector.tensor_copy(
                                    out=s_row[:, n0:n0 + nn - nmask],
                                    in_=ps[:, 0:nn - nmask]), "v")
                            if nmask:
                                c.inc(nc.vector.tensor_add(
                                    s_row[:, n0 + nn - 128:n0 + nn],
                                    ps[:, nn - 128:nn], trim[:]), "v")
                            sbank_tok[g % 2] = c.counts["v"]
                        c.wait(nc.vector, "v")
                        c.inc(nc.vector.reduce_max(
                            sm0[:, 0:1], s_row[:, 0:tk], axis=AX.X), "v")
                        mcur = m_run[:, qt:qt + 1]
                        c.wait(nc.vector, "v")
                        if kb == 0:
                            c.inc(nc.vector.tensor_copy(out=mcur, in_=sm0[:, 0:1]), "v")
                        else:
                            c.inc(nc.vector.tensor_tensor(
                                sm1[:, 0:1], mcur, sm0[:, 0:1], op=ALU.max), "v")
                            c.wait(nc.vector, "v")
                            c.inc(nc.vector.tensor_sub(
                                sm0[:, 1:2], mcur, sm1[:, 0:1]), "v")
                            c.wait(nc.vector, "v")
                            c.inc(nc.vector.tensor_copy(out=mcur, in_=sm1[:, 0:1]), "v")
                            c.wait(nc.scalar, "v")
                            c.inc(nc.scalar.activation(
                                out=sm0[:, 2:3], in_=sm0[:, 1:2], func=AF.Exp,
                                scale=SCL), "s")
                        c.wait(nc.vector, "v")
                        c.inc(nc.vector.tensor_scalar_mul(sm0[:, 3:4], mcur, -SCL), "v")
                        c.wait(nc.scalar, "v")
                        c.inc(nc.scalar.activation(
                            out=p16[:, 0:tk], in_=s_row[:, 0:tk], func=AF.Exp,
                            bias=sm0[:, 3:4], scale=SCL, accum_out=sm0[:, 4:5]), "s")
                        c.wait(nc.vector, "s")
                        if kb == 0:
                            c.inc(nc.vector.tensor_copy(
                                out=l_run[:, qt:qt + 1], in_=sm0[:, 4:5]), "v")
                        else:
                            c.inc(nc.vector.scalar_tensor_tensor(
                                out=l_run[:, qt:qt + 1], in0=l_run[:, qt:qt + 1],
                                scalar=sm0[:, 2:3], in1=sm0[:, 4:5],
                                op0=ALU.mult, op1=ALU.add), "v")
                        ntk = tk // 128
                        c.wait(nc.tensor, "s")
                        c.wait(nc.tensor, "v")
                        for tkc in range(ntk):
                            c.inc(nc.tensor.transpose(
                                psA_T[:, tkc * 128:(tkc + 1) * 128],
                                p16[:, tkc * 128:(tkc + 1) * 128], eye16[:]), "t")
                        c.wait(nc.vector, "t")
                        c.inc(nc.vector.tensor_copy(
                            out=attnT[:, 0:ntk, :],
                            in_=psA_T[:, 0:ntk * 128].rearrange(
                                "p (a b) -> p a b", b=128)), "v")
                        pvb = psA_pv[qt % 2]
                        c.wait(nc.tensor, "v")
                        for dvc in range(DC):
                            for tkc in range(ntk):
                                ins = nc.tensor.matmul(
                                    pvb[:, dvc * 128:(dvc + 1) * 128],
                                    v_blk[:, tkc, dvc * 128:(dvc + 1) * 128],
                                    attnT[:, tkc, :],
                                    start=(tkc == 0), stop=(tkc == ntk - 1))
                        tok_pv = c.inc(ins, "t")
                        accs = acc[:, :, qt * 128:(qt + 1) * 128]
                        c.wait(nc.vector, "t", tok_pv)
                        if kb == 0:
                            c.inc(nc.vector.tensor_copy(
                                out=accs,
                                in_=pvb[:, 0:DM].rearrange("p (a b) -> p a b", b=128)), "v")
                        else:
                            c.wait(nc.tensor, "s")
                            tok_t = c.inc(nc.tensor.transpose(
                                psA_m[0:1, 0:128], sm0[:, 2:3], eye32[:]), "t")
                            c.wait(nc.vector, "t", tok_t)
                            c.inc(nc.vector.tensor_copy(
                                out=rowst[0:1, 0:128], in_=psA_m[0:1, 0:128]), "v")
                            c.wait(nc.tensor, "v")
                            tok_t = c.inc(nc.tensor.matmul(
                                psA_m[:, 128:256], ones_row[:], rowst[0:1, 0:128],
                                start=True, stop=True), "t")
                            c.wait(nc.vector, "t", tok_t)
                            for dcc in range(DC):
                                c.inc(nc.vector.tensor_mul(
                                    accs[:, dcc, :], accs[:, dcc, :],
                                    psA_m[:, 128:256]), "v")
                            c.wait(nc.vector, "v")
                            c.inc(nc.vector.tensor_add(
                                accs, accs,
                                pvb[:, 0:DM].rearrange("p (a b) -> p a b", b=128)), "v")
                        c.wait(nc.tensor, "v")

                x1tm_tok = [0] * QT
                for qt in range(QT):
                    t0 = q0 + qt * 128
                    c.wait(nc.sync, "v")
                    tok_d = c.inc(nc.sync.dma_start(
                        out=xt32[:], in_=x_h[t0:t0 + 128, :]), "dx32", 16)
                    c.wait(nc.vector, "v")
                    c.inc(nc.vector.reciprocal(sm0[:, 5:6], l_run[:, qt:qt + 1]), "v")
                    pp = psA_pv[qt % 2]
                    c.wait(nc.tensor, "v")
                    for dcc in range(DC):
                        ins = nc.tensor.transpose(
                            pp[:, dcc * 128:(dcc + 1) * 128],
                            acc[:, dcc, qt * 128:(qt + 1) * 128], eye32[:])
                    tok_t = c.inc(ins, "t")
                    c.wait(nc.vector, "t", tok_t)
                    c.wait(nc.vector, "dx32", tok_d)
                    c.wait(nc.vector, "v")
                    c.inc(nc.vector.scalar_tensor_tensor(
                        out=rtm[:], in0=pp[:, 0:DM], scalar=sm0[:, 5:6],
                        in1=xt32[:], op0=ALU.mult, op1=ALU.add), "v")
                    c.wait(nc.vector, "v")
                    c.inc(nc.vector.bn_stats(bst[:, 0:6], rtm[:, 0:512]), "v")
                    c.inc(nc.vector.bn_stats(bst[:, 6:12], rtm[:, 512:1024]), "v")
                    c.wait(nc.vector, "v")
                    c.inc(nc.vector.bn_aggr(mv[:], bst[:]), "v")
                    c.wait(nc.scalar, "v")
                    c.inc(nc.scalar.activation(
                        out=sm0[:, 6:7], in_=mv[:, 1:2], func=AF.Ln,
                        bias=eps128[:]), "s")
                    c.wait(nc.scalar, "s")
                    c.inc(nc.scalar.activation(
                        out=sm0[:, 7:8], in_=sm0[:, 6:7], func=AF.Exp,
                        scale=-0.5), "s")
                    c.wait(nc.vector, "s")
                    x1tm_tok[qt] = c.inc(nc.vector.tensor_scalar(
                        out=x1_tm[:, qt, :], in0=rtm[:], scalar1=mv[:, 0:1],
                        scalar2=sm0[:, 7:8], op0=ALU.subtract, op1=ALU.mult), "v")
                trev_tok = 0
                for qt in range(QT):
                    c.wait(nc.tensor, "v", max(x1tm_tok[qt], trev_tok))
                    for dcc in range(DC):
                        ins = nc.tensor.transpose(
                            psA_T[:, dcc * 128:(dcc + 1) * 128],
                            x1_tm[:, qt, dcc * 128:(dcc + 1) * 128], eye16[:])
                    tok_t = c.inc(ins, "t")
                    c.wait(nc.vector, "t", tok_t)
                    trev_tok = c.inc(nc.vector.tensor_copy(
                        out=x1q[:, :, qt * 128:(qt + 1) * 128],
                        in_=psA_T[:, 0:DM].rearrange("p (a b) -> p a b", b=128)), "v")
                c.wait(nc.tensor, "v")

                if "t_x1" in tap_h:
                    c.wait(nc.gpsimd, "v")
                    c.inc(nc.gpsimd.dma_start(
                        out=tap_h["t_x1"][:, q0:q0 + Q].rearrange(
                            "(dc p) t -> p dc t", p=128),
                        in_=x1q[:]), "dcg", 16)
                if "t_acc" in tap_h:
                    c.wait(nc.gpsimd, "v")
                    c.inc(nc.gpsimd.dma_start(
                        out=tap_h["t_acc"][:, q0:q0 + Q].rearrange(
                            "(dc p) t -> p dc t", p=128),
                        in_=acc[:]), "dcg", 16)
                c.join(ALL_ENG)

            with ExitStack() as PF:
                sb = lambda nm, shp, dt: PF.enter_context(
                    nc.sbuf_tensor(f"{nm}_r{r}", shp, dt))
                psb = lambda nm, shp, dt: PF.enter_context(
                    nc.psum_tensor(f"{nm}_r{r}", shp, dt))
                w1sl = [sb(f"w1sl{i}", [128, DC, 512], BF16) for i in range(3)]
                w2sl = [sb(f"w2sl{i}", [128, 4, DM], BF16) for i in range(3)]
                h_sb = sb("h_sb", [128, HC, CAP], BF16)
                x1_g = sb("x1_g", [128, DC, CAP], BF16)
                out_g = sb("out_g", [128, JT, DM], BF16)
                P_sb = [sb(f"P_sb{i}", [128, QT, CAP], BF16) for i in range(2)]
                PwT = [sb(f"PwT{i}", [128, JT, Q], BF16) for i in range(2)]
                selm = sb("selm", [128, QT, NE], BF16)
                selm32 = sb("selm32", [128, QT, NE], F32)
                cumg = sb("cumg", [128, QT, NE], F32)
                offr = sb("offr", [1, QT, NE], F32)
                totr = sb("totr", [1, QT * NE], F32)
                cumgT = sb("cumgT", [NE, Q], F32)
                wbcf = [sb(f"wbcf{i}", [128, Q], BF16) for i in range(2)]
                tmpP = sb("tmpP", [128, 128], F32)
                ytile = [sb("yt0", [128, DM], F32)]
                wT8 = sb("wT8", [NE, Q], F32)
                wT8b = sb("wT8b", [NE, Q], BF16)
                prb = sb("prb", [128, NE], F32)

                psO = [psb(f"psO{i}", [128, 1024], F32) for i in range(3)]
                psU = psb("psU", [128, 512], F32)
                psU2 = psb("psU2", [128, 512], F32)

                for tt in range(QT):
                    c.wait(nc.tensor, "v")
                    for dcc in range(DC):
                        ins = nc.tensor.matmul(
                            psU[:, 0:NE], x1q[:, dcc, tt * 128:(tt + 1) * 128],
                            wr_sb[:, dcc, :], start=(dcc == 0), stop=(dcc == DC - 1))
                    tok_t = c.inc(ins, "t")
                    c.wait(nc.vector, "t", tok_t)
                    c.inc(nc.vector.reduce_max(
                        sm0[:, 0:1], psU[:, 0:NE], axis=AX.X), "v")
                    c.wait(nc.vector, "v")
                    c.inc(nc.vector.tensor_scalar_mul(sm0[:, 1:2], sm0[:, 0:1], -1.0), "v")
                    c.wait(nc.scalar, "v")
                    c.inc(nc.scalar.activation(
                        out=prb[:], in_=psU[:, 0:NE], func=AF.Exp,
                        bias=sm0[:, 1:2], accum_out=sm0[:, 2:3]), "s")
                    c.wait(nc.vector, "s")
                    c.inc(nc.vector.reciprocal(sm0[:, 3:4], sm0[:, 2:3]), "v")
                    c.wait(nc.vector, "v")
                    c.inc(nc.vector.tensor_scalar_mul(prb[:], prb[:], sm0[:, 3:4]), "v")
                    c.wait(nc.vector, "v")
                    c.inc(nc.vector.reduce_max(sm0[:, 4:5], prb[:], axis=AX.X), "v")
                    c.wait(nc.vector, "v")
                    c.inc(nc.vector.tensor_scalar(
                        out=sm1[:, 0:NE], in0=prb[:], scalar1=sm0[:, 4:5],
                        scalar2=None, op0=ALU.is_ge), "v")
                    c.wait(nc.vector, "v")
                    c.inc(nc.vector.scalar_tensor_tensor(
                        out=sm1[:, 0:NE], in0=sm1[:, 0:NE], scalar=NEG,
                        in1=prb[:], op0=ALU.mult, op1=ALU.add), "v")
                    c.wait(nc.vector, "v")
                    c.inc(nc.vector.reduce_max(sm0[:, 5:6], sm1[:, 0:NE], axis=AX.X), "v")
                    c.wait(nc.vector, "v")
                    c.inc(nc.vector.tensor_scalar(
                        out=sm1[:, 0:NE], in0=prb[:], scalar1=sm0[:, 5:6],
                        scalar2=None, op0=ALU.is_ge), "v")
                    c.wait(nc.vector, "v")
                    c.inc(nc.vector.tensor_copy(out=selm[:, tt, :], in_=sm1[:, 0:NE]), "v")
                    c.inc(nc.vector.tensor_copy(out=selm32[:, tt, :], in_=sm1[:, 0:NE]), "v")
                    c.wait(nc.vector, "v")
                    c.inc(nc.vector.tensor_mul(prb[:], prb[:], sm1[:, 0:NE]), "v")
                    c.wait(nc.tensor, "s")
                    c.wait(nc.tensor, "v")
                    tok_t = c.inc(nc.tensor.transpose(
                        psU[0:NE, 128:256], prb[:], eye32[:]), "t")
                    c.wait(nc.vector, "t", tok_t)
                    c.inc(nc.vector.tensor_copy(
                        out=wT8[:, tt * 128:(tt + 1) * 128], in_=psU[0:NE, 128:256]), "v")
                    c.wait(nc.tensor, "v")
                c.wait(nc.vector, "v")
                c.inc(nc.vector.tensor_copy(out=wT8b[:], in_=wT8[:]), "v")

                c.wait(nc.tensor, "v")
                for tt in range(QT):
                    ins = nc.tensor.matmul(
                        psU[0:1, 256:256 + NE].rearrange("a b -> a b") if False
                        else psU[0:1, 256 + tt * NE:256 + (tt + 1) * NE],
                        ones_col[:], selm[:, tt, :], start=True, stop=True)
                tok_t = c.inc(ins, "t")
                c.wait(nc.vector, "t", tok_t)
                c.inc(nc.vector.tensor_copy(
                    out=totr[0:1, :], in_=psU[0:1, 256:256 + QT * NE]), "v")
                c.wait(nc.vector, "v")
                c.inc(nc.vector.memset(offr[0:1, 0, :], 0.0), "v")
                for tt in range(1, QT):
                    c.wait(nc.vector, "v")
                    c.inc(nc.vector.tensor_add(
                        offr[0:1, tt, :], offr[0:1, tt - 1, :],
                        totr[0:1, (tt - 1) * NE:tt * NE]), "v")
                c.wait(nc.tensor, "v")
                for tt in range(QT):
                    ins = nc.tensor.matmul(
                        psU[:, 0:NE], triu_sb[:], selm[:, tt, :],
                        start=True, stop=True)
                    tok_t = c.inc(ins, "t")
                    ins = nc.tensor.matmul(
                        psU[:, NE:2 * NE], ones_row[:], offr[0:1, tt, :],
                        start=True, stop=True)
                    tok_t2 = c.inc(ins, "t")
                    c.wait(nc.vector, "t", tok_t2)
                    c.inc(nc.vector.tensor_copy(
                        out=cumg[:, tt, :], in_=psU[:, 0:NE]), "v")
                    c.wait(nc.vector, "v")
                    c.inc(nc.vector.tensor_add(
                        cumg[:, tt, :], cumg[:, tt, :], psU[:, NE:2 * NE]), "v")
                    c.wait(nc.tensor, "v")

                c.wait(nc.tensor, "v")
                for tt in range(QT):
                    tok_t = c.inc(nc.tensor.transpose(
                        psU[0:NE, 0:128], cumg[:, tt, :], eye32[:]), "t")
                    c.wait(nc.vector, "t", tok_t)
                    c.inc(nc.vector.tensor_copy(
                        out=cumgT[:, tt * 128:(tt + 1) * 128], in_=psU[0:NE, 0:128]), "v")
                    c.wait(nc.tensor, "v")

                c.wait(nc.vector, "v")
                for dcc in range(DC):
                    c.inc(nc.vector.tensor_copy(
                        out=acc[:, dcc, :], in_=x1q[:, dcc, :]), "v")

                w1_buf = [0, 0, 0]
                w1_wb = [0, 0, 0]
                w2_buf = [0, 0, 0]
                w2_wb = [0, 0, 0]
                gath_tok = [0]
                pso2_tok = [0]
                b2r_tok = [0]
                tok_P = [0, 0]
                psu_tok = [0, 0, 0, 0]

                def bslot(i):
                    if i == 0:
                        return psU[:, 0:256]
                    if i == 1:
                        return psU2[:, 0:256]
                    if i == 2:
                        return psO[2][:, 0:256]
                    return psO[2][:, 512:768]

                def build_P(e_):
                    par = e_ % 2
                    for tt in range(QT):
                        sl = tt % 4
                        pu = bslot(sl)
                        c.wait(nc.tensor, "v", max(psu_tok[sl], gath_tok[0]))
                        c.inc(nc.tensor.matmul(
                            pu[:, 0:128], sel_sb[:, e_ * 128:(e_ + 1) * 128],
                            wT8[:, tt * 128:(tt + 1) * 128], start=True, stop=True), "t")
                        tok_t = c.inc(nc.tensor.matmul(
                            pu[:, 128:256],
                            sel_sb[:, e_ * 128:(e_ + 1) * 128],
                            cumgT[:, tt * 128:(tt + 1) * 128], start=True, stop=True), "t")
                        c.wait(nc.vector, "t", tok_t)
                        c.wait(nc.vector, "v")
                        c.inc(nc.vector.tensor_copy(
                            out=wbcf[par][:, tt * 128:(tt + 1) * 128],
                            in_=pu[:, 0:128]), "v")
                        c.inc(nc.vector.tensor_scalar(
                            out=P_sb[par][:, tt, 0:CAP], in0=jrow[:, 0:CAP],
                            scalar1=cumg[:, tt, e_:e_ + 1],
                            scalar2=selm32[:, tt, e_:e_ + 1],
                            op0=ALU.is_equal, op1=ALU.mult), "v")
                        for jt in range(JT):
                            c.wait(nc.vector, "v")
                            c.inc(nc.vector.tensor_scalar(
                                out=tmpP[:], in0=pu[:, 128:256],
                                scalar1=jcol[:, jt:jt + 1], scalar2=None,
                                op0=ALU.is_equal), "v")
                            c.wait(nc.vector, "v")
                            c.inc(nc.vector.tensor_mul(
                                PwT[par][:, jt, tt * 128:(tt + 1) * 128],
                                tmpP[:], wbcf[par][:, tt * 128:(tt + 1) * 128]), "v")
                        psu_tok[sl] = c.counts["v"]
                    tok_P[par] = c.counts["v"]
                    pso2_tok[0] = max(pso2_tok[0], c.counts["v"])

                def issue_w1(e_, hs):
                    bi = hs % 3
                    if r == 0:
                        c.wait(nc.gpsimd, "t", w1_buf[bi])
                        if w1_wb[bi]:
                            c.wait(nc.gpsimd, f"dwbW_{bi}", w1_wb[bi])
                        dsem = f"dw1_{bi}"
                        tok_d = c.inc(nc.gpsimd.dma_start(
                            out=w1sl[bi][:],
                            in_=w1_h[e_][:, hs * 512:(hs + 1) * 512]
                            .rearrange("(dc p) h -> p dc h", p=128)), dsem, 16)
                        c.wait(nc.sync, dsem, tok_d)
                        w1_wb[bi] = c.inc(nc.sync.dma_start(
                            out=w1b_h[e_][:, hs * 512:(hs + 1) * 512]
                            .rearrange("(dc p) h -> p dc h", p=128),
                            in_=w1sl[bi][:]), f"dwbW_{bi}", 16)
                    else:
                        c.wait(nc.sync, "t", w1_buf[bi])
                        dsem = f"dw1s_{bi}"
                        tok_d = c.inc(nc.sync.dma_start(
                            out=w1sl[bi][:],
                            in_=w1b_h[e_][:, hs * 512:(hs + 1) * 512]
                            .rearrange("(dc p) h -> p dc h", p=128)), dsem, 16)
                    return dsem, tok_d

                def issue_w2(e_, hs):
                    bi = hs % 3
                    if r == 0:
                        c.wait(nc.gpsimd, "t", w2_buf[bi])
                        if w2_wb[bi]:
                            c.wait(nc.gpsimd, f"dwb_{bi}", w2_wb[bi])
                        dsem = f"dw2_{bi}"
                        tok_d = c.inc(nc.gpsimd.dma_start(
                            out=w2sl[bi][:],
                            in_=w2_h[e_, hs * 512:(hs + 1) * 512, :]
                            .rearrange("(hcc p) d -> p hcc d", p=128)), dsem, 16)
                        c.wait(nc.scalar, dsem, tok_d)
                        w2_wb[bi] = c.inc(nc.scalar.dma_start(
                            out=w2b_h[e_, hs * 512:(hs + 1) * 512, :]
                            .rearrange("(hcc p) d -> p hcc d", p=128),
                            in_=w2sl[bi][:]), f"dwb_{bi}", 16)
                    else:
                        c.wait(nc.scalar, "t", w2_buf[bi])
                        dsem = f"dw2s_{bi}"
                        tok_d = c.inc(nc.scalar.dma_start(
                            out=w2sl[bi][:],
                            in_=w2b_h[e_, hs * 512:(hs + 1) * 512, :]
                            .rearrange("(hcc p) d -> p hcc d", p=128)), dsem, 16)
                    return dsem, tok_d

                build_P(0)
                for e in range(NE):
                    par = e % 2
                    c.wait(nc.tensor, "v", max(tok_P[par], pso2_tok[0]))
                    for dcc in range(DC):
                        for tch in range(QT):
                            ins = nc.tensor.matmul(
                                psO[2][:, 0:CAP],
                                x1_tm[:, tch, dcc * 128:(dcc + 1) * 128],
                                P_sb[par][:, tch, 0:CAP],
                                start=(tch == 0), stop=(tch == QT - 1))
                        tok_t = c.inc(ins, "t")
                        c.wait(nc.vector, "t", tok_t)
                        c.inc(nc.vector.tensor_copy(
                            out=x1_g[:, dcc, :], in_=psO[2][:, 0:CAP]), "v")
                        c.wait(nc.tensor, "v", c.counts["v"])
                    gath_tok[0] = c.counts["v"]
                    pso2_tok[0] = c.counts["v"]
                    if e + 1 < NE:
                        build_P(e + 1)
                    c.wait(nc.vector, "v")
                    for dcc in range(DC):
                        c.inc(nc.vector.scalar_tensor_tensor(
                            out=acc[:, dcc, :], in0=wbcf[par][:],
                            scalar=b2T[:, dcc, e:e + 1], in1=acc[:, dcc, :],
                            op0=ALU.mult, op1=ALU.add), "v")
                    w1_pend = {i: issue_w1(e, i) for i in range(3)}
                    w2_pend = {i: issue_w2(e, i) for i in range(3)}
                    gel_toks = [0] * HC
                    for hs in range(8):
                        bi = hs % 3
                        dsem, tok_d = w1_pend.pop(hs)
                        c.wait(nc.tensor, dsem, tok_d)
                        for hh in range(4):
                            hc = hs * 4 + hh
                            ph = psO[(hc % 4) // 2][:, 512 * (hc % 2):512 * (hc % 2) + CAP]
                            if hc >= 4:
                                c.wait(nc.tensor, "s", gel_toks[hc - 4])
                            for dcc in range(DC):
                                ins = nc.tensor.matmul(
                                    ph,
                                    w1sl[bi][:, dcc, hh * 128:(hh + 1) * 128],
                                    x1_g[:, dcc, :],
                                    start=(dcc == 0), stop=(dcc == DC - 1))
                            tok_t = c.inc(ins, "t")
                            if hh == 3:
                                w1_buf[bi] = tok_t
                            c.wait(nc.scalar, "t", tok_t)
                            gel_toks[hc] = c.inc(nc.scalar.activation(
                                out=h_sb[:, hc, :], in_=ph, func=AF.Gelu,
                                bias=b1T[:, hc, e:e + 1]), "s")
                        if hs + 3 < 8:
                            w1_pend[hs + 3] = issue_w1(e, hs + 3)
                    for hs in range(8):
                        bi = hs % 3
                        dsem, tok_d = w2_pend.pop(hs)
                        if hs == 0:
                            c.wait(nc.tensor, "s", gel_toks[HC - 1])
                            c.wait(nc.tensor, "v")
                        c.wait(nc.tensor, dsem, tok_d)
                        for hh in range(4):
                            hc = hs * 4 + hh
                            for jt in range(JT):
                                for dh2 in range(2):
                                    ins = nc.tensor.matmul(
                                        psO[jt][:, dh2 * 512:(dh2 + 1) * 512],
                                        h_sb[:, hc, jt * 128:(jt + 1) * 128],
                                        w2sl[bi][:, hh, dh2 * 512:(dh2 + 1) * 512],
                                        start=(hc == 0), stop=(hc == HC - 1))
                        tok_pe = c.inc(ins, "t")
                        w2_buf[bi] = tok_pe
                        if hs + 3 < 8:
                            w2_pend[hs + 3] = issue_w2(e, hs + 3)
                    c.wait(nc.vector, "t")
                    c.wait(nc.vector, "v")
                    for jt in range(JT):
                        c.inc(nc.vector.tensor_copy(
                            out=out_g[:, jt, :], in_=psO[jt][:, 0:DM]), "v")
                    def sc_slot(sc_, dcc_):
                        if sc_ % 2 == 0:
                            return psO[dcc_ // 4][
                                :, (dcc_ % 4) * SCB:(dcc_ % 4) * SCB + SCB]
                        if dcc_ < 4:
                            return psO[2][:, dcc_ * SCB:(dcc_ + 1) * SCB] \
                                if SCB * 4 <= 1024 else psO[2][:, (dcc_ % 4) * SCB:]
                        u = (psU, psU2)[(dcc_ - 4) // 2]
                        return u[:, ((dcc_ - 4) % 2) * SCB:((dcc_ - 4) % 2) * SCB + SCB]
                    c.wait(nc.tensor, "v")
                    sc_add_tok = [0] * NSC
                    for sc in range(NSC):
                        if sc >= 2:
                            c.wait(nc.tensor, "v", sc_add_tok[sc - 2])
                        for dcc in range(DC):
                            for jc in range(JT):
                                ins = nc.tensor.matmul(
                                    sc_slot(sc, dcc),
                                    out_g[:, jc, dcc * 128:(dcc + 1) * 128],
                                    PwT[par][:, jc, sc * SCB:sc * SCB + SCB],
                                    start=(jc == 0), stop=(jc == JT - 1))
                        tok_t = c.inc(ins, "t")
                        c.wait(nc.vector, "t", tok_t)
                        c.wait(nc.vector, "v")
                        for dcc in range(DC):
                            c.inc(nc.vector.tensor_add(
                                acc[:, dcc, sc * SCB:sc * SCB + SCB],
                                acc[:, dcc, sc * SCB:sc * SCB + SCB],
                                sc_slot(sc, dcc)), "v")
                        sc_add_tok[sc] = c.counts["v"]
                    for _i in range(4):
                        psu_tok[_i] = max(psu_tok[_i], c.counts["v"])
                    pso2_tok[0] = max(pso2_tok[0], c.counts["v"])
                    c.wait(nc.tensor, "v")

                for tt2 in range(Q // 128):
                    pb = psO[tt2 % 2]
                    c.wait(nc.tensor, "v")
                    for dcc in range(DC):
                        ins = nc.tensor.transpose(
                            pb[:, dcc * 128:(dcc + 1) * 128],
                            acc[:, dcc, tt2 * 128:(tt2 + 1) * 128], eye32[:])
                    tok_t = c.inc(ins, "t")
                    c.wait(nc.vector, "t", tok_t)
                    c.wait(nc.vector, "v")
                    c.inc(nc.vector.bn_stats(bst[:, 0:6], pb[:, 0:512]), "v")
                    c.inc(nc.vector.bn_stats(bst[:, 6:12], pb[:, 512:1024]), "v")
                    c.wait(nc.vector, "v")
                    c.inc(nc.vector.bn_aggr(mv[:], bst[:]), "v")
                    c.wait(nc.scalar, "v")
                    c.inc(nc.scalar.activation(
                        out=sm0[:, 6:7], in_=mv[:, 1:2], func=AF.Ln,
                        bias=eps128[:]), "s")
                    c.wait(nc.scalar, "s")
                    c.inc(nc.scalar.activation(
                        out=sm0[:, 7:8], in_=sm0[:, 6:7], func=AF.Exp,
                        scale=-0.5), "s")
                    yb = ytile[0]
                    c.wait(nc.vector, "s")
                    c.wait(nc.vector, "dy0")
                    c.inc(nc.vector.tensor_scalar(
                        out=yb[:], in0=pb[:, 0:DM], scalar1=mv[:, 0:1],
                        scalar2=sm0[:, 7:8], op0=ALU.subtract, op1=ALU.mult), "v")
                    tok_v = c.counts["v"]
                    c.wait(nc.tensor, "v")
                    c.wait(nc.sync, "v", tok_v)
                    t0 = q0 + tt2 * 128
                    c.inc(nc.sync.dma_start(
                        out=y_h[t0:t0 + 128, :], in_=yb[:]), "dy0", 16)

                if "t_w" in tap_h:
                    c.wait(nc.sync, "v")
                    c.inc(nc.sync.dma_start(
                        out=tap_h["t_w"][:, q0:q0 + Q], in_=wT8[:]), "dct", 16)
                c.join(ALL_ENG)

        c.wait(nc.sync, "dy0")
        c.wait(nc.sync, "dy1")
    return nc


def kernel(L0_attn_x, L0_attn_Wq, L0_attn_Wk, L0_attn_Wv, L0_attn_Wo,
           L0_ffn_w_router, L0_ffn_w1, L0_ffn_b1, L0_ffn_w2, L0_ffn_b2):
    B, S, D = L0_attn_x.shape
    nc = build_nc(S=S)
    common = {
        "wq": np.ascontiguousarray(L0_attn_Wq, dtype=np.float32),
        "wk": np.ascontiguousarray(L0_attn_Wk, dtype=np.float32),
        "wv": np.ascontiguousarray(L0_attn_Wv, dtype=np.float32),
        "wo": np.ascontiguousarray(L0_attn_Wo, dtype=np.float32),
        "wr": np.ascontiguousarray(L0_ffn_w_router, dtype=np.float32),
        "w1": np.ascontiguousarray(L0_ffn_w1, dtype=np.float32),
        "b1": np.ascontiguousarray(L0_ffn_b1, dtype=np.float32),
        "w2": np.ascontiguousarray(L0_ffn_w2, dtype=np.float32),
        "b2": np.ascontiguousarray(L0_ffn_b2, dtype=np.float32),
    }
    in_maps = []
    for b in range(B):
        m = dict(common)
        m["x"] = np.ascontiguousarray(L0_attn_x[b], dtype=np.float32)
        in_maps.append(m)
    res = run_bass_kernel_spmd(nc, in_maps, core_ids=list(range(B)))
    return np.stack([res.results[b]["y"] for b in range(B)], axis=0)


# version 21
# speedup vs baseline: 1.2788x; 1.0209x over previous
from contextlib import ExitStack

import numpy as np
import ml_dtypes

import concourse.bass as bass
import concourse.mybir as mybir
from concourse.bass_utils import run_bass_kernel_spmd

F32 = mybir.dt.float32
BF16 = mybir.dt.bfloat16
AF = mybir.ActivationFunctionType
ALU = mybir.AluOpType
AX = mybir.AxisListType

DM = 1024
DH = 4096
NE = 8
DC = DM // 128
HC = DH // 128
NEG = -1.0e30
EPS = 1e-5
SCL = 1.0 / 32.0


class Ctx:

    def __init__(self, nc):
        self.nc = nc
        self.sems = {}
        self.counts = {}
        for n in ("t", "v", "s"):
            self.new(n)

    def new(self, n):
        if n not in self.sems:
            self.sems[n] = self.nc.alloc_semaphore(n)
            self.counts[n] = 0

    def inc(self, ins, name, n=1):
        ins.then_inc(self.sems[name], n)
        self.counts[name] += n
        return self.counts[name]

    def wait(self, eng, name, val=None):
        v = self.counts[name] if val is None else val
        if v > 0:
            eng.wait_ge(self.sems[name], v)
        return v

    def join(self, engines):
        for eng in engines:
            for n in self.sems:
                self.wait(eng, n)


def build_nc(S=4096, n_rounds=4, taps=()):
    Q = S // n_rounds
    QT = Q // 128
    TB = min(512, Q)
    NTB = Q // TB
    KG = (Q + 511) // 512
    JT = max(1, -(-3 * Q // (8 * 128)))
    CAP = JT * 128
    SCB = min(256, Q)
    NSC = Q // SCB

    nc = bass.Bass("TRN2", target_bir_lowering=False)
    c = Ctx(nc)
    for _sn in ("dc", "dcg", "dct", "dwa", "dwq", "dwk", "dwv", "dx0", "dx1",
                "dx2", "dx3", "dx32", "dw1_0", "dw1_1", "dw1_2", "dw1s_0", "dw1s_1", "dw1s_2",
                "dwbW_0", "dwbW_1", "dwbW_2",
                "dw2_0", "dw2_1", "dw2_2", "dw2s_0", "dw2s_1", "dw2s_2",
                "dwb_0", "dwb_1", "dwb_2", "dy0", "dy1", "dkc", "dkw"):
        c.new(_sn)

    x_h = nc.dram_tensor("x", [S, DM], F32, kind="ExternalInput").ap()
    wq_h = nc.dram_tensor("wq", [DM, DM], F32, kind="ExternalInput").ap()
    wk_h = nc.dram_tensor("wk", [DM, DM], F32, kind="ExternalInput").ap()
    wv_h = nc.dram_tensor("wv", [DM, DM], F32, kind="ExternalInput").ap()
    wo_h = nc.dram_tensor("wo", [DM, DM], F32, kind="ExternalInput").ap()
    wr_h = nc.dram_tensor("wr", [DM, NE], F32, kind="ExternalInput").ap()
    w1_h = nc.dram_tensor("w1", [NE, DM, DH], F32, kind="ExternalInput").ap()
    b1_h = nc.dram_tensor("b1", [NE, DH], F32, kind="ExternalInput").ap()
    w2_h = nc.dram_tensor("w2", [NE, DH, DM], F32, kind="ExternalInput").ap()
    b2_h = nc.dram_tensor("b2", [NE, DM], F32, kind="ExternalInput").ap()
    y_h = nc.dram_tensor("y", [S, DM], F32, kind="ExternalOutput").ap()
    w1b_h = nc.dram_tensor("w1b", [NE, DM, DH], BF16, kind="Internal").ap()
    w2b_h = nc.dram_tensor("w2b", [NE, DH, DM], BF16, kind="Internal").ap()
    wvob_h = nc.dram_tensor("wvob", [DM, DM], BF16, kind="Internal").ap()
    kc_h = nc.dram_tensor("kcache", [n_rounds, 128, DC, Q], BF16, kind="Internal").ap()
    vc_h = nc.dram_tensor("vcache", [n_rounds, 128, QT, DM], BF16, kind="Internal").ap()
    tap_h = {}
    for tn, shp in (("t_r", [DM, S]), ("t_x1", [DM, S]), ("t_w", [NE, S]),
                    ("t_acc", [DM, S])):
        if tn in taps:
            tap_h[tn] = nc.dram_tensor(tn, shp, F32, kind="ExternalOutput").ap()

    eye16_c = nc.inline_tensor(np.eye(128, dtype=ml_dtypes.bfloat16), "eye16c")
    eye32_c = nc.inline_tensor(np.eye(128, dtype=np.float32), "eye32c")
    tri = np.where(np.arange(128)[None, :] <= np.arange(128)[:, None], 0.0, NEG)
    tri_c = nc.inline_tensor(tri.astype(np.float32), "tric")
    ones_col_c = nc.inline_tensor(np.ones((128, 1), dtype=ml_dtypes.bfloat16), "onescolc")
    ones_row_c = nc.inline_tensor(np.ones((1, 128), dtype=np.float32), "onesrowc")
    sel = np.zeros((NE, NE * 128), dtype=np.float32)
    for e in range(NE):
        sel[e, e * 128:(e + 1) * 128] = 1.0
    sel_c = nc.inline_tensor(sel, "selc")
    jrow_c = nc.inline_tensor(
        (np.arange(CAP, dtype=np.float32) + 1.0)[None, :].repeat(128, 0), "jrowc")
    jcol_c = nc.inline_tensor(
        (np.arange(JT * 128, dtype=np.float32) + 1.0).reshape(JT, 128).T.copy(),
        "jcolc")
    eps_c = nc.inline_tensor(np.full((1, 1), EPS, dtype=np.float32), "epsc")
    eps128_c = nc.inline_tensor(np.full((128, 1), EPS, dtype=np.float32), "eps128c")
    triu_c = nc.inline_tensor(
        np.triu(np.ones((128, 128))).astype(ml_dtypes.bfloat16), "triuc")

    ALL_ENG = (nc.tensor, nc.vector, nc.scalar, nc.gpsimd, nc.sync)

    with ExitStack() as P:
        acc = P.enter_context(nc.sbuf_tensor("acc", [128, DC, Q], F32))
        x1q = P.enter_context(nc.sbuf_tensor("x1q", [128, DC, Q], BF16))
        eye16 = P.enter_context(nc.sbuf_tensor("eye16", [128, 128], BF16))
        eye32 = P.enter_context(nc.sbuf_tensor("eye32", [128, 128], F32))
        trim = P.enter_context(nc.sbuf_tensor("trim", [128, 128], F32))
        ones_col = P.enter_context(nc.sbuf_tensor("ones_col", [128, 1], BF16))
        ones_row = P.enter_context(nc.sbuf_tensor("ones_row", [1, 128], F32))
        sel_sb = P.enter_context(nc.sbuf_tensor("sel_sb", [NE, NE * 128], F32))
        b1T = P.enter_context(nc.sbuf_tensor("b1T", [128, HC, NE], F32))
        b2T = P.enter_context(nc.sbuf_tensor("b2T", [128, DC, NE], F32))
        wr_sb = P.enter_context(nc.sbuf_tensor("wr_sb", [128, DC, NE], BF16))
        m_run = P.enter_context(nc.sbuf_tensor("m_run", [128, QT], F32))
        l_run = P.enter_context(nc.sbuf_tensor("l_run", [128, QT], F32))
        sm0 = P.enter_context(nc.sbuf_tensor("sm0", [128, 8], F32))
        sm1 = P.enter_context(nc.sbuf_tensor("sm1", [128, 8], F32))
        epsb = P.enter_context(nc.sbuf_tensor("epsb", [1, 1], F32))
        eps128 = P.enter_context(nc.sbuf_tensor("eps128", [128, 1], F32))
        bst = P.enter_context(nc.sbuf_tensor("bst", [128, 12], F32))
        mv = P.enter_context(nc.sbuf_tensor("mv", [128, 2], F32))
        triu_sb = P.enter_context(nc.sbuf_tensor("triu_sb", [128, 128], BF16))
        jrow = P.enter_context(nc.sbuf_tensor("jrow", [128, CAP], F32))
        jcol = P.enter_context(nc.sbuf_tensor("jcol", [128, JT], F32))
        x1_tm = P.enter_context(nc.sbuf_tensor("x1_tm", [128, QT, DM], BF16))

        with ExitStack() as PH:
            b1raw = PH.enter_context(nc.sbuf_tensor("b1raw", [NE, DH], F32))
            b2raw = PH.enter_context(nc.sbuf_tensor("b2raw", [NE, DM], F32))
            wv_tmp = PH.enter_context(nc.sbuf_tensor("wv_tmp", [128, DC, DM], BF16))
            wvT_tmp = PH.enter_context(nc.sbuf_tensor("wvT_tmp", [128, DC, DM], BF16))
            wvo_pre = PH.enter_context(nc.sbuf_tensor("wvo_pre", [128, DC, DM], BF16))
            psP = PH.enter_context(nc.psum_tensor("psP", [128, 512], F32))
            psP_T = PH.enter_context(nc.psum_tensor("psP_T", [128, 1024], BF16))

            for dst, src in ((eye16, eye16_c), (eye32, eye32_c), (trim, tri_c),
                             (ones_col, ones_col_c), (ones_row, ones_row_c),
                             (sel_sb, sel_c), (epsb, eps_c), (triu_sb, triu_c),
                             (jrow, jrow_c), (jcol, jcol_c), (eps128, eps128_c)):
                c.inc(nc.sync.dma_start(out=dst[:], in_=src.ap()), "dc", 16)
            c.inc(nc.sync.dma_start(out=b1raw[:], in_=b1_h), "dc", 16)
            c.inc(nc.sync.dma_start(out=b2raw[:], in_=b2_h), "dc", 16)
            c.inc(nc.gpsimd.dma_start(
                out=wr_sb[:], in_=wr_h.rearrange("(dc p) e -> p dc e", p=128)), "dwa", 16)
            c.wait(nc.tensor, "dc")
            for hc in range(HC):
                tok_t = c.inc(nc.tensor.transpose(
                    psP[:, 0:NE], b1raw[0:NE, hc * 128:(hc + 1) * 128],
                    eye32[0:NE, 0:NE]), "t")
                c.wait(nc.vector, "t", tok_t)
                c.inc(nc.vector.tensor_copy(out=b1T[:, hc, :], in_=psP[:, 0:NE]), "v")
                c.wait(nc.tensor, "v")
            for dc in range(DC):
                tok_t = c.inc(nc.tensor.transpose(
                    psP[:, 8:8 + NE], b2raw[0:NE, dc * 128:(dc + 1) * 128],
                    eye32[0:NE, 0:NE]), "t")
                c.wait(nc.vector, "t", tok_t)
                c.inc(nc.vector.tensor_copy(out=b2T[:, dc, :], in_=psP[:, 8:8 + NE]), "v")
                c.wait(nc.tensor, "v")

            tok_wv = c.inc(nc.gpsimd.dma_start(
                out=wv_tmp[:], in_=wv_h.rearrange("(dc p) n -> p dc n", p=128)), "dwa", 16)
            c.wait(nc.tensor, "dwa", tok_wv)
            for dc in range(DC):
                for dvc in range(DC):
                    c.inc(nc.tensor.transpose(
                        psP_T[:, dvc * 128:(dvc + 1) * 128],
                        wv_tmp[:, dc, dvc * 128:(dvc + 1) * 128], eye16[:]), "t")
                c.wait(nc.vector, "t")
                c.inc(nc.vector.tensor_copy(
                    out=wvT_tmp[:, dc, :], in_=psP_T[:, 0:DM]), "v")
                c.wait(nc.tensor, "v")
            c.wait(nc.gpsimd, "t")
            tok_wo = c.inc(nc.gpsimd.dma_start(
                out=wv_tmp[:], in_=wo_h.rearrange("(dc p) n -> p dc n", p=128)), "dwa", 16)
            c.wait(nc.tensor, "dwa", tok_wo)
            for dc in range(DC):
                for half in range(DM // 512):
                    for dvc in range(DC):
                        ins = nc.tensor.matmul(
                            psP[:, 0:512],
                            wvT_tmp[:, dc, dvc * 128:(dvc + 1) * 128],
                            wv_tmp[:, dvc, half * 512:half * 512 + 512],
                            start=(dvc == 0), stop=(dvc == DC - 1))
                    tok_t = c.inc(ins, "t")
                    c.wait(nc.vector, "t", tok_t)
                    c.inc(nc.vector.tensor_copy(
                        out=wvo_pre[:, dc, half * 512:half * 512 + 512],
                        in_=psP[:, 0:512]), "v")
                    c.wait(nc.tensor, "v")
            c.wait(nc.sync, "v")
            c.inc(nc.sync.dma_start(
                out=wvob_h.rearrange("(dc p) n -> p dc n", p=128),
                in_=wvo_pre[:]), "dc", 16)
            c.join(ALL_ENG)

        for r in range(n_rounds):
            q0 = r * Q

            with ExitStack() as PA:
                sb = lambda nm, shp, dt: PA.enter_context(
                    nc.sbuf_tensor(f"{nm}_r{r}", shp, dt))
                psb = lambda nm, shp, dt: PA.enter_context(
                    nc.psum_tensor(f"{nm}_r{r}", shp, dt))
                q_fm = sb("q_fm", [128, DC, Q], BF16)
                k_blk = sb("k_blk", [128, DC, Q], BF16)
                v_blk = sb("v_blk", [128, QT, DM], BF16)
                wq_sb = sb("wq_sb", [128, DC, DM], BF16)
                wk_sb = sb("wk_sb", [128, DC, DM], BF16)
                wvo_sb = sb("wvo_sb", [128, DC, DM], BF16)
                xt16 = [sb(f"xt16_{i}", [128, 2, DM], BF16) for i in range(2)]
                xf16 = sb("xf16", [128, DC, 512], BF16)
                xt32 = sb("xt32", [128, DM], F32)
                rtm = sb("rtm", [128, DM], F32)
                rowst = sb("rowst", [1, 128], F32)
                s_row = sb("s_row", [128, Q], BF16)
                p16 = sb("p16", [128, Q], BF16)
                attnT = sb("attnT", [128, QT, 128], BF16)

                psA_s = psb("psA_s", [128, 512], F32)
                psA_s2 = psb("psA_s2", [128, 512], F32)
                psA_pv = [psb(f"psA_pv{i}", [128, 1024], F32) for i in range(2)]
                psA_T = psb("psA_T", [128, 1024], BF16)
                psA_m = psb("psA_m", [128, 512], F32)

                def load_w_fm(dst, src, sem):
                    return c.inc(nc.gpsimd.dma_start(
                        out=dst[:], in_=src.rearrange("(dc p) n -> p dc n", p=128)),
                        sem, 16)

                tok_wq = load_w_fm(wq_sb, wq_h, "dwq")
                tok_wk = load_w_fm(wk_sb, wk_h, "dwk")
                tok_wv = c.inc(nc.sync.dma_start(
                    out=wvo_sb[:],
                    in_=wvob_h.rearrange("(dc p) n -> p dc n", p=128)), "dwv", 16)

                c.wait(nc.tensor, "dwq", tok_wq)
                xb_tok = [0, 0]
                sbank_tok = [0, 0]

                def load_x_tiles(base, gt_, use):
                    np_ = (gt_ + 1) // 2
                    toks = []
                    for p_ in range(np_):
                        nrow = min(256, gt_ * 128 - p_ * 256)
                        bi = p_ % 2
                        c.wait(nc.gpsimd, "t", xb_tok[bi])
                        toks.append((bi, c.inc(nc.gpsimd.dma_start(
                            out=xt16[bi][:, 0:nrow // 128, :],
                            in_=x_h[base + p_ * 256:base + p_ * 256 + nrow, :]
                            .rearrange("(b p) d -> p b d", p=128)),
                            f"dx{bi}", 16), nrow // 128, p_))
                    for bi, tok_d, nt, p_ in toks:
                        c.wait(nc.tensor, f"dx{bi}", tok_d)
                        for j_ in range(nt):
                            use(p_ * 2 + j_, xt16[bi][:, j_, :])
                        xb_tok[bi] = c.counts["t"]

                for g in range(KG):
                    gn = min(512, Q - g * 512)
                    gt = gn // 128

                    def use_pre(i, src_):
                        for dcc in range(DC):
                            c.inc(nc.tensor.transpose(
                                psA_T[:, dcc * 128:(dcc + 1) * 128],
                                src_[:, dcc * 128:(dcc + 1) * 128], eye16[:]), "t")
                        c.wait(nc.vector, "t")
                        c.inc(nc.vector.tensor_copy(
                            out=xf16[:, :, i * 128:(i + 1) * 128],
                            in_=psA_T[:, 0:DM].rearrange("p (a b) -> p a b", b=128)), "v")
                        c.wait(nc.tensor, "v")

                    load_x_tiles(q0 + g * 512, gt, use_pre)
                    for kc in range(DC):
                        for dcc in range(DC):
                            ins = nc.tensor.matmul(
                                psA_s[:, 0:gn], wq_sb[:, dcc, kc * 128:(kc + 1) * 128],
                                xf16[:, dcc, 0:gn],
                                start=(dcc == 0), stop=(dcc == DC - 1))
                        tok_t = c.inc(ins, "t")
                        c.wait(nc.vector, "t", tok_t)
                        c.inc(nc.vector.tensor_copy(
                            out=q_fm[:, kc, g * 512:g * 512 + gn], in_=psA_s[:, 0:gn]), "v")
                        c.wait(nc.tensor, "v")

                for kb in range(r + 1):
                    kb0 = kb * Q
                    if kb < r:
                        c.wait(nc.sync, "t")
                        tok_kc = c.inc(nc.sync.dma_start(
                            out=k_blk[:], in_=kc_h[kb]), "dkc", 16)
                        tok_vc = c.inc(nc.sync.dma_start(
                            out=v_blk[:], in_=vc_h[kb]), "dkc", 16)
                        c.wait(nc.tensor, "dkc", tok_vc)
                    if kb == r:
                        c.wait(nc.tensor, "dwk", tok_wk)
                        for g in range(KG):
                            gn = min(512, Q - g * 512)
                            gt = gn // 128
                            def use_kv(i, src, g=g):
                                for dcc in range(DC):
                                    c.inc(nc.tensor.transpose(
                                        psA_T[:, dcc * 128:(dcc + 1) * 128],
                                        src[:, dcc * 128:(dcc + 1) * 128], eye16[:]), "t")
                                c.wait(nc.vector, "t")
                                c.inc(nc.vector.tensor_copy(
                                    out=xf16[:, :, i * 128:(i + 1) * 128],
                                    in_=psA_T[:, 0:DM].rearrange("p (a b) -> p a b", b=128)), "v")
                                c.wait(nc.tensor, "v")
                                c.wait(nc.tensor, "dwv", tok_wv)
                                for half in range(DM // 512):
                                    for dcc in range(DC):
                                        ins = nc.tensor.matmul(
                                            psA_pv[0][:, half * 512:half * 512 + 512],
                                            xf16[:, dcc, i * 128:(i + 1) * 128],
                                            wvo_sb[:, dcc, half * 512:half * 512 + 512],
                                            start=(dcc == 0), stop=(dcc == DC - 1))
                                tok_t = c.inc(ins, "t")
                                c.wait(nc.vector, "t", tok_t)
                                c.inc(nc.vector.tensor_copy(
                                    out=v_blk[:, g * 4 + i, :], in_=psA_pv[0][:, 0:DM]), "v")
                                c.wait(nc.tensor, "v")

                            load_x_tiles(kb0 + g * 512, gt, use_kv)
                            for kc in range(DC):
                                for dcc in range(DC):
                                    ins = nc.tensor.matmul(
                                        psA_s[:, 0:gn],
                                        wk_sb[:, dcc, kc * 128:(kc + 1) * 128],
                                        xf16[:, dcc, 0:gn],
                                        start=(dcc == 0), stop=(dcc == DC - 1))
                                tok_t = c.inc(ins, "t")
                                c.wait(nc.vector, "t", tok_t)
                                c.inc(nc.vector.tensor_copy(
                                    out=k_blk[:, kc, g * 512:g * 512 + gn],
                                    in_=psA_s[:, 0:gn]), "v")
                                c.wait(nc.tensor, "v")

                    if kb == r and r + 1 < n_rounds:
                        c.wait(nc.sync, "v")
                        c.inc(nc.sync.dma_start(out=kc_h[r], in_=k_blk[:]), "dkw", 16)
                        c.inc(nc.sync.dma_start(out=vc_h[r], in_=v_blk[:]), "dkw", 16)
                    diag = (kb == r)
                    for qt in range(QT):
                        tk = (qt + 1) * 128 if diag else Q
                        ng = (tk + 511) // 512
                        for g in range(ng):
                            n0 = g * 512
                            nn = min(512, tk - n0)
                            ps = psA_s if g % 2 == 0 else psA_s2
                            c.wait(nc.tensor, "v", sbank_tok[g % 2])
                            for kc in range(DC):
                                ins = nc.tensor.matmul(
                                    ps[:, 0:nn], q_fm[:, kc, qt * 128:(qt + 1) * 128],
                                    k_blk[:, kc, n0:n0 + nn],
                                    start=(kc == 0), stop=(kc == DC - 1))
                            tok_t = c.inc(ins, "t")
                            c.wait(nc.vector, "t", tok_t)
                            nmask = 128 if (diag and g == ng - 1) else 0
                            if nn - nmask > 0:
                                c.inc(nc.vector.tensor_copy(
                                    out=s_row[:, n0:n0 + nn - nmask],
                                    in_=ps[:, 0:nn - nmask]), "v")
                            if nmask:
                                c.inc(nc.vector.tensor_add(
                                    s_row[:, n0 + nn - 128:n0 + nn],
                                    ps[:, nn - 128:nn], trim[:]), "v")
                            sbank_tok[g % 2] = c.counts["v"]
                        c.wait(nc.vector, "v")
                        c.inc(nc.vector.reduce_max(
                            sm0[:, 0:1], s_row[:, 0:tk], axis=AX.X), "v")
                        mcur = m_run[:, qt:qt + 1]
                        c.wait(nc.vector, "v")
                        if kb == 0:
                            c.inc(nc.vector.tensor_copy(out=mcur, in_=sm0[:, 0:1]), "v")
                        else:
                            c.inc(nc.vector.tensor_tensor(
                                sm1[:, 0:1], mcur, sm0[:, 0:1], op=ALU.max), "v")
                            c.wait(nc.vector, "v")
                            c.inc(nc.vector.tensor_sub(
                                sm0[:, 1:2], mcur, sm1[:, 0:1]), "v")
                            c.wait(nc.vector, "v")
                            c.inc(nc.vector.tensor_copy(out=mcur, in_=sm1[:, 0:1]), "v")
                            c.wait(nc.scalar, "v")
                            c.inc(nc.scalar.activation(
                                out=sm0[:, 2:3], in_=sm0[:, 1:2], func=AF.Exp,
                                scale=SCL), "s")
                        c.wait(nc.vector, "v")
                        c.inc(nc.vector.tensor_scalar_mul(sm0[:, 3:4], mcur, -SCL), "v")
                        c.wait(nc.scalar, "v")
                        c.inc(nc.scalar.activation(
                            out=p16[:, 0:tk], in_=s_row[:, 0:tk], func=AF.Exp,
                            bias=sm0[:, 3:4], scale=SCL, accum_out=sm0[:, 4:5]), "s")
                        c.wait(nc.vector, "s")
                        if kb == 0:
                            c.inc(nc.vector.tensor_copy(
                                out=l_run[:, qt:qt + 1], in_=sm0[:, 4:5]), "v")
                        else:
                            c.inc(nc.vector.scalar_tensor_tensor(
                                out=l_run[:, qt:qt + 1], in0=l_run[:, qt:qt + 1],
                                scalar=sm0[:, 2:3], in1=sm0[:, 4:5],
                                op0=ALU.mult, op1=ALU.add), "v")
                        ntk = tk // 128
                        c.wait(nc.tensor, "s")
                        c.wait(nc.tensor, "v")
                        for tkc in range(ntk):
                            c.inc(nc.tensor.transpose(
                                psA_T[:, tkc * 128:(tkc + 1) * 128],
                                p16[:, tkc * 128:(tkc + 1) * 128], eye16[:]), "t")
                        c.wait(nc.vector, "t")
                        c.inc(nc.vector.tensor_copy(
                            out=attnT[:, 0:ntk, :],
                            in_=psA_T[:, 0:ntk * 128].rearrange(
                                "p (a b) -> p a b", b=128)), "v")
                        pvb = psA_pv[qt % 2]
                        c.wait(nc.tensor, "v")
                        for dvc in range(DC):
                            for tkc in range(ntk):
                                ins = nc.tensor.matmul(
                                    pvb[:, dvc * 128:(dvc + 1) * 128],
                                    v_blk[:, tkc, dvc * 128:(dvc + 1) * 128],
                                    attnT[:, tkc, :],
                                    start=(tkc == 0), stop=(tkc == ntk - 1))
                        tok_pv = c.inc(ins, "t")
                        accs = acc[:, :, qt * 128:(qt + 1) * 128]
                        c.wait(nc.vector, "t", tok_pv)
                        if kb == 0:
                            c.inc(nc.vector.tensor_copy(
                                out=accs,
                                in_=pvb[:, 0:DM].rearrange("p (a b) -> p a b", b=128)), "v")
                        else:
                            c.wait(nc.tensor, "s")
                            tok_t = c.inc(nc.tensor.transpose(
                                psA_m[0:1, 0:128], sm0[:, 2:3], eye32[:]), "t")
                            c.wait(nc.vector, "t", tok_t)
                            c.inc(nc.vector.tensor_copy(
                                out=rowst[0:1, 0:128], in_=psA_m[0:1, 0:128]), "v")
                            c.wait(nc.tensor, "v")
                            tok_t = c.inc(nc.tensor.matmul(
                                psA_m[:, 128:256], ones_row[:], rowst[0:1, 0:128],
                                start=True, stop=True), "t")
                            c.wait(nc.vector, "t", tok_t)
                            for dcc in range(DC):
                                c.inc(nc.vector.tensor_mul(
                                    accs[:, dcc, :], accs[:, dcc, :],
                                    psA_m[:, 128:256]), "v")
                            c.wait(nc.vector, "v")
                            c.inc(nc.vector.tensor_add(
                                accs, accs,
                                pvb[:, 0:DM].rearrange("p (a b) -> p a b", b=128)), "v")
                        c.wait(nc.tensor, "v")

                x1tm_tok = [0] * QT
                for qt in range(QT):
                    t0 = q0 + qt * 128
                    c.wait(nc.sync, "v")
                    tok_d = c.inc(nc.sync.dma_start(
                        out=xt32[:], in_=x_h[t0:t0 + 128, :]), "dx32", 16)
                    c.wait(nc.vector, "v")
                    c.inc(nc.vector.reciprocal(sm0[:, 5:6], l_run[:, qt:qt + 1]), "v")
                    pp = psA_pv[qt % 2]
                    c.wait(nc.tensor, "v")
                    for dcc in range(DC):
                        ins = nc.tensor.transpose(
                            pp[:, dcc * 128:(dcc + 1) * 128],
                            acc[:, dcc, qt * 128:(qt + 1) * 128], eye32[:])
                    tok_t = c.inc(ins, "t")
                    c.wait(nc.vector, "t", tok_t)
                    c.wait(nc.vector, "dx32", tok_d)
                    c.wait(nc.vector, "v")
                    c.inc(nc.vector.scalar_tensor_tensor(
                        out=rtm[:], in0=pp[:, 0:DM], scalar=sm0[:, 5:6],
                        in1=xt32[:], op0=ALU.mult, op1=ALU.add), "v")
                    c.wait(nc.vector, "v")
                    c.inc(nc.vector.bn_stats(bst[:, 0:6], rtm[:, 0:512]), "v")
                    c.inc(nc.vector.bn_stats(bst[:, 6:12], rtm[:, 512:1024]), "v")
                    c.wait(nc.vector, "v")
                    c.inc(nc.vector.bn_aggr(mv[:], bst[:]), "v")
                    c.wait(nc.scalar, "v")
                    c.inc(nc.scalar.activation(
                        out=sm0[:, 6:7], in_=mv[:, 1:2], func=AF.Ln,
                        bias=eps128[:]), "s")
                    c.wait(nc.scalar, "s")
                    c.inc(nc.scalar.activation(
                        out=sm0[:, 7:8], in_=sm0[:, 6:7], func=AF.Exp,
                        scale=-0.5), "s")
                    c.wait(nc.vector, "s")
                    x1tm_tok[qt] = c.inc(nc.vector.tensor_scalar(
                        out=x1_tm[:, qt, :], in0=rtm[:], scalar1=mv[:, 0:1],
                        scalar2=sm0[:, 7:8], op0=ALU.subtract, op1=ALU.mult), "v")
                trev_tok = 0
                for qt in range(QT):
                    c.wait(nc.tensor, "v", max(x1tm_tok[qt], trev_tok))
                    for dcc in range(DC):
                        ins = nc.tensor.transpose(
                            psA_T[:, dcc * 128:(dcc + 1) * 128],
                            x1_tm[:, qt, dcc * 128:(dcc + 1) * 128], eye16[:])
                    tok_t = c.inc(ins, "t")
                    c.wait(nc.vector, "t", tok_t)
                    trev_tok = c.inc(nc.vector.tensor_copy(
                        out=x1q[:, :, qt * 128:(qt + 1) * 128],
                        in_=psA_T[:, 0:DM].rearrange("p (a b) -> p a b", b=128)), "v")
                c.wait(nc.tensor, "v")

                if "t_x1" in tap_h:
                    c.wait(nc.gpsimd, "v")
                    c.inc(nc.gpsimd.dma_start(
                        out=tap_h["t_x1"][:, q0:q0 + Q].rearrange(
                            "(dc p) t -> p dc t", p=128),
                        in_=x1q[:]), "dcg", 16)
                if "t_acc" in tap_h:
                    c.wait(nc.gpsimd, "v")
                    c.inc(nc.gpsimd.dma_start(
                        out=tap_h["t_acc"][:, q0:q0 + Q].rearrange(
                            "(dc p) t -> p dc t", p=128),
                        in_=acc[:]), "dcg", 16)
                c.join(ALL_ENG)

            with ExitStack() as PF:
                sb = lambda nm, shp, dt: PF.enter_context(
                    nc.sbuf_tensor(f"{nm}_r{r}", shp, dt))
                psb = lambda nm, shp, dt: PF.enter_context(
                    nc.psum_tensor(f"{nm}_r{r}", shp, dt))
                w1sl = [sb(f"w1sl{i}", [128, DC, 512], BF16) for i in range(3)]
                w2sl = [sb(f"w2sl{i}", [128, 4, DM], BF16) for i in range(3)]
                h_sb = sb("h_sb", [128, HC, CAP], BF16)
                x1_g = sb("x1_g", [128, DC, CAP], BF16)
                out_g = sb("out_g", [128, JT, DM], BF16)
                P_sb = [sb(f"P_sb{i}", [128, QT, CAP], BF16) for i in range(2)]
                PwT = [sb(f"PwT{i}", [128, JT, Q], BF16) for i in range(2)]
                selm = sb("selm", [128, QT, NE], BF16)
                selm32 = sb("selm32", [128, QT, NE], F32)
                cumg = sb("cumg", [128, QT, NE], F32)
                offr = sb("offr", [1, QT, NE], F32)
                totr = sb("totr", [1, QT * NE], F32)
                cumgT = sb("cumgT", [NE, Q], F32)
                wbcf = [sb(f"wbcf{i}", [128, Q], BF16) for i in range(2)]
                tmpP = sb("tmpP", [128, 128], F32)
                ytile = [sb("yt0", [128, DM], F32)]
                wT8 = sb("wT8", [NE, Q], F32)
                wT8b = sb("wT8b", [NE, Q], BF16)
                prb = sb("prb", [128, NE], F32)

                psO = [psb(f"psO{i}", [128, 1024], F32) for i in range(3)]
                psU = psb("psU", [128, 512], F32)
                psU2 = psb("psU2", [128, 512], F32)

                for tt in range(QT):
                    c.wait(nc.tensor, "v")
                    for dcc in range(DC):
                        ins = nc.tensor.matmul(
                            psU[:, 0:NE], x1q[:, dcc, tt * 128:(tt + 1) * 128],
                            wr_sb[:, dcc, :], start=(dcc == 0), stop=(dcc == DC - 1))
                    tok_t = c.inc(ins, "t")
                    c.wait(nc.vector, "t", tok_t)
                    c.inc(nc.vector.reduce_max(
                        sm0[:, 0:1], psU[:, 0:NE], axis=AX.X), "v")
                    c.wait(nc.vector, "v")
                    c.inc(nc.vector.tensor_scalar_mul(sm0[:, 1:2], sm0[:, 0:1], -1.0), "v")
                    c.wait(nc.scalar, "v")
                    c.inc(nc.scalar.activation(
                        out=prb[:], in_=psU[:, 0:NE], func=AF.Exp,
                        bias=sm0[:, 1:2], accum_out=sm0[:, 2:3]), "s")
                    c.wait(nc.vector, "s")
                    c.inc(nc.vector.reciprocal(sm0[:, 3:4], sm0[:, 2:3]), "v")
                    c.wait(nc.vector, "v")
                    c.inc(nc.vector.tensor_scalar_mul(prb[:], prb[:], sm0[:, 3:4]), "v")
                    c.wait(nc.vector, "v")
                    c.inc(nc.vector.reduce_max(sm0[:, 4:5], prb[:], axis=AX.X), "v")
                    c.wait(nc.vector, "v")
                    c.inc(nc.vector.tensor_scalar(
                        out=sm1[:, 0:NE], in0=prb[:], scalar1=sm0[:, 4:5],
                        scalar2=None, op0=ALU.is_ge), "v")
                    c.wait(nc.vector, "v")
                    c.inc(nc.vector.scalar_tensor_tensor(
                        out=sm1[:, 0:NE], in0=sm1[:, 0:NE], scalar=NEG,
                        in1=prb[:], op0=ALU.mult, op1=ALU.add), "v")
                    c.wait(nc.vector, "v")
                    c.inc(nc.vector.reduce_max(sm0[:, 5:6], sm1[:, 0:NE], axis=AX.X), "v")
                    c.wait(nc.vector, "v")
                    c.inc(nc.vector.tensor_scalar(
                        out=sm1[:, 0:NE], in0=prb[:], scalar1=sm0[:, 5:6],
                        scalar2=None, op0=ALU.is_ge), "v")
                    c.wait(nc.vector, "v")
                    c.inc(nc.vector.tensor_copy(out=selm[:, tt, :], in_=sm1[:, 0:NE]), "v")
                    c.inc(nc.vector.tensor_copy(out=selm32[:, tt, :], in_=sm1[:, 0:NE]), "v")
                    c.wait(nc.vector, "v")
                    c.inc(nc.vector.tensor_mul(prb[:], prb[:], sm1[:, 0:NE]), "v")
                    c.wait(nc.tensor, "s")
                    c.wait(nc.tensor, "v")
                    tok_t = c.inc(nc.tensor.transpose(
                        psU[0:NE, 128:256], prb[:], eye32[:]), "t")
                    c.wait(nc.vector, "t", tok_t)
                    c.inc(nc.vector.tensor_copy(
                        out=wT8[:, tt * 128:(tt + 1) * 128], in_=psU[0:NE, 128:256]), "v")
                    c.wait(nc.tensor, "v")
                c.wait(nc.vector, "v")
                c.inc(nc.vector.tensor_copy(out=wT8b[:], in_=wT8[:]), "v")

                c.wait(nc.tensor, "v")
                for tt in range(QT):
                    ins = nc.tensor.matmul(
                        psU[0:1, 256:256 + NE].rearrange("a b -> a b") if False
                        else psU[0:1, 256 + tt * NE:256 + (tt + 1) * NE],
                        ones_col[:], selm[:, tt, :], start=True, stop=True)
                tok_t = c.inc(ins, "t")
                c.wait(nc.vector, "t", tok_t)
                c.inc(nc.vector.tensor_copy(
                    out=totr[0:1, :], in_=psU[0:1, 256:256 + QT * NE]), "v")
                c.wait(nc.vector, "v")
                c.inc(nc.vector.memset(offr[0:1, 0, :], 0.0), "v")
                for tt in range(1, QT):
                    c.wait(nc.vector, "v")
                    c.inc(nc.vector.tensor_add(
                        offr[0:1, tt, :], offr[0:1, tt - 1, :],
                        totr[0:1, (tt - 1) * NE:tt * NE]), "v")
                c.wait(nc.tensor, "v")
                for tt in range(QT):
                    ins = nc.tensor.matmul(
                        psU[:, 0:NE], triu_sb[:], selm[:, tt, :],
                        start=True, stop=True)
                    tok_t = c.inc(ins, "t")
                    ins = nc.tensor.matmul(
                        psU[:, NE:2 * NE], ones_row[:], offr[0:1, tt, :],
                        start=True, stop=True)
                    tok_t2 = c.inc(ins, "t")
                    c.wait(nc.vector, "t", tok_t2)
                    c.inc(nc.vector.tensor_copy(
                        out=cumg[:, tt, :], in_=psU[:, 0:NE]), "v")
                    c.wait(nc.vector, "v")
                    c.inc(nc.vector.tensor_add(
                        cumg[:, tt, :], cumg[:, tt, :], psU[:, NE:2 * NE]), "v")
                    c.wait(nc.tensor, "v")

                c.wait(nc.tensor, "v")
                for tt in range(QT):
                    tok_t = c.inc(nc.tensor.transpose(
                        psU[0:NE, 0:128], cumg[:, tt, :], eye32[:]), "t")
                    c.wait(nc.vector, "t", tok_t)
                    c.inc(nc.vector.tensor_copy(
                        out=cumgT[:, tt * 128:(tt + 1) * 128], in_=psU[0:NE, 0:128]), "v")
                    c.wait(nc.tensor, "v")

                c.wait(nc.vector, "v")
                for dcc in range(DC):
                    c.inc(nc.vector.tensor_copy(
                        out=acc[:, dcc, :], in_=x1q[:, dcc, :]), "v")

                w1_buf = [0, 0, 0]
                w1_wb = [0, 0, 0]
                w2_buf = [0, 0, 0]
                w2_wb = [0, 0, 0]
                gath_tok = [0]
                pso2_tok = [0]
                b2r_tok = [0]
                tok_P = [0, 0]
                psu_tok = [0, 0, 0, 0]

                def bslot(i):
                    if i == 0:
                        return psU[:, 0:256]
                    if i == 1:
                        return psU2[:, 0:256]
                    if i == 2:
                        return psO[2][:, 0:256]
                    return psO[2][:, 512:768]

                def build_P(e_):
                    par = e_ % 2
                    for tt in range(QT):
                        sl = tt % 4
                        pu = bslot(sl)
                        c.wait(nc.tensor, "v", max(psu_tok[sl], gath_tok[0]))
                        c.inc(nc.tensor.matmul(
                            pu[:, 0:128], sel_sb[:, e_ * 128:(e_ + 1) * 128],
                            wT8[:, tt * 128:(tt + 1) * 128], start=True, stop=True), "t")
                        tok_t = c.inc(nc.tensor.matmul(
                            pu[:, 128:256],
                            sel_sb[:, e_ * 128:(e_ + 1) * 128],
                            cumgT[:, tt * 128:(tt + 1) * 128], start=True, stop=True), "t")
                        c.wait(nc.vector, "t", tok_t)
                        c.wait(nc.vector, "v")
                        c.inc(nc.vector.tensor_copy(
                            out=wbcf[par][:, tt * 128:(tt + 1) * 128],
                            in_=pu[:, 0:128]), "v")
                        c.inc(nc.vector.tensor_scalar(
                            out=P_sb[par][:, tt, 0:CAP], in0=jrow[:, 0:CAP],
                            scalar1=cumg[:, tt, e_:e_ + 1],
                            scalar2=selm32[:, tt, e_:e_ + 1],
                            op0=ALU.is_equal, op1=ALU.mult), "v")
                        for jt in range(JT):
                            c.wait(nc.vector, "v")
                            c.inc(nc.vector.tensor_scalar(
                                out=tmpP[:], in0=pu[:, 128:256],
                                scalar1=jcol[:, jt:jt + 1], scalar2=None,
                                op0=ALU.is_equal), "v")
                            c.wait(nc.vector, "v")
                            c.inc(nc.vector.tensor_mul(
                                PwT[par][:, jt, tt * 128:(tt + 1) * 128],
                                tmpP[:], wbcf[par][:, tt * 128:(tt + 1) * 128]), "v")
                        psu_tok[sl] = c.counts["v"]
                    tok_P[par] = c.counts["v"]
                    pso2_tok[0] = max(pso2_tok[0], c.counts["v"])

                def issue_w1(e_, hs):
                    bi = hs % 3
                    if r == 0:
                        c.wait(nc.gpsimd, "t", w1_buf[bi])
                        if w1_wb[bi]:
                            c.wait(nc.gpsimd, f"dwbW_{bi}", w1_wb[bi])
                        dsem = f"dw1_{bi}"
                        tok_d = c.inc(nc.gpsimd.dma_start(
                            out=w1sl[bi][:],
                            in_=w1_h[e_][:, hs * 512:(hs + 1) * 512]
                            .rearrange("(dc p) h -> p dc h", p=128)), dsem, 16)
                        c.wait(nc.sync, dsem, tok_d)
                        w1_wb[bi] = c.inc(nc.sync.dma_start(
                            out=w1b_h[e_][:, hs * 512:(hs + 1) * 512]
                            .rearrange("(dc p) h -> p dc h", p=128),
                            in_=w1sl[bi][:]), f"dwbW_{bi}", 16)
                    else:
                        c.wait(nc.sync, "t", w1_buf[bi])
                        dsem = f"dw1s_{bi}"
                        tok_d = c.inc(nc.sync.dma_start(
                            out=w1sl[bi][:],
                            in_=w1b_h[e_][:, hs * 512:(hs + 1) * 512]
                            .rearrange("(dc p) h -> p dc h", p=128)), dsem, 16)
                    return dsem, tok_d

                def issue_w2(e_, hs):
                    bi = hs % 3
                    if r == 0:
                        c.wait(nc.gpsimd, "t", w2_buf[bi])
                        if w2_wb[bi]:
                            c.wait(nc.gpsimd, f"dwb_{bi}", w2_wb[bi])
                        dsem = f"dw2_{bi}"
                        tok_d = c.inc(nc.gpsimd.dma_start(
                            out=w2sl[bi][:],
                            in_=w2_h[e_, hs * 512:(hs + 1) * 512, :]
                            .rearrange("(hcc p) d -> p hcc d", p=128)), dsem, 16)
                        c.wait(nc.scalar, dsem, tok_d)
                        w2_wb[bi] = c.inc(nc.scalar.dma_start(
                            out=w2b_h[e_, hs * 512:(hs + 1) * 512, :]
                            .rearrange("(hcc p) d -> p hcc d", p=128),
                            in_=w2sl[bi][:]), f"dwb_{bi}", 16)
                    else:
                        c.wait(nc.scalar, "t", w2_buf[bi])
                        dsem = f"dw2s_{bi}"
                        tok_d = c.inc(nc.scalar.dma_start(
                            out=w2sl[bi][:],
                            in_=w2b_h[e_, hs * 512:(hs + 1) * 512, :]
                            .rearrange("(hcc p) d -> p hcc d", p=128)), dsem, 16)
                    return dsem, tok_d

                build_P(0)
                for e in range(NE):
                    par = e % 2
                    c.wait(nc.tensor, "v", max(tok_P[par], pso2_tok[0]))
                    for dcc in range(DC):
                        for tch in range(QT):
                            ins = nc.tensor.matmul(
                                psO[2][:, 0:CAP],
                                x1_tm[:, tch, dcc * 128:(dcc + 1) * 128],
                                P_sb[par][:, tch, 0:CAP],
                                start=(tch == 0), stop=(tch == QT - 1))
                        tok_t = c.inc(ins, "t")
                        c.wait(nc.vector, "t", tok_t)
                        c.inc(nc.vector.tensor_copy(
                            out=x1_g[:, dcc, :], in_=psO[2][:, 0:CAP]), "v")
                        c.wait(nc.tensor, "v", c.counts["v"])
                    gath_tok[0] = c.counts["v"]
                    pso2_tok[0] = c.counts["v"]
                    if e + 1 < NE:
                        build_P(e + 1)
                    c.wait(nc.vector, "v")
                    for dcc in range(DC):
                        c.inc(nc.vector.scalar_tensor_tensor(
                            out=acc[:, dcc, :], in0=wbcf[par][:],
                            scalar=b2T[:, dcc, e:e + 1], in1=acc[:, dcc, :],
                            op0=ALU.mult, op1=ALU.add), "v")
                    w1_pend = {i: issue_w1(e, i) for i in range(3)}
                    w2_pend = {i: issue_w2(e, i) for i in range(3)}
                    gel_toks = [0] * HC
                    for hs in range(8):
                        bi = hs % 3
                        dsem, tok_d = w1_pend.pop(hs)
                        c.wait(nc.tensor, dsem, tok_d)
                        for hh in range(4):
                            hc = hs * 4 + hh
                            ph = psO[(hc % 4) // 2][:, 512 * (hc % 2):512 * (hc % 2) + CAP]
                            if hc >= 4:
                                c.wait(nc.tensor, "s", gel_toks[hc - 4])
                            for dcc in range(DC):
                                ins = nc.tensor.matmul(
                                    ph,
                                    w1sl[bi][:, dcc, hh * 128:(hh + 1) * 128],
                                    x1_g[:, dcc, :],
                                    start=(dcc == 0), stop=(dcc == DC - 1))
                            tok_t = c.inc(ins, "t")
                            if hh == 3:
                                w1_buf[bi] = tok_t
                            c.wait(nc.scalar, "t", tok_t)
                            gel_toks[hc] = c.inc(nc.scalar.activation(
                                out=h_sb[:, hc, :], in_=ph, func=AF.Gelu,
                                bias=b1T[:, hc, e:e + 1]), "s")
                        if hs + 3 < 8:
                            w1_pend[hs + 3] = issue_w1(e, hs + 3)
                    for hs in range(8):
                        bi = hs % 3
                        dsem, tok_d = w2_pend.pop(hs)
                        if hs == 0:
                            c.wait(nc.tensor, "s", gel_toks[HC - 1])
                            c.wait(nc.tensor, "v")
                        c.wait(nc.tensor, dsem, tok_d)
                        for hh in range(4):
                            hc = hs * 4 + hh
                            for jt in range(JT):
                                for dh2 in range(2):
                                    ins = nc.tensor.matmul(
                                        psO[jt][:, dh2 * 512:(dh2 + 1) * 512],
                                        h_sb[:, hc, jt * 128:(jt + 1) * 128],
                                        w2sl[bi][:, hh, dh2 * 512:(dh2 + 1) * 512],
                                        start=(hc == 0), stop=(hc == HC - 1))
                        tok_pe = c.inc(ins, "t")
                        w2_buf[bi] = tok_pe
                        if hs + 3 < 8:
                            w2_pend[hs + 3] = issue_w2(e, hs + 3)
                    c.wait(nc.vector, "t")
                    c.wait(nc.vector, "v")
                    for jt in range(JT):
                        c.inc(nc.vector.tensor_copy(
                            out=out_g[:, jt, :], in_=psO[jt][:, 0:DM]), "v")
                    def sc_slot(sc_, dcc_):
                        if sc_ % 2 == 0:
                            return psO[dcc_ // 4][
                                :, (dcc_ % 4) * SCB:(dcc_ % 4) * SCB + SCB]
                        if dcc_ < 4:
                            return psO[2][:, dcc_ * SCB:(dcc_ + 1) * SCB] \
                                if SCB * 4 <= 1024 else psO[2][:, (dcc_ % 4) * SCB:]
                        u = (psU, psU2)[(dcc_ - 4) // 2]
                        return u[:, ((dcc_ - 4) % 2) * SCB:((dcc_ - 4) % 2) * SCB + SCB]
                    c.wait(nc.tensor, "v")
                    sc_add_tok = [0] * NSC
                    for sc in range(NSC):
                        if sc >= 2:
                            c.wait(nc.tensor, "v", sc_add_tok[sc - 2])
                        for dcc in range(DC):
                            for jc in range(JT):
                                ins = nc.tensor.matmul(
                                    sc_slot(sc, dcc),
                                    out_g[:, jc, dcc * 128:(dcc + 1) * 128],
                                    PwT[par][:, jc, sc * SCB:sc * SCB + SCB],
                                    start=(jc == 0), stop=(jc == JT - 1))
                        tok_t = c.inc(ins, "t")
                        c.wait(nc.vector, "t", tok_t)
                        c.wait(nc.vector, "v")
                        for dcc in range(DC):
                            c.inc(nc.vector.tensor_add(
                                acc[:, dcc, sc * SCB:sc * SCB + SCB],
                                acc[:, dcc, sc * SCB:sc * SCB + SCB],
                                sc_slot(sc, dcc)), "v")
                        sc_add_tok[sc] = c.counts["v"]
                    for _i in range(4):
                        psu_tok[_i] = max(psu_tok[_i], c.counts["v"])
                    pso2_tok[0] = max(pso2_tok[0], c.counts["v"])
                    c.wait(nc.tensor, "v")

                for tt2 in range(Q // 128):
                    pb = psO[tt2 % 2]
                    c.wait(nc.tensor, "v")
                    for dcc in range(DC):
                        ins = nc.tensor.transpose(
                            pb[:, dcc * 128:(dcc + 1) * 128],
                            acc[:, dcc, tt2 * 128:(tt2 + 1) * 128], eye32[:])
                    tok_t = c.inc(ins, "t")
                    c.wait(nc.vector, "t", tok_t)
                    c.wait(nc.vector, "v")
                    c.inc(nc.vector.bn_stats(bst[:, 0:6], pb[:, 0:512]), "v")
                    c.inc(nc.vector.bn_stats(bst[:, 6:12], pb[:, 512:1024]), "v")
                    c.wait(nc.vector, "v")
                    c.inc(nc.vector.bn_aggr(mv[:], bst[:]), "v")
                    c.wait(nc.scalar, "v")
                    c.inc(nc.scalar.activation(
                        out=sm0[:, 6:7], in_=mv[:, 1:2], func=AF.Ln,
                        bias=eps128[:]), "s")
                    c.wait(nc.scalar, "s")
                    c.inc(nc.scalar.activation(
                        out=sm0[:, 7:8], in_=sm0[:, 6:7], func=AF.Exp,
                        scale=-0.5), "s")
                    yb = ytile[0]
                    c.wait(nc.vector, "s")
                    c.wait(nc.vector, "dy0")
                    c.inc(nc.vector.tensor_scalar(
                        out=yb[:], in0=pb[:, 0:DM], scalar1=mv[:, 0:1],
                        scalar2=sm0[:, 7:8], op0=ALU.subtract, op1=ALU.mult), "v")
                    tok_v = c.counts["v"]
                    c.wait(nc.tensor, "v")
                    c.wait(nc.sync, "v", tok_v)
                    t0 = q0 + tt2 * 128
                    c.inc(nc.sync.dma_start(
                        out=y_h[t0:t0 + 128, :], in_=yb[:]), "dy0", 16)

                if "t_w" in tap_h:
                    c.wait(nc.sync, "v")
                    c.inc(nc.sync.dma_start(
                        out=tap_h["t_w"][:, q0:q0 + Q], in_=wT8[:]), "dct", 16)
                c.join(ALL_ENG)

        c.wait(nc.sync, "dy0")
        c.wait(nc.sync, "dy1")
    return nc


def kernel(L0_attn_x, L0_attn_Wq, L0_attn_Wk, L0_attn_Wv, L0_attn_Wo,
           L0_ffn_w_router, L0_ffn_w1, L0_ffn_b1, L0_ffn_w2, L0_ffn_b2):
    B, S, D = L0_attn_x.shape
    nc = build_nc(S=S)
    common = {
        "wq": np.ascontiguousarray(L0_attn_Wq, dtype=np.float32),
        "wk": np.ascontiguousarray(L0_attn_Wk, dtype=np.float32),
        "wv": np.ascontiguousarray(L0_attn_Wv, dtype=np.float32),
        "wo": np.ascontiguousarray(L0_attn_Wo, dtype=np.float32),
        "wr": np.ascontiguousarray(L0_ffn_w_router, dtype=np.float32),
        "w1": np.ascontiguousarray(L0_ffn_w1, dtype=np.float32),
        "b1": np.ascontiguousarray(L0_ffn_b1, dtype=np.float32),
        "w2": np.ascontiguousarray(L0_ffn_w2, dtype=np.float32),
        "b2": np.ascontiguousarray(L0_ffn_b2, dtype=np.float32),
    }
    in_maps = []
    for b in range(B):
        m = dict(common)
        m["x"] = np.ascontiguousarray(L0_attn_x[b], dtype=np.float32)
        in_maps.append(m)
    res = run_bass_kernel_spmd(nc, in_maps, core_ids=list(range(B)))
    return np.stack([res.results[b]["y"] for b in range(B)], axis=0)


# version 22
# speedup vs baseline: 1.3411x; 1.0487x over previous
from contextlib import ExitStack

import numpy as np
import ml_dtypes

import concourse.bass as bass
import concourse.mybir as mybir
from concourse.bass_utils import run_bass_kernel_spmd

F32 = mybir.dt.float32
BF16 = mybir.dt.bfloat16
AF = mybir.ActivationFunctionType
ALU = mybir.AluOpType
AX = mybir.AxisListType

DM = 1024
DH = 4096
NE = 8
DC = DM // 128
HC = DH // 128
NEG = -1.0e30
EPS = 1e-5
SCL = 1.0 / 32.0


class Ctx:

    def __init__(self, nc):
        self.nc = nc
        self.sems = {}
        self.counts = {}
        for n in ("t", "v", "s"):
            self.new(n)

    def new(self, n):
        if n not in self.sems:
            self.sems[n] = self.nc.alloc_semaphore(n)
            self.counts[n] = 0

    def inc(self, ins, name, n=1):
        ins.then_inc(self.sems[name], n)
        self.counts[name] += n
        return self.counts[name]

    def wait(self, eng, name, val=None):
        v = self.counts[name] if val is None else val
        if v > 0:
            eng.wait_ge(self.sems[name], v)
        return v

    def join(self, engines):
        for eng in engines:
            for n in self.sems:
                self.wait(eng, n)


def build_nc(S=4096, n_rounds=4, taps=()):
    Q = S // n_rounds
    QT = Q // 128
    TB = min(512, Q)
    NTB = Q // TB
    KG = (Q + 511) // 512
    JT = max(1, -(-3 * Q // (8 * 128)))
    CAP = JT * 128
    SCB = min(256, Q)
    NSC = Q // SCB

    nc = bass.Bass("TRN2", target_bir_lowering=False)
    c = Ctx(nc)
    for _sn in ("dc", "dcg", "dct", "dwa", "dwq", "dwk", "dwv", "dx0", "dx1",
                "dx2", "dx3", "dx32", "dw1_0", "dw1_1", "dw1_2", "dw1s_0", "dw1s_1", "dw1s_2",
                "dwbW_0", "dwbW_1", "dwbW_2",
                "dw2_0", "dw2_1", "dw2_2", "dw2s_0", "dw2s_1", "dw2s_2",
                "dwb_0", "dwb_1", "dwb_2", "dy0", "dy1", "dkc", "dkw"):
        c.new(_sn)

    x_h = nc.dram_tensor("x", [S, DM], F32, kind="ExternalInput").ap()
    wq_h = nc.dram_tensor("wq", [DM, DM], F32, kind="ExternalInput").ap()
    wk_h = nc.dram_tensor("wk", [DM, DM], F32, kind="ExternalInput").ap()
    wv_h = nc.dram_tensor("wv", [DM, DM], F32, kind="ExternalInput").ap()
    wo_h = nc.dram_tensor("wo", [DM, DM], F32, kind="ExternalInput").ap()
    wr_h = nc.dram_tensor("wr", [DM, NE], F32, kind="ExternalInput").ap()
    w1_h = nc.dram_tensor("w1", [NE, DM, DH], F32, kind="ExternalInput").ap()
    b1_h = nc.dram_tensor("b1", [NE, DH], F32, kind="ExternalInput").ap()
    w2_h = nc.dram_tensor("w2", [NE, DH, DM], F32, kind="ExternalInput").ap()
    b2_h = nc.dram_tensor("b2", [NE, DM], F32, kind="ExternalInput").ap()
    y_h = nc.dram_tensor("y", [S, DM], F32, kind="ExternalOutput").ap()
    w1b_h = nc.dram_tensor("w1b", [NE, DM, DH], BF16, kind="Internal").ap()
    w2b_h = nc.dram_tensor("w2b", [NE, DH, DM], BF16, kind="Internal").ap()
    wvob_h = nc.dram_tensor("wvob", [DM, DM], BF16, kind="Internal").ap()
    kc_h = nc.dram_tensor("kcache", [n_rounds, 128, DC, Q], BF16, kind="Internal").ap()
    vc_h = nc.dram_tensor("vcache", [n_rounds, 128, QT, DM], BF16, kind="Internal").ap()
    tap_h = {}
    for tn, shp in (("t_r", [DM, S]), ("t_x1", [DM, S]), ("t_w", [NE, S]),
                    ("t_acc", [DM, S])):
        if tn in taps:
            tap_h[tn] = nc.dram_tensor(tn, shp, F32, kind="ExternalOutput").ap()

    eye16_c = nc.inline_tensor(np.eye(128, dtype=ml_dtypes.bfloat16), "eye16c")
    eye32_c = nc.inline_tensor(np.eye(128, dtype=np.float32), "eye32c")
    tri = np.where(np.arange(128)[None, :] <= np.arange(128)[:, None], 0.0, NEG)
    tri_c = nc.inline_tensor(tri.astype(np.float32), "tric")
    ones_col_c = nc.inline_tensor(np.ones((128, 1), dtype=ml_dtypes.bfloat16), "onescolc")
    ones_row_c = nc.inline_tensor(np.ones((1, 128), dtype=np.float32), "onesrowc")
    sel = np.zeros((NE, NE * 128), dtype=np.float32)
    for e in range(NE):
        sel[e, e * 128:(e + 1) * 128] = 1.0
    sel_c = nc.inline_tensor(sel, "selc")
    jrow_c = nc.inline_tensor(
        (np.arange(CAP, dtype=np.float32) + 1.0)[None, :].repeat(128, 0), "jrowc")
    jcol_c = nc.inline_tensor(
        (np.arange(JT * 128, dtype=np.float32) + 1.0).reshape(JT, 128).T.copy(),
        "jcolc")
    eps_c = nc.inline_tensor(np.full((1, 1), EPS, dtype=np.float32), "epsc")
    eps128_c = nc.inline_tensor(np.full((128, 1), EPS, dtype=np.float32), "eps128c")
    triu_c = nc.inline_tensor(
        np.triu(np.ones((128, 128))).astype(ml_dtypes.bfloat16), "triuc")

    ALL_ENG = (nc.tensor, nc.vector, nc.scalar, nc.gpsimd, nc.sync)

    with ExitStack() as P:
        acc = P.enter_context(nc.sbuf_tensor("acc", [128, DC, Q], F32))
        x1q = P.enter_context(nc.sbuf_tensor("x1q", [128, DC, Q], BF16))
        eye16 = P.enter_context(nc.sbuf_tensor("eye16", [128, 128], BF16))
        eye32 = P.enter_context(nc.sbuf_tensor("eye32", [128, 128], F32))
        trim = P.enter_context(nc.sbuf_tensor("trim", [128, 128], F32))
        ones_col = P.enter_context(nc.sbuf_tensor("ones_col", [128, 1], BF16))
        ones_row = P.enter_context(nc.sbuf_tensor("ones_row", [1, 128], F32))
        sel_sb = P.enter_context(nc.sbuf_tensor("sel_sb", [NE, NE * 128], F32))
        b1T = P.enter_context(nc.sbuf_tensor("b1T", [128, HC, NE], F32))
        b2T = P.enter_context(nc.sbuf_tensor("b2T", [128, DC, NE], F32))
        wr_sb = P.enter_context(nc.sbuf_tensor("wr_sb", [128, DC, NE], BF16))
        m_run = P.enter_context(nc.sbuf_tensor("m_run", [128, QT], F32))
        l_run = P.enter_context(nc.sbuf_tensor("l_run", [128, QT], F32))
        sm0 = P.enter_context(nc.sbuf_tensor("sm0", [128, 8], F32))
        sm1 = P.enter_context(nc.sbuf_tensor("sm1", [128, 8], F32))
        epsb = P.enter_context(nc.sbuf_tensor("epsb", [1, 1], F32))
        eps128 = P.enter_context(nc.sbuf_tensor("eps128", [128, 1], F32))
        bst = P.enter_context(nc.sbuf_tensor("bst", [128, 12], F32))
        mv = P.enter_context(nc.sbuf_tensor("mv", [128, 2], F32))
        triu_sb = P.enter_context(nc.sbuf_tensor("triu_sb", [128, 128], BF16))
        jrow = P.enter_context(nc.sbuf_tensor("jrow", [128, CAP], F32))
        jcol = P.enter_context(nc.sbuf_tensor("jcol", [128, JT], F32))
        x1_tm = P.enter_context(nc.sbuf_tensor("x1_tm", [128, QT, DM], BF16))

        with ExitStack() as PH:
            b1raw = PH.enter_context(nc.sbuf_tensor("b1raw", [NE, DH], F32))
            b2raw = PH.enter_context(nc.sbuf_tensor("b2raw", [NE, DM], F32))
            wv_tmp = PH.enter_context(nc.sbuf_tensor("wv_tmp", [128, DC, DM], BF16))
            wvT_tmp = PH.enter_context(nc.sbuf_tensor("wvT_tmp", [128, DC, DM], BF16))
            wvo_pre = PH.enter_context(nc.sbuf_tensor("wvo_pre", [128, DC, DM], BF16))
            psP = PH.enter_context(nc.psum_tensor("psP", [128, 512], F32))
            psP_T = PH.enter_context(nc.psum_tensor("psP_T", [128, 1024], BF16))

            for dst, src in ((eye16, eye16_c), (eye32, eye32_c), (trim, tri_c),
                             (ones_col, ones_col_c), (ones_row, ones_row_c),
                             (sel_sb, sel_c), (epsb, eps_c), (triu_sb, triu_c),
                             (jrow, jrow_c), (jcol, jcol_c), (eps128, eps128_c)):
                c.inc(nc.sync.dma_start(out=dst[:], in_=src.ap()), "dc", 16)
            c.inc(nc.sync.dma_start(out=b1raw[:], in_=b1_h), "dc", 16)
            c.inc(nc.sync.dma_start(out=b2raw[:], in_=b2_h), "dc", 16)
            c.inc(nc.gpsimd.dma_start(
                out=wr_sb[:], in_=wr_h.rearrange("(dc p) e -> p dc e", p=128)), "dwa", 16)
            c.wait(nc.tensor, "dc")
            for hc in range(HC):
                tok_t = c.inc(nc.tensor.transpose(
                    psP[:, 0:NE], b1raw[0:NE, hc * 128:(hc + 1) * 128],
                    eye32[0:NE, 0:NE]), "t")
                c.wait(nc.vector, "t", tok_t)
                c.inc(nc.vector.tensor_copy(out=b1T[:, hc, :], in_=psP[:, 0:NE]), "v")
                c.wait(nc.tensor, "v")
            for dc in range(DC):
                tok_t = c.inc(nc.tensor.transpose(
                    psP[:, 8:8 + NE], b2raw[0:NE, dc * 128:(dc + 1) * 128],
                    eye32[0:NE, 0:NE]), "t")
                c.wait(nc.vector, "t", tok_t)
                c.inc(nc.vector.tensor_copy(out=b2T[:, dc, :], in_=psP[:, 8:8 + NE]), "v")
                c.wait(nc.tensor, "v")

            tok_wv = c.inc(nc.gpsimd.dma_start(
                out=wv_tmp[:], in_=wv_h.rearrange("(dc p) n -> p dc n", p=128)), "dwa", 16)
            c.wait(nc.tensor, "dwa", tok_wv)
            for dc in range(DC):
                for dvc in range(DC):
                    c.inc(nc.tensor.transpose(
                        psP_T[:, dvc * 128:(dvc + 1) * 128],
                        wv_tmp[:, dc, dvc * 128:(dvc + 1) * 128], eye16[:]), "t")
                c.wait(nc.vector, "t")
                c.inc(nc.vector.tensor_copy(
                    out=wvT_tmp[:, dc, :], in_=psP_T[:, 0:DM]), "v")
                c.wait(nc.tensor, "v")
            c.wait(nc.gpsimd, "t")
            tok_wo = c.inc(nc.gpsimd.dma_start(
                out=wv_tmp[:], in_=wo_h.rearrange("(dc p) n -> p dc n", p=128)), "dwa", 16)
            c.wait(nc.tensor, "dwa", tok_wo)
            for dc in range(DC):
                for half in range(DM // 512):
                    for dvc in range(DC):
                        ins = nc.tensor.matmul(
                            psP[:, 0:512],
                            wvT_tmp[:, dc, dvc * 128:(dvc + 1) * 128],
                            wv_tmp[:, dvc, half * 512:half * 512 + 512],
                            start=(dvc == 0), stop=(dvc == DC - 1))
                    tok_t = c.inc(ins, "t")
                    c.wait(nc.vector, "t", tok_t)
                    c.inc(nc.vector.tensor_copy(
                        out=wvo_pre[:, dc, half * 512:half * 512 + 512],
                        in_=psP[:, 0:512]), "v")
                    c.wait(nc.tensor, "v")
            c.wait(nc.sync, "v")
            c.inc(nc.sync.dma_start(
                out=wvob_h.rearrange("(dc p) n -> p dc n", p=128),
                in_=wvo_pre[:]), "dc", 16)
            c.join(ALL_ENG)

        for r in range(n_rounds):
            q0 = r * Q

            with ExitStack() as PA:
                sb = lambda nm, shp, dt: PA.enter_context(
                    nc.sbuf_tensor(f"{nm}_r{r}", shp, dt))
                psb = lambda nm, shp, dt: PA.enter_context(
                    nc.psum_tensor(f"{nm}_r{r}", shp, dt))
                q_fm = sb("q_fm", [128, DC, Q], BF16)
                k_blk = sb("k_blk", [128, DC, Q], BF16)
                v_blk = sb("v_blk", [128, QT, DM], BF16)
                wq_sb = sb("wq_sb", [128, DC, DM], BF16)
                wk_sb = sb("wk_sb", [128, DC, DM], BF16)
                wvo_sb = sb("wvo_sb", [128, DC, DM], BF16)
                xt16 = [sb(f"xt16_{i}", [128, 2, DM], BF16) for i in range(2)]
                xf16 = sb("xf16", [128, DC, 512], BF16)
                xt32 = sb("xt32", [128, DM], F32)
                rtm = sb("rtm", [128, DM], F32)
                rowst = sb("rowst", [1, 128], F32)
                s_row = sb("s_row", [128, Q], BF16)
                p16 = sb("p16", [128, Q], BF16)
                attnT = sb("attnT", [128, QT, 128], BF16)

                psA_s = psb("psA_s", [128, 512], F32)
                psA_s2 = psb("psA_s2", [128, 512], F32)
                psA_pv = [psb(f"psA_pv{i}", [128, 1024], F32) for i in range(2)]
                psA_T = psb("psA_T", [128, 1024], BF16)
                psA_m = psb("psA_m", [128, 512], F32)

                def load_w_fm(dst, src, sem):
                    return c.inc(nc.gpsimd.dma_start(
                        out=dst[:], in_=src.rearrange("(dc p) n -> p dc n", p=128)),
                        sem, 16)

                tok_wq = load_w_fm(wq_sb, wq_h, "dwq")
                tok_wk = load_w_fm(wk_sb, wk_h, "dwk")
                tok_wv = c.inc(nc.sync.dma_start(
                    out=wvo_sb[:],
                    in_=wvob_h.rearrange("(dc p) n -> p dc n", p=128)), "dwv", 16)

                c.wait(nc.tensor, "dwq", tok_wq)
                xb_tok = [0, 0]
                sbank_tok = [0, 0]

                def load_x_tiles(base, gt_, use):
                    np_ = (gt_ + 1) // 2
                    toks = []
                    for p_ in range(np_):
                        nrow = min(256, gt_ * 128 - p_ * 256)
                        bi = p_ % 2
                        c.wait(nc.gpsimd, "t", xb_tok[bi])
                        toks.append((bi, c.inc(nc.gpsimd.dma_start(
                            out=xt16[bi][:, 0:nrow // 128, :],
                            in_=x_h[base + p_ * 256:base + p_ * 256 + nrow, :]
                            .rearrange("(b p) d -> p b d", p=128)),
                            f"dx{bi}", 16), nrow // 128, p_))
                    for bi, tok_d, nt, p_ in toks:
                        c.wait(nc.tensor, f"dx{bi}", tok_d)
                        for j_ in range(nt):
                            use(p_ * 2 + j_, xt16[bi][:, j_, :])
                        xb_tok[bi] = c.counts["t"]

                for g in range(KG):
                    gn = min(512, Q - g * 512)
                    gt = gn // 128

                    def use_pre(i, src_):
                        for dcc in range(DC):
                            c.inc(nc.tensor.transpose(
                                psA_T[:, dcc * 128:(dcc + 1) * 128],
                                src_[:, dcc * 128:(dcc + 1) * 128], eye16[:]), "t")
                        c.wait(nc.vector, "t")
                        c.inc(nc.vector.tensor_copy(
                            out=xf16[:, :, i * 128:(i + 1) * 128],
                            in_=psA_T[:, 0:DM].rearrange("p (a b) -> p a b", b=128)), "v")
                        c.wait(nc.tensor, "v")

                    load_x_tiles(q0 + g * 512, gt, use_pre)
                    for kc in range(DC):
                        for dcc in range(DC):
                            ins = nc.tensor.matmul(
                                psA_s[:, 0:gn], wq_sb[:, dcc, kc * 128:(kc + 1) * 128],
                                xf16[:, dcc, 0:gn],
                                start=(dcc == 0), stop=(dcc == DC - 1))
                        tok_t = c.inc(ins, "t")
                        c.wait(nc.vector, "t", tok_t)
                        c.inc(nc.vector.tensor_copy(
                            out=q_fm[:, kc, g * 512:g * 512 + gn], in_=psA_s[:, 0:gn]), "v")
                        c.wait(nc.tensor, "v")

                for kb in range(r + 1):
                    kb0 = kb * Q
                    if kb < r:
                        c.wait(nc.sync, "t")
                        tok_kc = c.inc(nc.sync.dma_start(
                            out=k_blk[:], in_=kc_h[kb]), "dkc", 16)
                        tok_vc = c.inc(nc.sync.dma_start(
                            out=v_blk[:], in_=vc_h[kb]), "dkc", 16)
                        c.wait(nc.tensor, "dkc", tok_vc)
                    if kb == r:
                        c.wait(nc.tensor, "dwk", tok_wk)
                        for g in range(KG):
                            gn = min(512, Q - g * 512)
                            gt = gn // 128
                            def use_kv(i, src, g=g):
                                for dcc in range(DC):
                                    c.inc(nc.tensor.transpose(
                                        psA_T[:, dcc * 128:(dcc + 1) * 128],
                                        src[:, dcc * 128:(dcc + 1) * 128], eye16[:]), "t")
                                c.wait(nc.vector, "t")
                                c.inc(nc.vector.tensor_copy(
                                    out=xf16[:, :, i * 128:(i + 1) * 128],
                                    in_=psA_T[:, 0:DM].rearrange("p (a b) -> p a b", b=128)), "v")
                                c.wait(nc.tensor, "v")
                                c.wait(nc.tensor, "dwv", tok_wv)
                                for half in range(DM // 512):
                                    for dcc in range(DC):
                                        ins = nc.tensor.matmul(
                                            psA_pv[0][:, half * 512:half * 512 + 512],
                                            xf16[:, dcc, i * 128:(i + 1) * 128],
                                            wvo_sb[:, dcc, half * 512:half * 512 + 512],
                                            start=(dcc == 0), stop=(dcc == DC - 1))
                                tok_t = c.inc(ins, "t")
                                c.wait(nc.vector, "t", tok_t)
                                c.inc(nc.vector.tensor_copy(
                                    out=v_blk[:, g * 4 + i, :], in_=psA_pv[0][:, 0:DM]), "v")
                                c.wait(nc.tensor, "v")

                            load_x_tiles(kb0 + g * 512, gt, use_kv)
                            for kc in range(DC):
                                for dcc in range(DC):
                                    ins = nc.tensor.matmul(
                                        psA_s[:, 0:gn],
                                        wk_sb[:, dcc, kc * 128:(kc + 1) * 128],
                                        xf16[:, dcc, 0:gn],
                                        start=(dcc == 0), stop=(dcc == DC - 1))
                                tok_t = c.inc(ins, "t")
                                c.wait(nc.vector, "t", tok_t)
                                c.inc(nc.vector.tensor_copy(
                                    out=k_blk[:, kc, g * 512:g * 512 + gn],
                                    in_=psA_s[:, 0:gn]), "v")
                                c.wait(nc.tensor, "v")

                    if kb == r and r + 1 < n_rounds:
                        c.wait(nc.sync, "v")
                        c.inc(nc.sync.dma_start(out=kc_h[r], in_=k_blk[:]), "dkw", 16)
                        c.inc(nc.sync.dma_start(out=vc_h[r], in_=v_blk[:]), "dkw", 16)
                    diag = (kb == r)
                    for qt in range(QT):
                        tk = (qt + 1) * 128 if diag else Q
                        ng = (tk + 511) // 512
                        for g in range(ng):
                            n0 = g * 512
                            nn = min(512, tk - n0)
                            ps = psA_s if g % 2 == 0 else psA_s2
                            c.wait(nc.tensor, "v", sbank_tok[g % 2])
                            for kc in range(DC):
                                ins = nc.tensor.matmul(
                                    ps[:, 0:nn], q_fm[:, kc, qt * 128:(qt + 1) * 128],
                                    k_blk[:, kc, n0:n0 + nn],
                                    start=(kc == 0), stop=(kc == DC - 1))
                            tok_t = c.inc(ins, "t")
                            c.wait(nc.vector, "t", tok_t)
                            nmask = 128 if (diag and g == ng - 1) else 0
                            if nn - nmask > 0:
                                c.inc(nc.vector.tensor_copy(
                                    out=s_row[:, n0:n0 + nn - nmask],
                                    in_=ps[:, 0:nn - nmask]), "v")
                            if nmask:
                                c.inc(nc.vector.tensor_add(
                                    s_row[:, n0 + nn - 128:n0 + nn],
                                    ps[:, nn - 128:nn], trim[:]), "v")
                            sbank_tok[g % 2] = c.counts["v"]
                        c.wait(nc.vector, "v")
                        c.inc(nc.vector.reduce_max(
                            sm0[:, 0:1], s_row[:, 0:tk], axis=AX.X), "v")
                        mcur = m_run[:, qt:qt + 1]
                        c.wait(nc.vector, "v")
                        if kb == 0:
                            c.inc(nc.vector.tensor_copy(out=mcur, in_=sm0[:, 0:1]), "v")
                        else:
                            c.inc(nc.vector.tensor_tensor(
                                sm1[:, 0:1], mcur, sm0[:, 0:1], op=ALU.max), "v")
                            c.wait(nc.vector, "v")
                            c.inc(nc.vector.tensor_sub(
                                sm0[:, 1:2], mcur, sm1[:, 0:1]), "v")
                            c.wait(nc.vector, "v")
                            c.inc(nc.vector.tensor_copy(out=mcur, in_=sm1[:, 0:1]), "v")
                            c.wait(nc.scalar, "v")
                            c.inc(nc.scalar.activation(
                                out=sm0[:, 2:3], in_=sm0[:, 1:2], func=AF.Exp,
                                scale=SCL), "s")
                        c.wait(nc.vector, "v")
                        c.inc(nc.vector.tensor_scalar_mul(sm0[:, 3:4], mcur, -SCL), "v")
                        c.wait(nc.scalar, "v")
                        c.inc(nc.scalar.activation(
                            out=p16[:, 0:tk], in_=s_row[:, 0:tk], func=AF.Exp,
                            bias=sm0[:, 3:4], scale=SCL, accum_out=sm0[:, 4:5]), "s")
                        c.wait(nc.vector, "s")
                        if kb == 0:
                            c.inc(nc.vector.tensor_copy(
                                out=l_run[:, qt:qt + 1], in_=sm0[:, 4:5]), "v")
                        else:
                            c.inc(nc.vector.scalar_tensor_tensor(
                                out=l_run[:, qt:qt + 1], in0=l_run[:, qt:qt + 1],
                                scalar=sm0[:, 2:3], in1=sm0[:, 4:5],
                                op0=ALU.mult, op1=ALU.add), "v")
                        ntk = tk // 128
                        c.wait(nc.tensor, "s")
                        c.wait(nc.tensor, "v")
                        for tkc in range(ntk):
                            c.inc(nc.tensor.transpose(
                                psA_T[:, tkc * 128:(tkc + 1) * 128],
                                p16[:, tkc * 128:(tkc + 1) * 128], eye16[:]), "t")
                        c.wait(nc.vector, "t")
                        c.inc(nc.vector.tensor_copy(
                            out=attnT[:, 0:ntk, :],
                            in_=psA_T[:, 0:ntk * 128].rearrange(
                                "p (a b) -> p a b", b=128)), "v")
                        pvb = psA_pv[qt % 2]
                        c.wait(nc.tensor, "v")
                        for dvc in range(DC):
                            for tkc in range(ntk):
                                ins = nc.tensor.matmul(
                                    pvb[:, dvc * 128:(dvc + 1) * 128],
                                    v_blk[:, tkc, dvc * 128:(dvc + 1) * 128],
                                    attnT[:, tkc, :],
                                    start=(tkc == 0), stop=(tkc == ntk - 1))
                        tok_pv = c.inc(ins, "t")
                        accs = acc[:, :, qt * 128:(qt + 1) * 128]
                        c.wait(nc.vector, "t", tok_pv)
                        if kb == 0:
                            c.inc(nc.vector.tensor_copy(
                                out=accs,
                                in_=pvb[:, 0:DM].rearrange("p (a b) -> p a b", b=128)), "v")
                        else:
                            c.wait(nc.tensor, "s")
                            tok_t = c.inc(nc.tensor.transpose(
                                psA_m[0:1, 0:128], sm0[:, 2:3], eye32[:]), "t")
                            c.wait(nc.vector, "t", tok_t)
                            c.inc(nc.vector.tensor_copy(
                                out=rowst[0:1, 0:128], in_=psA_m[0:1, 0:128]), "v")
                            c.wait(nc.tensor, "v")
                            tok_t = c.inc(nc.tensor.matmul(
                                psA_m[:, 128:256], ones_row[:], rowst[0:1, 0:128],
                                start=True, stop=True), "t")
                            c.wait(nc.vector, "t", tok_t)
                            for dcc in range(DC):
                                c.inc(nc.vector.tensor_mul(
                                    accs[:, dcc, :], accs[:, dcc, :],
                                    psA_m[:, 128:256]), "v")
                            c.wait(nc.vector, "v")
                            c.inc(nc.vector.tensor_add(
                                accs, accs,
                                pvb[:, 0:DM].rearrange("p (a b) -> p a b", b=128)), "v")
                        c.wait(nc.tensor, "v")

                x1tm_tok = [0] * QT
                for qt in range(QT):
                    t0 = q0 + qt * 128
                    c.wait(nc.sync, "v")
                    tok_d = c.inc(nc.sync.dma_start(
                        out=xt32[:], in_=x_h[t0:t0 + 128, :]), "dx32", 16)
                    c.wait(nc.vector, "v")
                    c.inc(nc.vector.reciprocal(sm0[:, 5:6], l_run[:, qt:qt + 1]), "v")
                    pp = psA_pv[qt % 2]
                    c.wait(nc.tensor, "v")
                    for dcc in range(DC):
                        ins = nc.tensor.transpose(
                            pp[:, dcc * 128:(dcc + 1) * 128],
                            acc[:, dcc, qt * 128:(qt + 1) * 128], eye32[:])
                    tok_t = c.inc(ins, "t")
                    c.wait(nc.vector, "t", tok_t)
                    c.wait(nc.vector, "dx32", tok_d)
                    c.wait(nc.vector, "v")
                    c.inc(nc.vector.scalar_tensor_tensor(
                        out=rtm[:], in0=pp[:, 0:DM], scalar=sm0[:, 5:6],
                        in1=xt32[:], op0=ALU.mult, op1=ALU.add), "v")
                    c.wait(nc.vector, "v")
                    c.inc(nc.vector.bn_stats(bst[:, 0:6], rtm[:, 0:512]), "v")
                    c.inc(nc.vector.bn_stats(bst[:, 6:12], rtm[:, 512:1024]), "v")
                    c.wait(nc.vector, "v")
                    c.inc(nc.vector.bn_aggr(mv[:], bst[:]), "v")
                    c.wait(nc.scalar, "v")
                    c.inc(nc.scalar.activation(
                        out=sm0[:, 6:7], in_=mv[:, 1:2], func=AF.Ln,
                        bias=eps128[:]), "s")
                    c.wait(nc.scalar, "s")
                    c.inc(nc.scalar.activation(
                        out=sm0[:, 7:8], in_=sm0[:, 6:7], func=AF.Exp,
                        scale=-0.5), "s")
                    c.wait(nc.vector, "s")
                    x1tm_tok[qt] = c.inc(nc.vector.tensor_scalar(
                        out=x1_tm[:, qt, :], in0=rtm[:], scalar1=mv[:, 0:1],
                        scalar2=sm0[:, 7:8], op0=ALU.subtract, op1=ALU.mult), "v")
                trev_tok = 0
                for qt in range(QT):
                    c.wait(nc.tensor, "v", max(x1tm_tok[qt], trev_tok))
                    for dcc in range(DC):
                        ins = nc.tensor.transpose(
                            psA_T[:, dcc * 128:(dcc + 1) * 128],
                            x1_tm[:, qt, dcc * 128:(dcc + 1) * 128], eye16[:])
                    tok_t = c.inc(ins, "t")
                    c.wait(nc.vector, "t", tok_t)
                    trev_tok = c.inc(nc.vector.tensor_copy(
                        out=x1q[:, :, qt * 128:(qt + 1) * 128],
                        in_=psA_T[:, 0:DM].rearrange("p (a b) -> p a b", b=128)), "v")
                c.wait(nc.tensor, "v")

                if "t_x1" in tap_h:
                    c.wait(nc.gpsimd, "v")
                    c.inc(nc.gpsimd.dma_start(
                        out=tap_h["t_x1"][:, q0:q0 + Q].rearrange(
                            "(dc p) t -> p dc t", p=128),
                        in_=x1q[:]), "dcg", 16)
                if "t_acc" in tap_h:
                    c.wait(nc.gpsimd, "v")
                    c.inc(nc.gpsimd.dma_start(
                        out=tap_h["t_acc"][:, q0:q0 + Q].rearrange(
                            "(dc p) t -> p dc t", p=128),
                        in_=acc[:]), "dcg", 16)
                c.join(ALL_ENG)

            with ExitStack() as PF:
                sb = lambda nm, shp, dt: PF.enter_context(
                    nc.sbuf_tensor(f"{nm}_r{r}", shp, dt))
                psb = lambda nm, shp, dt: PF.enter_context(
                    nc.psum_tensor(f"{nm}_r{r}", shp, dt))
                w1sl = [sb(f"w1sl{i}", [128, DC, 512], BF16) for i in range(3)]
                w2sl = [sb(f"w2sl{i}", [128, 4, DM], BF16) for i in range(3)]
                h_sb = sb("h_sb", [128, HC, CAP], BF16)
                x1_g = sb("x1_g", [128, DC, CAP], BF16)
                out_g = sb("out_g", [128, JT, DM], BF16)
                P_sb = [sb(f"P_sb{i}", [128, QT, CAP], BF16) for i in range(2)]
                PwT = [sb(f"PwT{i}", [128, JT, Q], BF16) for i in range(2)]
                selm = sb("selm", [128, QT, NE], BF16)
                selm32 = sb("selm32", [128, QT, NE], F32)
                cumg = sb("cumg", [128, QT, NE], F32)
                offr = sb("offr", [1, QT, NE], F32)
                totr = sb("totr", [1, QT * NE], F32)
                cumgT = sb("cumgT", [NE, Q], F32)
                wbcf = [sb(f"wbcf{i}", [128, Q], BF16) for i in range(2)]
                tmpP = sb("tmpP", [128, 512], F32)
                ytile = [sb("yt0", [128, DM], F32)]
                wT8 = sb("wT8", [NE, Q], F32)
                wT8b = sb("wT8b", [NE, Q], BF16)
                prb = sb("prb", [128, NE], F32)

                psO = [psb(f"psO{i}", [128, 1024], F32) for i in range(3)]
                psU = psb("psU", [128, 512], F32)
                psU2 = psb("psU2", [128, 512], F32)

                for tt in range(QT):
                    c.wait(nc.tensor, "v")
                    for dcc in range(DC):
                        ins = nc.tensor.matmul(
                            psU[:, 0:NE], x1q[:, dcc, tt * 128:(tt + 1) * 128],
                            wr_sb[:, dcc, :], start=(dcc == 0), stop=(dcc == DC - 1))
                    tok_t = c.inc(ins, "t")
                    c.wait(nc.vector, "t", tok_t)
                    c.inc(nc.vector.reduce_max(
                        sm0[:, 0:1], psU[:, 0:NE], axis=AX.X), "v")
                    c.wait(nc.vector, "v")
                    c.inc(nc.vector.tensor_scalar_mul(sm0[:, 1:2], sm0[:, 0:1], -1.0), "v")
                    c.wait(nc.scalar, "v")
                    c.inc(nc.scalar.activation(
                        out=prb[:], in_=psU[:, 0:NE], func=AF.Exp,
                        bias=sm0[:, 1:2], accum_out=sm0[:, 2:3]), "s")
                    c.wait(nc.vector, "s")
                    c.inc(nc.vector.reciprocal(sm0[:, 3:4], sm0[:, 2:3]), "v")
                    c.wait(nc.vector, "v")
                    c.inc(nc.vector.tensor_scalar_mul(prb[:], prb[:], sm0[:, 3:4]), "v")
                    c.wait(nc.vector, "v")
                    c.inc(nc.vector.reduce_max(sm0[:, 4:5], prb[:], axis=AX.X), "v")
                    c.wait(nc.vector, "v")
                    c.inc(nc.vector.tensor_scalar(
                        out=sm1[:, 0:NE], in0=prb[:], scalar1=sm0[:, 4:5],
                        scalar2=None, op0=ALU.is_ge), "v")
                    c.wait(nc.vector, "v")
                    c.inc(nc.vector.scalar_tensor_tensor(
                        out=sm1[:, 0:NE], in0=sm1[:, 0:NE], scalar=NEG,
                        in1=prb[:], op0=ALU.mult, op1=ALU.add), "v")
                    c.wait(nc.vector, "v")
                    c.inc(nc.vector.reduce_max(sm0[:, 5:6], sm1[:, 0:NE], axis=AX.X), "v")
                    c.wait(nc.vector, "v")
                    c.inc(nc.vector.tensor_scalar(
                        out=sm1[:, 0:NE], in0=prb[:], scalar1=sm0[:, 5:6],
                        scalar2=None, op0=ALU.is_ge), "v")
                    c.wait(nc.vector, "v")
                    c.inc(nc.vector.tensor_copy(out=selm[:, tt, :], in_=sm1[:, 0:NE]), "v")
                    c.inc(nc.vector.tensor_copy(out=selm32[:, tt, :], in_=sm1[:, 0:NE]), "v")
                    c.wait(nc.vector, "v")
                    c.inc(nc.vector.tensor_mul(prb[:], prb[:], sm1[:, 0:NE]), "v")
                    c.wait(nc.tensor, "s")
                    c.wait(nc.tensor, "v")
                    tok_t = c.inc(nc.tensor.transpose(
                        psU[0:NE, 128:256], prb[:], eye32[:]), "t")
                    c.wait(nc.vector, "t", tok_t)
                    c.inc(nc.vector.tensor_copy(
                        out=wT8[:, tt * 128:(tt + 1) * 128], in_=psU[0:NE, 128:256]), "v")
                    c.wait(nc.tensor, "v")
                c.wait(nc.vector, "v")
                c.inc(nc.vector.tensor_copy(out=wT8b[:], in_=wT8[:]), "v")

                c.wait(nc.tensor, "v")
                for tt in range(QT):
                    ins = nc.tensor.matmul(
                        psU[0:1, 256:256 + NE].rearrange("a b -> a b") if False
                        else psU[0:1, 256 + tt * NE:256 + (tt + 1) * NE],
                        ones_col[:], selm[:, tt, :], start=True, stop=True)
                tok_t = c.inc(ins, "t")
                c.wait(nc.vector, "t", tok_t)
                c.inc(nc.vector.tensor_copy(
                    out=totr[0:1, :], in_=psU[0:1, 256:256 + QT * NE]), "v")
                c.wait(nc.vector, "v")
                c.inc(nc.vector.memset(offr[0:1, 0, :], 0.0), "v")
                for tt in range(1, QT):
                    c.wait(nc.vector, "v")
                    c.inc(nc.vector.tensor_add(
                        offr[0:1, tt, :], offr[0:1, tt - 1, :],
                        totr[0:1, (tt - 1) * NE:tt * NE]), "v")
                c.wait(nc.tensor, "v")
                for tt in range(QT):
                    ins = nc.tensor.matmul(
                        psU[:, 0:NE], triu_sb[:], selm[:, tt, :],
                        start=True, stop=True)
                    tok_t = c.inc(ins, "t")
                    ins = nc.tensor.matmul(
                        psU[:, NE:2 * NE], ones_row[:], offr[0:1, tt, :],
                        start=True, stop=True)
                    tok_t2 = c.inc(ins, "t")
                    c.wait(nc.vector, "t", tok_t2)
                    c.inc(nc.vector.tensor_copy(
                        out=cumg[:, tt, :], in_=psU[:, 0:NE]), "v")
                    c.wait(nc.vector, "v")
                    c.inc(nc.vector.tensor_add(
                        cumg[:, tt, :], cumg[:, tt, :], psU[:, NE:2 * NE]), "v")
                    c.wait(nc.tensor, "v")

                c.wait(nc.tensor, "v")
                for tt in range(QT):
                    tok_t = c.inc(nc.tensor.transpose(
                        psU[0:NE, 0:128], cumg[:, tt, :], eye32[:]), "t")
                    c.wait(nc.vector, "t", tok_t)
                    c.inc(nc.vector.tensor_copy(
                        out=cumgT[:, tt * 128:(tt + 1) * 128], in_=psU[0:NE, 0:128]), "v")
                    c.wait(nc.tensor, "v")

                c.wait(nc.vector, "v")
                for dcc in range(DC):
                    c.inc(nc.vector.tensor_copy(
                        out=acc[:, dcc, :], in_=x1q[:, dcc, :]), "v")

                w1_buf = [0, 0, 0]
                w1_wb = [0, 0, 0]
                w2_buf = [0, 0, 0]
                w2_wb = [0, 0, 0]
                gath_tok = [0]
                pso2_tok = [0]
                b2r_tok = [0]
                tok_P = [0, 0]
                psu_tok = [0, 0]

                def build_P(e_):
                    par = e_ % 2
                    nh = (Q + 511) // 512
                    for hf in range(nh):
                        h0 = hf * 512
                        hn = min(512, Q - h0)
                        if hf % 2 == 0:
                            pw, pc = psU[:, 0:hn], psU2[:, 0:hn]
                            sl = 0
                        else:
                            pw, pc = psO[2][:, 0:hn], psO[2][:, 512:512 + hn]
                            sl = 1
                        c.wait(nc.tensor, "v", max(psu_tok[sl], gath_tok[0]))
                        c.inc(nc.tensor.matmul(
                            pw, sel_sb[:, e_ * 128:(e_ + 1) * 128],
                            wT8[:, h0:h0 + hn], start=True, stop=True), "t")
                        tok_t = c.inc(nc.tensor.matmul(
                            pc, sel_sb[:, e_ * 128:(e_ + 1) * 128],
                            cumgT[:, h0:h0 + hn], start=True, stop=True), "t")
                        c.wait(nc.vector, "t", tok_t)
                        c.wait(nc.vector, "v")
                        c.inc(nc.vector.tensor_copy(
                            out=wbcf[par][:, h0:h0 + hn], in_=pw), "v")
                        for tt in range(h0 // 128, (h0 + hn) // 128):
                            c.inc(nc.vector.tensor_scalar(
                                out=P_sb[par][:, tt, 0:CAP], in0=jrow[:, 0:CAP],
                                scalar1=cumg[:, tt, e_:e_ + 1],
                                scalar2=selm32[:, tt, e_:e_ + 1],
                                op0=ALU.is_equal, op1=ALU.mult), "v")
                        for jt in range(JT):
                            c.wait(nc.vector, "v")
                            c.inc(nc.vector.tensor_scalar(
                                out=tmpP[:, 0:hn], in0=pc,
                                scalar1=jcol[:, jt:jt + 1], scalar2=None,
                                op0=ALU.is_equal), "v")
                            c.wait(nc.vector, "v")
                            c.inc(nc.vector.tensor_mul(
                                PwT[par][:, jt, h0:h0 + hn],
                                tmpP[:, 0:hn], wbcf[par][:, h0:h0 + hn]), "v")
                        psu_tok[sl] = c.counts["v"]
                    tok_P[par] = c.counts["v"]
                    pso2_tok[0] = max(pso2_tok[0], c.counts["v"])

                def issue_w1(e_, hs):
                    bi = hs % 3
                    if r == 0:
                        c.wait(nc.gpsimd, "t", w1_buf[bi])
                        if w1_wb[bi]:
                            c.wait(nc.gpsimd, f"dwbW_{bi}", w1_wb[bi])
                        dsem = f"dw1_{bi}"
                        tok_d = c.inc(nc.gpsimd.dma_start(
                            out=w1sl[bi][:],
                            in_=w1_h[e_][:, hs * 512:(hs + 1) * 512]
                            .rearrange("(dc p) h -> p dc h", p=128)), dsem, 16)
                        c.wait(nc.sync, dsem, tok_d)
                        w1_wb[bi] = c.inc(nc.sync.dma_start(
                            out=w1b_h[e_][:, hs * 512:(hs + 1) * 512]
                            .rearrange("(dc p) h -> p dc h", p=128),
                            in_=w1sl[bi][:]), f"dwbW_{bi}", 16)
                    else:
                        c.wait(nc.sync, "t", w1_buf[bi])
                        dsem = f"dw1s_{bi}"
                        tok_d = c.inc(nc.sync.dma_start(
                            out=w1sl[bi][:],
                            in_=w1b_h[e_][:, hs * 512:(hs + 1) * 512]
                            .rearrange("(dc p) h -> p dc h", p=128)), dsem, 16)
                    return dsem, tok_d

                def issue_w2(e_, hs):
                    bi = hs % 3
                    if r == 0:
                        c.wait(nc.gpsimd, "t", w2_buf[bi])
                        if w2_wb[bi]:
                            c.wait(nc.gpsimd, f"dwb_{bi}", w2_wb[bi])
                        dsem = f"dw2_{bi}"
                        tok_d = c.inc(nc.gpsimd.dma_start(
                            out=w2sl[bi][:],
                            in_=w2_h[e_, hs * 512:(hs + 1) * 512, :]
                            .rearrange("(hcc p) d -> p hcc d", p=128)), dsem, 16)
                        c.wait(nc.scalar, dsem, tok_d)
                        w2_wb[bi] = c.inc(nc.scalar.dma_start(
                            out=w2b_h[e_, hs * 512:(hs + 1) * 512, :]
                            .rearrange("(hcc p) d -> p hcc d", p=128),
                            in_=w2sl[bi][:]), f"dwb_{bi}", 16)
                    else:
                        c.wait(nc.scalar, "t", w2_buf[bi])
                        dsem = f"dw2s_{bi}"
                        tok_d = c.inc(nc.scalar.dma_start(
                            out=w2sl[bi][:],
                            in_=w2b_h[e_, hs * 512:(hs + 1) * 512, :]
                            .rearrange("(hcc p) d -> p hcc d", p=128)), dsem, 16)
                    return dsem, tok_d

                build_P(0)
                for e in range(NE):
                    par = e % 2
                    c.wait(nc.tensor, "v", max(tok_P[par], pso2_tok[0]))
                    for dcc in range(DC):
                        for tch in range(QT):
                            ins = nc.tensor.matmul(
                                psO[2][:, 0:CAP],
                                x1_tm[:, tch, dcc * 128:(dcc + 1) * 128],
                                P_sb[par][:, tch, 0:CAP],
                                start=(tch == 0), stop=(tch == QT - 1))
                        tok_t = c.inc(ins, "t")
                        c.wait(nc.vector, "t", tok_t)
                        c.inc(nc.vector.tensor_copy(
                            out=x1_g[:, dcc, :], in_=psO[2][:, 0:CAP]), "v")
                        c.wait(nc.tensor, "v", c.counts["v"])
                    gath_tok[0] = c.counts["v"]
                    pso2_tok[0] = c.counts["v"]
                    if e + 1 < NE:
                        build_P(e + 1)
                    c.wait(nc.vector, "v")
                    for dcc in range(DC):
                        c.inc(nc.vector.scalar_tensor_tensor(
                            out=acc[:, dcc, :], in0=wbcf[par][:],
                            scalar=b2T[:, dcc, e:e + 1], in1=acc[:, dcc, :],
                            op0=ALU.mult, op1=ALU.add), "v")
                    w1_pend = {i: issue_w1(e, i) for i in range(3)}
                    w2_pend = {i: issue_w2(e, i) for i in range(3)}
                    gel_toks = [0] * HC
                    for hs in range(8):
                        bi = hs % 3
                        dsem, tok_d = w1_pend.pop(hs)
                        c.wait(nc.tensor, dsem, tok_d)
                        for hh in range(4):
                            hc = hs * 4 + hh
                            ph = psO[(hc % 4) // 2][:, 512 * (hc % 2):512 * (hc % 2) + CAP]
                            if hc >= 4:
                                c.wait(nc.tensor, "s", gel_toks[hc - 4])
                            for dcc in range(DC):
                                ins = nc.tensor.matmul(
                                    ph,
                                    w1sl[bi][:, dcc, hh * 128:(hh + 1) * 128],
                                    x1_g[:, dcc, :],
                                    start=(dcc == 0), stop=(dcc == DC - 1))
                            tok_t = c.inc(ins, "t")
                            if hh == 3:
                                w1_buf[bi] = tok_t
                            c.wait(nc.scalar, "t", tok_t)
                            gel_toks[hc] = c.inc(nc.scalar.activation(
                                out=h_sb[:, hc, :], in_=ph, func=AF.Gelu,
                                bias=b1T[:, hc, e:e + 1]), "s")
                        if hs + 3 < 8:
                            w1_pend[hs + 3] = issue_w1(e, hs + 3)
                    for hs in range(8):
                        bi = hs % 3
                        dsem, tok_d = w2_pend.pop(hs)
                        if hs == 0:
                            c.wait(nc.tensor, "s", gel_toks[HC - 1])
                            c.wait(nc.tensor, "v")
                        c.wait(nc.tensor, dsem, tok_d)
                        for hh in range(4):
                            hc = hs * 4 + hh
                            for jt in range(JT):
                                for dh2 in range(2):
                                    ins = nc.tensor.matmul(
                                        psO[jt][:, dh2 * 512:(dh2 + 1) * 512],
                                        h_sb[:, hc, jt * 128:(jt + 1) * 128],
                                        w2sl[bi][:, hh, dh2 * 512:(dh2 + 1) * 512],
                                        start=(hc == 0), stop=(hc == HC - 1))
                        tok_pe = c.inc(ins, "t")
                        w2_buf[bi] = tok_pe
                        if hs + 3 < 8:
                            w2_pend[hs + 3] = issue_w2(e, hs + 3)
                    c.wait(nc.vector, "t")
                    c.wait(nc.vector, "v")
                    for jt in range(JT):
                        c.inc(nc.vector.tensor_copy(
                            out=out_g[:, jt, :], in_=psO[jt][:, 0:DM]), "v")
                    def sc_slot(sc_, dcc_):
                        if sc_ % 2 == 0:
                            return psO[dcc_ // 4][
                                :, (dcc_ % 4) * SCB:(dcc_ % 4) * SCB + SCB]
                        if dcc_ < 4:
                            return psO[2][:, dcc_ * SCB:(dcc_ + 1) * SCB] \
                                if SCB * 4 <= 1024 else psO[2][:, (dcc_ % 4) * SCB:]
                        u = (psU, psU2)[(dcc_ - 4) // 2]
                        return u[:, ((dcc_ - 4) % 2) * SCB:((dcc_ - 4) % 2) * SCB + SCB]
                    c.wait(nc.tensor, "v")
                    sc_add_tok = [0] * NSC
                    for sc in range(NSC):
                        if sc >= 2:
                            c.wait(nc.tensor, "v", sc_add_tok[sc - 2])
                        for dcc in range(DC):
                            for jc in range(JT):
                                ins = nc.tensor.matmul(
                                    sc_slot(sc, dcc),
                                    out_g[:, jc, dcc * 128:(dcc + 1) * 128],
                                    PwT[par][:, jc, sc * SCB:sc * SCB + SCB],
                                    start=(jc == 0), stop=(jc == JT - 1))
                        tok_t = c.inc(ins, "t")
                        c.wait(nc.vector, "t", tok_t)
                        c.wait(nc.vector, "v")
                        for dcc in range(DC):
                            c.inc(nc.vector.tensor_add(
                                acc[:, dcc, sc * SCB:sc * SCB + SCB],
                                acc[:, dcc, sc * SCB:sc * SCB + SCB],
                                sc_slot(sc, dcc)), "v")
                        sc_add_tok[sc] = c.counts["v"]
                    for _i in range(2):
                        psu_tok[_i] = max(psu_tok[_i], c.counts["v"])
                    pso2_tok[0] = max(pso2_tok[0], c.counts["v"])
                    c.wait(nc.tensor, "v")

                for tt2 in range(Q // 128):
                    pb = psO[tt2 % 2]
                    c.wait(nc.tensor, "v")
                    for dcc in range(DC):
                        ins = nc.tensor.transpose(
                            pb[:, dcc * 128:(dcc + 1) * 128],
                            acc[:, dcc, tt2 * 128:(tt2 + 1) * 128], eye32[:])
                    tok_t = c.inc(ins, "t")
                    c.wait(nc.vector, "t", tok_t)
                    c.wait(nc.vector, "v")
                    c.inc(nc.vector.bn_stats(bst[:, 0:6], pb[:, 0:512]), "v")
                    c.inc(nc.vector.bn_stats(bst[:, 6:12], pb[:, 512:1024]), "v")
                    c.wait(nc.vector, "v")
                    c.inc(nc.vector.bn_aggr(mv[:], bst[:]), "v")
                    c.wait(nc.scalar, "v")
                    c.inc(nc.scalar.activation(
                        out=sm0[:, 6:7], in_=mv[:, 1:2], func=AF.Ln,
                        bias=eps128[:]), "s")
                    c.wait(nc.scalar, "s")
                    c.inc(nc.scalar.activation(
                        out=sm0[:, 7:8], in_=sm0[:, 6:7], func=AF.Exp,
                        scale=-0.5), "s")
                    yb = ytile[0]
                    c.wait(nc.vector, "s")
                    c.wait(nc.vector, "dy0")
                    c.inc(nc.vector.tensor_scalar(
                        out=yb[:], in0=pb[:, 0:DM], scalar1=mv[:, 0:1],
                        scalar2=sm0[:, 7:8], op0=ALU.subtract, op1=ALU.mult), "v")
                    tok_v = c.counts["v"]
                    c.wait(nc.tensor, "v")
                    c.wait(nc.sync, "v", tok_v)
                    t0 = q0 + tt2 * 128
                    c.inc(nc.sync.dma_start(
                        out=y_h[t0:t0 + 128, :], in_=yb[:]), "dy0", 16)

                if "t_w" in tap_h:
                    c.wait(nc.sync, "v")
                    c.inc(nc.sync.dma_start(
                        out=tap_h["t_w"][:, q0:q0 + Q], in_=wT8[:]), "dct", 16)
                c.join(ALL_ENG)

        c.wait(nc.sync, "dy0")
        c.wait(nc.sync, "dy1")
    return nc


def kernel(L0_attn_x, L0_attn_Wq, L0_attn_Wk, L0_attn_Wv, L0_attn_Wo,
           L0_ffn_w_router, L0_ffn_w1, L0_ffn_b1, L0_ffn_w2, L0_ffn_b2):
    B, S, D = L0_attn_x.shape
    nc = build_nc(S=S)
    common = {
        "wq": np.ascontiguousarray(L0_attn_Wq, dtype=np.float32),
        "wk": np.ascontiguousarray(L0_attn_Wk, dtype=np.float32),
        "wv": np.ascontiguousarray(L0_attn_Wv, dtype=np.float32),
        "wo": np.ascontiguousarray(L0_attn_Wo, dtype=np.float32),
        "wr": np.ascontiguousarray(L0_ffn_w_router, dtype=np.float32),
        "w1": np.ascontiguousarray(L0_ffn_w1, dtype=np.float32),
        "b1": np.ascontiguousarray(L0_ffn_b1, dtype=np.float32),
        "w2": np.ascontiguousarray(L0_ffn_w2, dtype=np.float32),
        "b2": np.ascontiguousarray(L0_ffn_b2, dtype=np.float32),
    }
    in_maps = []
    for b in range(B):
        m = dict(common)
        m["x"] = np.ascontiguousarray(L0_attn_x[b], dtype=np.float32)
        in_maps.append(m)
    res = run_bass_kernel_spmd(nc, in_maps, core_ids=list(range(B)))
    return np.stack([res.results[b]["y"] for b in range(B)], axis=0)


# version 23 (final)
# speedup vs baseline: 1.4151x; 1.0552x over previous
from contextlib import ExitStack

import numpy as np
import ml_dtypes

import concourse.bass as bass
import concourse.mybir as mybir
from concourse.bass_utils import run_bass_kernel_spmd

F32 = mybir.dt.float32
BF16 = mybir.dt.bfloat16
AF = mybir.ActivationFunctionType
ALU = mybir.AluOpType
AX = mybir.AxisListType

DM = 1024
DH = 4096
NE = 8
DC = DM // 128
HC = DH // 128
NEG = -1.0e30
EPS = 1e-5
SCL = 1.0 / 32.0


class Ctx:

    def __init__(self, nc):
        self.nc = nc
        self.sems = {}
        self.counts = {}
        for n in ("t", "v", "s"):
            self.new(n)

    def new(self, n):
        if n not in self.sems:
            self.sems[n] = self.nc.alloc_semaphore(n)
            self.counts[n] = 0

    def inc(self, ins, name, n=1):
        ins.then_inc(self.sems[name], n)
        self.counts[name] += n
        return self.counts[name]

    def wait(self, eng, name, val=None):
        v = self.counts[name] if val is None else val
        if v > 0:
            eng.wait_ge(self.sems[name], v)
        return v

    def join(self, engines):
        for eng in engines:
            for n in self.sems:
                self.wait(eng, n)


def build_nc(S=4096, n_rounds=4, taps=()):
    Q = S // n_rounds
    QT = Q // 128
    TB = min(512, Q)
    NTB = Q // TB
    KG = (Q + 511) // 512
    JT = max(1, -(-3 * Q // (8 * 128)))
    CAP = JT * 128
    SCB = min(256, Q)
    NSC = Q // SCB

    nc = bass.Bass("TRN2", target_bir_lowering=False)
    c = Ctx(nc)
    for _sn in ("dc", "dcg", "dct", "dwa", "dwq", "dwk", "dwv", "dx0", "dx1",
                "dx2", "dx3", "dx32", "dw1_0", "dw1_1", "dw1_2", "dw1s_0", "dw1s_1", "dw1s_2",
                "dwbW_0", "dwbW_1", "dwbW_2",
                "dw2_0", "dw2_1", "dw2_2", "dw2s_0", "dw2s_1", "dw2s_2",
                "dwb_0", "dwb_1", "dwb_2", "dy0", "dy1", "dkc", "dkw"):
        c.new(_sn)

    x_h = nc.dram_tensor("x", [S, DM], F32, kind="ExternalInput").ap()
    wq_h = nc.dram_tensor("wq", [DM, DM], F32, kind="ExternalInput").ap()
    wk_h = nc.dram_tensor("wk", [DM, DM], F32, kind="ExternalInput").ap()
    wv_h = nc.dram_tensor("wv", [DM, DM], F32, kind="ExternalInput").ap()
    wo_h = nc.dram_tensor("wo", [DM, DM], F32, kind="ExternalInput").ap()
    wr_h = nc.dram_tensor("wr", [DM, NE], F32, kind="ExternalInput").ap()
    w1_h = nc.dram_tensor("w1", [NE, DM, DH], F32, kind="ExternalInput").ap()
    b1_h = nc.dram_tensor("b1", [NE, DH], F32, kind="ExternalInput").ap()
    w2_h = nc.dram_tensor("w2", [NE, DH, DM], F32, kind="ExternalInput").ap()
    b2_h = nc.dram_tensor("b2", [NE, DM], F32, kind="ExternalInput").ap()
    y_h = nc.dram_tensor("y", [S, DM], F32, kind="ExternalOutput").ap()
    w1b_h = nc.dram_tensor("w1b", [NE, DM, DH], BF16, kind="Internal").ap()
    w2b_h = nc.dram_tensor("w2b", [NE, DH, DM], BF16, kind="Internal").ap()
    wvob_h = nc.dram_tensor("wvob", [DM, DM], BF16, kind="Internal").ap()
    kc_h = nc.dram_tensor("kcache", [n_rounds, 128, DC, Q], BF16, kind="Internal").ap()
    vc_h = nc.dram_tensor("vcache", [n_rounds, 128, QT, DM], BF16, kind="Internal").ap()
    tap_h = {}
    for tn, shp in (("t_r", [DM, S]), ("t_x1", [DM, S]), ("t_w", [NE, S]),
                    ("t_acc", [DM, S])):
        if tn in taps:
            tap_h[tn] = nc.dram_tensor(tn, shp, F32, kind="ExternalOutput").ap()

    eye16_c = nc.inline_tensor(np.eye(128, dtype=ml_dtypes.bfloat16), "eye16c")
    eye32_c = nc.inline_tensor(np.eye(128, dtype=np.float32), "eye32c")
    tri = np.where(np.arange(128)[None, :] <= np.arange(128)[:, None], 0.0, NEG)
    tri_c = nc.inline_tensor(tri.astype(np.float32), "tric")
    ones_col_c = nc.inline_tensor(np.ones((128, 1), dtype=ml_dtypes.bfloat16), "onescolc")
    ones_row_c = nc.inline_tensor(np.ones((1, 128), dtype=np.float32), "onesrowc")
    sel = np.zeros((NE, NE * 128), dtype=np.float32)
    for e in range(NE):
        sel[e, e * 128:(e + 1) * 128] = 1.0
    sel_c = nc.inline_tensor(sel, "selc")
    jrow_c = nc.inline_tensor(
        (np.arange(CAP, dtype=np.float32) + 1.0)[None, :].repeat(128, 0), "jrowc")
    jcol_c = nc.inline_tensor(
        (np.arange(JT * 128, dtype=np.float32) + 1.0).reshape(JT, 128).T.copy(),
        "jcolc")
    eps_c = nc.inline_tensor(np.full((1, 1), EPS, dtype=np.float32), "epsc")
    eps128_c = nc.inline_tensor(np.full((128, 1), EPS, dtype=np.float32), "eps128c")
    triu_c = nc.inline_tensor(
        np.triu(np.ones((128, 128))).astype(ml_dtypes.bfloat16), "triuc")

    ALL_ENG = (nc.tensor, nc.vector, nc.scalar, nc.gpsimd, nc.sync)

    with ExitStack() as P:
        acc = P.enter_context(nc.sbuf_tensor("acc", [128, DC, Q], F32))
        x1q = P.enter_context(nc.sbuf_tensor("x1q", [128, DC, Q], BF16))
        eye16 = P.enter_context(nc.sbuf_tensor("eye16", [128, 128], BF16))
        eye32 = P.enter_context(nc.sbuf_tensor("eye32", [128, 128], F32))
        trim = P.enter_context(nc.sbuf_tensor("trim", [128, 128], F32))
        ones_col = P.enter_context(nc.sbuf_tensor("ones_col", [128, 1], BF16))
        ones_row = P.enter_context(nc.sbuf_tensor("ones_row", [1, 128], F32))
        sel_sb = P.enter_context(nc.sbuf_tensor("sel_sb", [NE, NE * 128], F32))
        b1T = P.enter_context(nc.sbuf_tensor("b1T", [128, HC, NE], F32))
        b2T = P.enter_context(nc.sbuf_tensor("b2T", [128, DC, NE], F32))
        wr_sb = P.enter_context(nc.sbuf_tensor("wr_sb", [128, DC, NE], BF16))
        m_run = P.enter_context(nc.sbuf_tensor("m_run", [128, QT], F32))
        l_run = P.enter_context(nc.sbuf_tensor("l_run", [128, QT], F32))
        sm0 = P.enter_context(nc.sbuf_tensor("sm0", [128, 8], F32))
        sm1 = P.enter_context(nc.sbuf_tensor("sm1", [128, 8], F32))
        epsb = P.enter_context(nc.sbuf_tensor("epsb", [1, 1], F32))
        eps128 = P.enter_context(nc.sbuf_tensor("eps128", [128, 1], F32))
        bst = P.enter_context(nc.sbuf_tensor("bst", [128, 12], F32))
        mv = P.enter_context(nc.sbuf_tensor("mv", [128, 2], F32))
        triu_sb = P.enter_context(nc.sbuf_tensor("triu_sb", [128, 128], BF16))
        jrow = P.enter_context(nc.sbuf_tensor("jrow", [128, CAP], F32))
        jcol = P.enter_context(nc.sbuf_tensor("jcol", [128, JT], F32))
        x1_tm = P.enter_context(nc.sbuf_tensor("x1_tm", [128, QT, DM], BF16))

        with ExitStack() as PH:
            b1raw = PH.enter_context(nc.sbuf_tensor("b1raw", [NE, DH], F32))
            b2raw = PH.enter_context(nc.sbuf_tensor("b2raw", [NE, DM], F32))
            wv_tmp = PH.enter_context(nc.sbuf_tensor("wv_tmp", [128, DC, DM], BF16))
            wvT_tmp = PH.enter_context(nc.sbuf_tensor("wvT_tmp", [128, DC, DM], BF16))
            wvo_pre = PH.enter_context(nc.sbuf_tensor("wvo_pre", [128, DC, DM], BF16))
            psP = PH.enter_context(nc.psum_tensor("psP", [128, 512], F32))
            psP_T = PH.enter_context(nc.psum_tensor("psP_T", [128, 1024], BF16))

            for dst, src in ((eye16, eye16_c), (eye32, eye32_c), (trim, tri_c),
                             (ones_col, ones_col_c), (ones_row, ones_row_c),
                             (sel_sb, sel_c), (epsb, eps_c), (triu_sb, triu_c),
                             (jrow, jrow_c), (jcol, jcol_c), (eps128, eps128_c)):
                c.inc(nc.sync.dma_start(out=dst[:], in_=src.ap()), "dc", 16)
            c.inc(nc.sync.dma_start(out=b1raw[:], in_=b1_h), "dc", 16)
            c.inc(nc.sync.dma_start(out=b2raw[:], in_=b2_h), "dc", 16)
            c.inc(nc.gpsimd.dma_start(
                out=wr_sb[:], in_=wr_h.rearrange("(dc p) e -> p dc e", p=128)), "dwa", 16)
            c.wait(nc.tensor, "dc")
            for hc in range(HC):
                tok_t = c.inc(nc.tensor.transpose(
                    psP[:, 0:NE], b1raw[0:NE, hc * 128:(hc + 1) * 128],
                    eye32[0:NE, 0:NE]), "t")
                c.wait(nc.vector, "t", tok_t)
                c.inc(nc.vector.tensor_copy(out=b1T[:, hc, :], in_=psP[:, 0:NE]), "v")
                c.wait(nc.tensor, "v")
            for dc in range(DC):
                tok_t = c.inc(nc.tensor.transpose(
                    psP[:, 8:8 + NE], b2raw[0:NE, dc * 128:(dc + 1) * 128],
                    eye32[0:NE, 0:NE]), "t")
                c.wait(nc.vector, "t", tok_t)
                c.inc(nc.vector.tensor_copy(out=b2T[:, dc, :], in_=psP[:, 8:8 + NE]), "v")
                c.wait(nc.tensor, "v")

            tok_wv = c.inc(nc.gpsimd.dma_start(
                out=wv_tmp[:], in_=wv_h.rearrange("(dc p) n -> p dc n", p=128)), "dwa", 16)
            c.wait(nc.tensor, "dwa", tok_wv)
            for dc in range(DC):
                for dvc in range(DC):
                    c.inc(nc.tensor.transpose(
                        psP_T[:, dvc * 128:(dvc + 1) * 128],
                        wv_tmp[:, dc, dvc * 128:(dvc + 1) * 128], eye16[:]), "t")
                c.wait(nc.vector, "t")
                c.inc(nc.vector.tensor_copy(
                    out=wvT_tmp[:, dc, :], in_=psP_T[:, 0:DM]), "v")
                c.wait(nc.tensor, "v")
            c.wait(nc.gpsimd, "t")
            tok_wo = c.inc(nc.gpsimd.dma_start(
                out=wv_tmp[:], in_=wo_h.rearrange("(dc p) n -> p dc n", p=128)), "dwa", 16)
            c.wait(nc.tensor, "dwa", tok_wo)
            for dc in range(DC):
                for half in range(DM // 512):
                    for dvc in range(DC):
                        ins = nc.tensor.matmul(
                            psP[:, 0:512],
                            wvT_tmp[:, dc, dvc * 128:(dvc + 1) * 128],
                            wv_tmp[:, dvc, half * 512:half * 512 + 512],
                            start=(dvc == 0), stop=(dvc == DC - 1))
                    tok_t = c.inc(ins, "t")
                    c.wait(nc.vector, "t", tok_t)
                    c.inc(nc.vector.tensor_copy(
                        out=wvo_pre[:, dc, half * 512:half * 512 + 512],
                        in_=psP[:, 0:512]), "v")
                    c.wait(nc.tensor, "v")
            c.wait(nc.sync, "v")
            c.inc(nc.sync.dma_start(
                out=wvob_h.rearrange("(dc p) n -> p dc n", p=128),
                in_=wvo_pre[:]), "dc", 16)
            c.join(ALL_ENG)

        for r in range(n_rounds):
            q0 = r * Q

            with ExitStack() as PA:
                sb = lambda nm, shp, dt: PA.enter_context(
                    nc.sbuf_tensor(f"{nm}_r{r}", shp, dt))
                psb = lambda nm, shp, dt: PA.enter_context(
                    nc.psum_tensor(f"{nm}_r{r}", shp, dt))
                q_fm = sb("q_fm", [128, DC, Q], BF16)
                k_blk = sb("k_blk", [128, DC, Q], BF16)
                v_blk = sb("v_blk", [128, QT, DM], BF16)
                wq_sb = sb("wq_sb", [128, DC, DM], BF16)
                wk_sb = sb("wk_sb", [128, DC, DM], BF16)
                wvo_sb = sb("wvo_sb", [128, DC, DM], BF16)
                xt16 = [sb(f"xt16_{i}", [128, 2, DM], BF16) for i in range(2)]
                xf16 = sb("xf16", [128, DC, 512], BF16)
                xt32 = sb("xt32", [128, DM], F32)
                rtm = sb("rtm", [128, DM], F32)
                rowst = sb("rowst", [1, 128], F32)
                s_row = sb("s_row", [128, Q], BF16)
                p16 = sb("p16", [128, Q], BF16)
                attnT = sb("attnT", [128, QT, 128], BF16)

                psA_s = psb("psA_s", [128, 512], F32)
                psA_s2 = psb("psA_s2", [128, 512], F32)
                psA_pv = [psb(f"psA_pv{i}", [128, 1024], F32) for i in range(2)]
                psA_T = psb("psA_T", [128, 1024], BF16)
                psA_m = psb("psA_m", [128, 512], F32)

                def load_w_fm(dst, src, sem):
                    return c.inc(nc.gpsimd.dma_start(
                        out=dst[:], in_=src.rearrange("(dc p) n -> p dc n", p=128)),
                        sem, 16)

                tok_wq = load_w_fm(wq_sb, wq_h, "dwq")
                tok_wk = load_w_fm(wk_sb, wk_h, "dwk")
                tok_wv = c.inc(nc.sync.dma_start(
                    out=wvo_sb[:],
                    in_=wvob_h.rearrange("(dc p) n -> p dc n", p=128)), "dwv", 16)

                c.wait(nc.tensor, "dwq", tok_wq)
                xb_tok = [0, 0]
                sbank_tok = [0, 0]

                def load_x_tiles(base, gt_, use):
                    np_ = (gt_ + 1) // 2
                    toks = []
                    for p_ in range(np_):
                        nrow = min(256, gt_ * 128 - p_ * 256)
                        bi = p_ % 2
                        c.wait(nc.gpsimd, "t", xb_tok[bi])
                        toks.append((bi, c.inc(nc.gpsimd.dma_start(
                            out=xt16[bi][:, 0:nrow // 128, :],
                            in_=x_h[base + p_ * 256:base + p_ * 256 + nrow, :]
                            .rearrange("(b p) d -> p b d", p=128)),
                            f"dx{bi}", 16), nrow // 128, p_))
                    for bi, tok_d, nt, p_ in toks:
                        c.wait(nc.tensor, f"dx{bi}", tok_d)
                        for j_ in range(nt):
                            use(p_ * 2 + j_, xt16[bi][:, j_, :])
                        xb_tok[bi] = c.counts["t"]

                for g in range(KG):
                    gn = min(512, Q - g * 512)
                    gt = gn // 128

                    def use_pre(i, src_):
                        for dcc in range(DC):
                            c.inc(nc.tensor.transpose(
                                psA_T[:, dcc * 128:(dcc + 1) * 128],
                                src_[:, dcc * 128:(dcc + 1) * 128], eye16[:]), "t")
                        c.wait(nc.vector, "t")
                        c.inc(nc.vector.tensor_copy(
                            out=xf16[:, :, i * 128:(i + 1) * 128],
                            in_=psA_T[:, 0:DM].rearrange("p (a b) -> p a b", b=128)), "v")
                        c.wait(nc.tensor, "v")

                    load_x_tiles(q0 + g * 512, gt, use_pre)
                    for kc in range(DC):
                        for dcc in range(DC):
                            ins = nc.tensor.matmul(
                                psA_s[:, 0:gn], wq_sb[:, dcc, kc * 128:(kc + 1) * 128],
                                xf16[:, dcc, 0:gn],
                                start=(dcc == 0), stop=(dcc == DC - 1))
                        tok_t = c.inc(ins, "t")
                        c.wait(nc.vector, "t", tok_t)
                        c.inc(nc.vector.tensor_copy(
                            out=q_fm[:, kc, g * 512:g * 512 + gn], in_=psA_s[:, 0:gn]), "v")
                        c.wait(nc.tensor, "v")

                for kb in range(r + 1):
                    kb0 = kb * Q
                    if kb < r:
                        c.wait(nc.sync, "t")
                        tok_kc = c.inc(nc.sync.dma_start(
                            out=k_blk[:], in_=kc_h[kb]), "dkc", 16)
                        tok_vc = c.inc(nc.sync.dma_start(
                            out=v_blk[:], in_=vc_h[kb]), "dkc", 16)
                        c.wait(nc.tensor, "dkc", tok_vc)
                    if kb == r:
                        c.wait(nc.tensor, "dwk", tok_wk)
                        for g in range(KG):
                            gn = min(512, Q - g * 512)
                            gt = gn // 128
                            def use_kv(i, src, g=g):
                                for dcc in range(DC):
                                    c.inc(nc.tensor.transpose(
                                        psA_T[:, dcc * 128:(dcc + 1) * 128],
                                        src[:, dcc * 128:(dcc + 1) * 128], eye16[:]), "t")
                                c.wait(nc.vector, "t")
                                c.inc(nc.vector.tensor_copy(
                                    out=xf16[:, :, i * 128:(i + 1) * 128],
                                    in_=psA_T[:, 0:DM].rearrange("p (a b) -> p a b", b=128)), "v")
                                c.wait(nc.tensor, "v")
                                c.wait(nc.tensor, "dwv", tok_wv)
                                for half in range(DM // 512):
                                    for dcc in range(DC):
                                        ins = nc.tensor.matmul(
                                            psA_pv[0][:, half * 512:half * 512 + 512],
                                            xf16[:, dcc, i * 128:(i + 1) * 128],
                                            wvo_sb[:, dcc, half * 512:half * 512 + 512],
                                            start=(dcc == 0), stop=(dcc == DC - 1))
                                tok_t = c.inc(ins, "t")
                                c.wait(nc.vector, "t", tok_t)
                                c.inc(nc.vector.tensor_copy(
                                    out=v_blk[:, g * 4 + i, :], in_=psA_pv[0][:, 0:DM]), "v")
                                c.wait(nc.tensor, "v")

                            load_x_tiles(kb0 + g * 512, gt, use_kv)
                            for kc in range(DC):
                                for dcc in range(DC):
                                    ins = nc.tensor.matmul(
                                        psA_s[:, 0:gn],
                                        wk_sb[:, dcc, kc * 128:(kc + 1) * 128],
                                        xf16[:, dcc, 0:gn],
                                        start=(dcc == 0), stop=(dcc == DC - 1))
                                tok_t = c.inc(ins, "t")
                                c.wait(nc.vector, "t", tok_t)
                                c.inc(nc.vector.tensor_copy(
                                    out=k_blk[:, kc, g * 512:g * 512 + gn],
                                    in_=psA_s[:, 0:gn]), "v")
                                c.wait(nc.tensor, "v")

                    if kb == r and r + 1 < n_rounds:
                        c.wait(nc.sync, "v")
                        c.inc(nc.sync.dma_start(out=kc_h[r], in_=k_blk[:]), "dkw", 16)
                        c.inc(nc.sync.dma_start(out=vc_h[r], in_=v_blk[:]), "dkw", 16)
                    diag = (kb == r)
                    for qt in range(QT):
                        tk = (qt + 1) * 128 if diag else Q
                        ng = (tk + 511) // 512
                        for g in range(ng):
                            n0 = g * 512
                            nn = min(512, tk - n0)
                            ps = psA_s if g % 2 == 0 else psA_s2
                            c.wait(nc.tensor, "v", sbank_tok[g % 2])
                            for kc in range(DC):
                                ins = nc.tensor.matmul(
                                    ps[:, 0:nn], q_fm[:, kc, qt * 128:(qt + 1) * 128],
                                    k_blk[:, kc, n0:n0 + nn],
                                    start=(kc == 0), stop=(kc == DC - 1))
                            tok_t = c.inc(ins, "t")
                            c.wait(nc.vector, "t", tok_t)
                            nmask = 128 if (diag and g == ng - 1) else 0
                            if nn - nmask > 0:
                                c.inc(nc.vector.tensor_copy(
                                    out=s_row[:, n0:n0 + nn - nmask],
                                    in_=ps[:, 0:nn - nmask]), "v")
                            if nmask:
                                c.inc(nc.vector.tensor_add(
                                    s_row[:, n0 + nn - 128:n0 + nn],
                                    ps[:, nn - 128:nn], trim[:]), "v")
                            sbank_tok[g % 2] = c.counts["v"]
                        c.wait(nc.vector, "v")
                        c.inc(nc.vector.reduce_max(
                            sm0[:, 0:1], s_row[:, 0:tk], axis=AX.X), "v")
                        mcur = m_run[:, qt:qt + 1]
                        c.wait(nc.vector, "v")
                        if kb == 0:
                            c.inc(nc.vector.tensor_copy(out=mcur, in_=sm0[:, 0:1]), "v")
                        else:
                            c.inc(nc.vector.tensor_tensor(
                                sm1[:, 0:1], mcur, sm0[:, 0:1], op=ALU.max), "v")
                            c.wait(nc.vector, "v")
                            c.inc(nc.vector.tensor_sub(
                                sm0[:, 1:2], mcur, sm1[:, 0:1]), "v")
                            c.wait(nc.vector, "v")
                            c.inc(nc.vector.tensor_copy(out=mcur, in_=sm1[:, 0:1]), "v")
                            c.wait(nc.scalar, "v")
                            c.inc(nc.scalar.activation(
                                out=sm0[:, 2:3], in_=sm0[:, 1:2], func=AF.Exp,
                                scale=SCL), "s")
                        c.wait(nc.vector, "v")
                        c.inc(nc.vector.tensor_scalar_mul(sm0[:, 3:4], mcur, -SCL), "v")
                        c.wait(nc.scalar, "v")
                        c.inc(nc.scalar.activation(
                            out=p16[:, 0:tk], in_=s_row[:, 0:tk], func=AF.Exp,
                            bias=sm0[:, 3:4], scale=SCL, accum_out=sm0[:, 4:5]), "s")
                        c.wait(nc.vector, "s")
                        if kb == 0:
                            c.inc(nc.vector.tensor_copy(
                                out=l_run[:, qt:qt + 1], in_=sm0[:, 4:5]), "v")
                        else:
                            c.inc(nc.vector.scalar_tensor_tensor(
                                out=l_run[:, qt:qt + 1], in0=l_run[:, qt:qt + 1],
                                scalar=sm0[:, 2:3], in1=sm0[:, 4:5],
                                op0=ALU.mult, op1=ALU.add), "v")
                        ntk = tk // 128
                        c.wait(nc.tensor, "s")
                        c.wait(nc.tensor, "v")
                        for tkc in range(ntk):
                            c.inc(nc.tensor.transpose(
                                psA_T[:, tkc * 128:(tkc + 1) * 128],
                                p16[:, tkc * 128:(tkc + 1) * 128], eye16[:]), "t")
                        c.wait(nc.vector, "t")
                        c.inc(nc.vector.tensor_copy(
                            out=attnT[:, 0:ntk, :],
                            in_=psA_T[:, 0:ntk * 128].rearrange(
                                "p (a b) -> p a b", b=128)), "v")
                        pvb = psA_pv[qt % 2]
                        c.wait(nc.tensor, "v")
                        for dvc in range(DC):
                            for tkc in range(ntk):
                                ins = nc.tensor.matmul(
                                    pvb[:, dvc * 128:(dvc + 1) * 128],
                                    v_blk[:, tkc, dvc * 128:(dvc + 1) * 128],
                                    attnT[:, tkc, :],
                                    start=(tkc == 0), stop=(tkc == ntk - 1))
                        tok_pv = c.inc(ins, "t")
                        accs = acc[:, :, qt * 128:(qt + 1) * 128]
                        c.wait(nc.vector, "t", tok_pv)
                        if kb == 0:
                            c.inc(nc.vector.tensor_copy(
                                out=accs,
                                in_=pvb[:, 0:DM].rearrange("p (a b) -> p a b", b=128)), "v")
                        else:
                            c.wait(nc.tensor, "s")
                            tok_t = c.inc(nc.tensor.transpose(
                                psA_m[0:1, 0:128], sm0[:, 2:3], eye32[:]), "t")
                            c.wait(nc.vector, "t", tok_t)
                            c.inc(nc.vector.tensor_copy(
                                out=rowst[0:1, 0:128], in_=psA_m[0:1, 0:128]), "v")
                            c.wait(nc.tensor, "v")
                            tok_t = c.inc(nc.tensor.matmul(
                                psA_m[:, 128:256], ones_row[:], rowst[0:1, 0:128],
                                start=True, stop=True), "t")
                            c.wait(nc.vector, "t", tok_t)
                            for dcc in range(DC):
                                c.inc(nc.vector.tensor_mul(
                                    accs[:, dcc, :], accs[:, dcc, :],
                                    psA_m[:, 128:256]), "v")
                            c.wait(nc.vector, "v")
                            c.inc(nc.vector.tensor_add(
                                accs, accs,
                                pvb[:, 0:DM].rearrange("p (a b) -> p a b", b=128)), "v")
                        c.wait(nc.tensor, "v")

                x1tm_tok = [0] * QT
                for qt in range(QT):
                    t0 = q0 + qt * 128
                    c.wait(nc.sync, "v")
                    tok_d = c.inc(nc.sync.dma_start(
                        out=xt32[:], in_=x_h[t0:t0 + 128, :]), "dx32", 16)
                    c.wait(nc.vector, "v")
                    c.inc(nc.vector.reciprocal(sm0[:, 5:6], l_run[:, qt:qt + 1]), "v")
                    pp = psA_pv[qt % 2]
                    c.wait(nc.tensor, "v")
                    for dcc in range(DC):
                        ins = nc.tensor.transpose(
                            pp[:, dcc * 128:(dcc + 1) * 128],
                            acc[:, dcc, qt * 128:(qt + 1) * 128], eye32[:])
                    tok_t = c.inc(ins, "t")
                    c.wait(nc.vector, "t", tok_t)
                    c.wait(nc.vector, "dx32", tok_d)
                    c.wait(nc.vector, "v")
                    c.inc(nc.vector.scalar_tensor_tensor(
                        out=rtm[:], in0=pp[:, 0:DM], scalar=sm0[:, 5:6],
                        in1=xt32[:], op0=ALU.mult, op1=ALU.add), "v")
                    c.wait(nc.vector, "v")
                    c.inc(nc.vector.bn_stats(bst[:, 0:6], rtm[:, 0:512]), "v")
                    c.inc(nc.vector.bn_stats(bst[:, 6:12], rtm[:, 512:1024]), "v")
                    c.wait(nc.vector, "v")
                    c.inc(nc.vector.bn_aggr(mv[:], bst[:]), "v")
                    c.wait(nc.scalar, "v")
                    c.inc(nc.scalar.activation(
                        out=sm0[:, 6:7], in_=mv[:, 1:2], func=AF.Ln,
                        bias=eps128[:]), "s")
                    c.wait(nc.scalar, "s")
                    c.inc(nc.scalar.activation(
                        out=sm0[:, 7:8], in_=sm0[:, 6:7], func=AF.Exp,
                        scale=-0.5), "s")
                    c.wait(nc.vector, "s")
                    x1tm_tok[qt] = c.inc(nc.vector.tensor_scalar(
                        out=x1_tm[:, qt, :], in0=rtm[:], scalar1=mv[:, 0:1],
                        scalar2=sm0[:, 7:8], op0=ALU.subtract, op1=ALU.mult), "v")
                trev_tok = 0
                for qt in range(QT):
                    c.wait(nc.tensor, "v", max(x1tm_tok[qt], trev_tok))
                    for dcc in range(DC):
                        ins = nc.tensor.transpose(
                            psA_T[:, dcc * 128:(dcc + 1) * 128],
                            x1_tm[:, qt, dcc * 128:(dcc + 1) * 128], eye16[:])
                    tok_t = c.inc(ins, "t")
                    c.wait(nc.vector, "t", tok_t)
                    trev_tok = c.inc(nc.vector.tensor_copy(
                        out=x1q[:, :, qt * 128:(qt + 1) * 128],
                        in_=psA_T[:, 0:DM].rearrange("p (a b) -> p a b", b=128)), "v")
                c.wait(nc.tensor, "v")

                if "t_x1" in tap_h:
                    c.wait(nc.gpsimd, "v")
                    c.inc(nc.gpsimd.dma_start(
                        out=tap_h["t_x1"][:, q0:q0 + Q].rearrange(
                            "(dc p) t -> p dc t", p=128),
                        in_=x1q[:]), "dcg", 16)
                if "t_acc" in tap_h:
                    c.wait(nc.gpsimd, "v")
                    c.inc(nc.gpsimd.dma_start(
                        out=tap_h["t_acc"][:, q0:q0 + Q].rearrange(
                            "(dc p) t -> p dc t", p=128),
                        in_=acc[:]), "dcg", 16)
                c.join(ALL_ENG)

            with ExitStack() as PF:
                sb = lambda nm, shp, dt: PF.enter_context(
                    nc.sbuf_tensor(f"{nm}_r{r}", shp, dt))
                psb = lambda nm, shp, dt: PF.enter_context(
                    nc.psum_tensor(f"{nm}_r{r}", shp, dt))
                w1sl = [sb(f"w1sl{i}", [128, DC, 512], BF16) for i in range(3)]
                w2sl = [sb(f"w2sl{i}", [128, 4, DM], BF16) for i in range(3)]
                h_sb = sb("h_sb", [128, HC, CAP], BF16)
                x1_g = sb("x1_g", [128, DC, CAP], BF16)
                out_g = sb("out_g", [128, JT, DM], BF16)
                P_sb = [sb(f"P_sb{i}", [128, QT, CAP], BF16) for i in range(2)]
                PwT = [sb(f"PwT{i}", [128, JT, Q], BF16) for i in range(2)]
                selm = sb("selm", [128, QT, NE], BF16)
                selm32 = sb("selm32", [128, QT, NE], F32)
                cumg = sb("cumg", [128, QT, NE], F32)
                offr = sb("offr", [1, QT, NE], F32)
                totr = sb("totr", [1, QT * NE], F32)
                cumgT = sb("cumgT", [NE, Q], F32)
                wbcf = [sb(f"wbcf{i}", [128, Q], BF16) for i in range(2)]
                tmpP = sb("tmpP", [128, 512], F32)
                ytile = [sb("yt0", [128, DM], F32)]
                wT8 = sb("wT8", [NE, Q], F32)
                wT8b = sb("wT8b", [NE, Q], BF16)
                prb = sb("prb", [128, NE], F32)

                psO = [psb(f"psO{i}", [128, 1024], F32) for i in range(3)]
                psU = psb("psU", [128, 512], F32)
                psU2 = psb("psU2", [128, 512], F32)

                for tt in range(QT):
                    c.wait(nc.tensor, "v")
                    for dcc in range(DC):
                        ins = nc.tensor.matmul(
                            psU[:, 0:NE], x1q[:, dcc, tt * 128:(tt + 1) * 128],
                            wr_sb[:, dcc, :], start=(dcc == 0), stop=(dcc == DC - 1))
                    tok_t = c.inc(ins, "t")
                    c.wait(nc.vector, "t", tok_t)
                    c.inc(nc.vector.reduce_max(
                        sm0[:, 0:1], psU[:, 0:NE], axis=AX.X), "v")
                    c.wait(nc.vector, "v")
                    c.inc(nc.vector.tensor_scalar_mul(sm0[:, 1:2], sm0[:, 0:1], -1.0), "v")
                    c.wait(nc.scalar, "v")
                    c.inc(nc.scalar.activation(
                        out=prb[:], in_=psU[:, 0:NE], func=AF.Exp,
                        bias=sm0[:, 1:2], accum_out=sm0[:, 2:3]), "s")
                    c.wait(nc.vector, "s")
                    c.inc(nc.vector.reciprocal(sm0[:, 3:4], sm0[:, 2:3]), "v")
                    c.wait(nc.vector, "v")
                    c.inc(nc.vector.tensor_scalar_mul(prb[:], prb[:], sm0[:, 3:4]), "v")
                    c.wait(nc.vector, "v")
                    c.inc(nc.vector.reduce_max(sm0[:, 4:5], prb[:], axis=AX.X), "v")
                    c.wait(nc.vector, "v")
                    c.inc(nc.vector.tensor_scalar(
                        out=sm1[:, 0:NE], in0=prb[:], scalar1=sm0[:, 4:5],
                        scalar2=None, op0=ALU.is_ge), "v")
                    c.wait(nc.vector, "v")
                    c.inc(nc.vector.scalar_tensor_tensor(
                        out=sm1[:, 0:NE], in0=sm1[:, 0:NE], scalar=NEG,
                        in1=prb[:], op0=ALU.mult, op1=ALU.add), "v")
                    c.wait(nc.vector, "v")
                    c.inc(nc.vector.reduce_max(sm0[:, 5:6], sm1[:, 0:NE], axis=AX.X), "v")
                    c.wait(nc.vector, "v")
                    c.inc(nc.vector.tensor_scalar(
                        out=sm1[:, 0:NE], in0=prb[:], scalar1=sm0[:, 5:6],
                        scalar2=None, op0=ALU.is_ge), "v")
                    c.wait(nc.vector, "v")
                    c.inc(nc.vector.tensor_copy(out=selm[:, tt, :], in_=sm1[:, 0:NE]), "v")
                    c.inc(nc.vector.tensor_copy(out=selm32[:, tt, :], in_=sm1[:, 0:NE]), "v")
                    c.wait(nc.vector, "v")
                    c.inc(nc.vector.tensor_mul(prb[:], prb[:], sm1[:, 0:NE]), "v")
                    c.wait(nc.tensor, "s")
                    c.wait(nc.tensor, "v")
                    tok_t = c.inc(nc.tensor.transpose(
                        psU[0:NE, 128:256], prb[:], eye32[:]), "t")
                    c.wait(nc.vector, "t", tok_t)
                    c.inc(nc.vector.tensor_copy(
                        out=wT8[:, tt * 128:(tt + 1) * 128], in_=psU[0:NE, 128:256]), "v")
                    c.wait(nc.tensor, "v")
                c.wait(nc.vector, "v")
                c.inc(nc.vector.tensor_copy(out=wT8b[:], in_=wT8[:]), "v")

                c.wait(nc.tensor, "v")
                for tt in range(QT):
                    ins = nc.tensor.matmul(
                        psU[0:1, 256:256 + NE].rearrange("a b -> a b") if False
                        else psU[0:1, 256 + tt * NE:256 + (tt + 1) * NE],
                        ones_col[:], selm[:, tt, :], start=True, stop=True)
                tok_t = c.inc(ins, "t")
                c.wait(nc.vector, "t", tok_t)
                c.inc(nc.vector.tensor_copy(
                    out=totr[0:1, :], in_=psU[0:1, 256:256 + QT * NE]), "v")
                c.wait(nc.vector, "v")
                c.inc(nc.vector.memset(offr[0:1, 0, :], 0.0), "v")
                for tt in range(1, QT):
                    c.wait(nc.vector, "v")
                    c.inc(nc.vector.tensor_add(
                        offr[0:1, tt, :], offr[0:1, tt - 1, :],
                        totr[0:1, (tt - 1) * NE:tt * NE]), "v")
                c.wait(nc.tensor, "v")
                for tt in range(QT):
                    ins = nc.tensor.matmul(
                        psU[:, 0:NE], triu_sb[:], selm[:, tt, :],
                        start=True, stop=True)
                    tok_t = c.inc(ins, "t")
                    ins = nc.tensor.matmul(
                        psU[:, NE:2 * NE], ones_row[:], offr[0:1, tt, :],
                        start=True, stop=True)
                    tok_t2 = c.inc(ins, "t")
                    c.wait(nc.vector, "t", tok_t2)
                    c.inc(nc.vector.tensor_copy(
                        out=cumg[:, tt, :], in_=psU[:, 0:NE]), "v")
                    c.wait(nc.vector, "v")
                    c.inc(nc.vector.tensor_add(
                        cumg[:, tt, :], cumg[:, tt, :], psU[:, NE:2 * NE]), "v")
                    c.wait(nc.tensor, "v")

                c.wait(nc.tensor, "v")
                for tt in range(QT):
                    tok_t = c.inc(nc.tensor.transpose(
                        psU[0:NE, 0:128], cumg[:, tt, :], eye32[:]), "t")
                    c.wait(nc.vector, "t", tok_t)
                    c.inc(nc.vector.tensor_copy(
                        out=cumgT[:, tt * 128:(tt + 1) * 128], in_=psU[0:NE, 0:128]), "v")
                    c.wait(nc.tensor, "v")

                c.wait(nc.vector, "v")
                for dcc in range(DC):
                    c.inc(nc.vector.tensor_copy(
                        out=acc[:, dcc, :], in_=x1q[:, dcc, :]), "v")

                w1_buf = [0, 0, 0]
                w1_wb = [0, 0, 0]
                w2_buf = [0, 0, 0]
                w2_wb = [0, 0, 0]
                gath_tok = [0]
                pso2_tok = [0]
                b2r_tok = [0]
                tok_P = [0, 0]
                psu_tok = [0, 0]

                def build_P(e_):
                    par = e_ % 2
                    nh = (Q + 511) // 512
                    for hf in range(nh):
                        h0 = hf * 512
                        hn = min(512, Q - h0)
                        if hf % 2 == 0:
                            pw, pc = psU[:, 0:hn], psU2[:, 0:hn]
                            sl = 0
                        else:
                            pw, pc = psO[2][:, 0:hn], psO[2][:, 512:512 + hn]
                            sl = 1
                        c.wait(nc.tensor, "v", max(psu_tok[sl], gath_tok[0]))
                        c.inc(nc.tensor.matmul(
                            pw, sel_sb[:, e_ * 128:(e_ + 1) * 128],
                            wT8[:, h0:h0 + hn], start=True, stop=True), "t")
                        tok_t = c.inc(nc.tensor.matmul(
                            pc, sel_sb[:, e_ * 128:(e_ + 1) * 128],
                            cumgT[:, h0:h0 + hn], start=True, stop=True), "t")
                        c.wait(nc.vector, "t", tok_t)
                        c.wait(nc.vector, "v")
                        c.inc(nc.vector.tensor_copy(
                            out=wbcf[par][:, h0:h0 + hn], in_=pw), "v")
                        for tt in range(h0 // 128, (h0 + hn) // 128):
                            c.inc(nc.vector.tensor_scalar(
                                out=P_sb[par][:, tt, 0:CAP], in0=jrow[:, 0:CAP],
                                scalar1=cumg[:, tt, e_:e_ + 1],
                                scalar2=selm32[:, tt, e_:e_ + 1],
                                op0=ALU.is_equal, op1=ALU.mult), "v")
                        for jt in range(JT):
                            c.wait(nc.vector, "v")
                            c.inc(nc.vector.tensor_scalar(
                                out=tmpP[:, 0:hn], in0=pc,
                                scalar1=jcol[:, jt:jt + 1], scalar2=None,
                                op0=ALU.is_equal), "v")
                            c.wait(nc.vector, "v")
                            c.inc(nc.vector.tensor_mul(
                                PwT[par][:, jt, h0:h0 + hn],
                                tmpP[:, 0:hn], wbcf[par][:, h0:h0 + hn]), "v")
                        psu_tok[sl] = c.counts["v"]
                    tok_P[par] = c.counts["v"]
                    pso2_tok[0] = max(pso2_tok[0], c.counts["v"])

                def issue_w1(e_, hs):
                    bi = hs % 3
                    if r == 0:
                        c.wait(nc.gpsimd, "t", w1_buf[bi])
                        if w1_wb[bi]:
                            c.wait(nc.gpsimd, f"dwbW_{bi}", w1_wb[bi])
                        dsem = f"dw1_{bi}"
                        tok_d = c.inc(nc.gpsimd.dma_start(
                            out=w1sl[bi][:],
                            in_=w1_h[e_][:, hs * 512:(hs + 1) * 512]
                            .rearrange("(dc p) h -> p dc h", p=128)), dsem, 16)
                        c.wait(nc.sync, dsem, tok_d)
                        w1_wb[bi] = c.inc(nc.sync.dma_start(
                            out=w1b_h[e_][:, hs * 512:(hs + 1) * 512]
                            .rearrange("(dc p) h -> p dc h", p=128),
                            in_=w1sl[bi][:]), f"dwbW_{bi}", 16)
                    else:
                        c.wait(nc.sync, "t", w1_buf[bi])
                        dsem = f"dw1s_{bi}"
                        tok_d = c.inc(nc.sync.dma_start(
                            out=w1sl[bi][:],
                            in_=w1b_h[e_][:, hs * 512:(hs + 1) * 512]
                            .rearrange("(dc p) h -> p dc h", p=128)), dsem, 16)
                    return dsem, tok_d

                def issue_w2(e_, hs):
                    bi = hs % 3
                    if r == 0:
                        c.wait(nc.gpsimd, "t", w2_buf[bi])
                        if w2_wb[bi]:
                            c.wait(nc.gpsimd, f"dwb_{bi}", w2_wb[bi])
                        dsem = f"dw2_{bi}"
                        tok_d = c.inc(nc.gpsimd.dma_start(
                            out=w2sl[bi][:],
                            in_=w2_h[e_, hs * 512:(hs + 1) * 512, :]
                            .rearrange("(hcc p) d -> p hcc d", p=128)), dsem, 16)
                        c.wait(nc.scalar, dsem, tok_d)
                        w2_wb[bi] = c.inc(nc.scalar.dma_start(
                            out=w2b_h[e_, hs * 512:(hs + 1) * 512, :]
                            .rearrange("(hcc p) d -> p hcc d", p=128),
                            in_=w2sl[bi][:]), f"dwb_{bi}", 16)
                    else:
                        c.wait(nc.scalar, "t", w2_buf[bi])
                        dsem = f"dw2s_{bi}"
                        tok_d = c.inc(nc.scalar.dma_start(
                            out=w2sl[bi][:],
                            in_=w2b_h[e_, hs * 512:(hs + 1) * 512, :]
                            .rearrange("(hcc p) d -> p hcc d", p=128)), dsem, 16)
                    return dsem, tok_d

                build_P(0)
                for e in range(NE):
                    par = e % 2
                    g_tok = [max(tok_P[par], pso2_tok[0])] * 2
                    for dcc in range(DC):
                        gsl = psO[2][:, 0:CAP] if dcc % 2 == 0 \
                            else psO[2][:, 512:512 + CAP]
                        c.wait(nc.tensor, "v", g_tok[dcc % 2])
                        for tch in range(QT):
                            ins = nc.tensor.matmul(
                                gsl,
                                x1_tm[:, tch, dcc * 128:(dcc + 1) * 128],
                                P_sb[par][:, tch, 0:CAP],
                                start=(tch == 0), stop=(tch == QT - 1))
                        tok_t = c.inc(ins, "t")
                        c.wait(nc.vector, "t", tok_t)
                        c.inc(nc.vector.tensor_copy(
                            out=x1_g[:, dcc, :], in_=gsl), "v")
                        g_tok[dcc % 2] = c.counts["v"]
                    gath_tok[0] = c.counts["v"]
                    c.wait(nc.tensor, "v", gath_tok[0])
                    pso2_tok[0] = c.counts["v"]
                    if e + 1 < NE:
                        build_P(e + 1)
                    c.wait(nc.vector, "v")
                    for dcc in range(DC):
                        c.inc(nc.vector.scalar_tensor_tensor(
                            out=acc[:, dcc, :], in0=wbcf[par][:],
                            scalar=b2T[:, dcc, e:e + 1], in1=acc[:, dcc, :],
                            op0=ALU.mult, op1=ALU.add), "v")
                    w1_pend = {i: issue_w1(e, i) for i in range(3)}
                    w2_pend = {i: issue_w2(e, i) for i in range(3)}
                    gel_toks = [0] * HC
                    for hs in range(8):
                        bi = hs % 3
                        dsem, tok_d = w1_pend.pop(hs)
                        c.wait(nc.tensor, dsem, tok_d)
                        for hh in range(4):
                            hc = hs * 4 + hh
                            ph = psO[(hc % 4) // 2][:, 512 * (hc % 2):512 * (hc % 2) + CAP]
                            if hc >= 4:
                                c.wait(nc.tensor, "s", gel_toks[hc - 4])
                            for dcc in range(DC):
                                ins = nc.tensor.matmul(
                                    ph,
                                    w1sl[bi][:, dcc, hh * 128:(hh + 1) * 128],
                                    x1_g[:, dcc, :],
                                    start=(dcc == 0), stop=(dcc == DC - 1))
                            tok_t = c.inc(ins, "t")
                            if hh == 3:
                                w1_buf[bi] = tok_t
                            c.wait(nc.scalar, "t", tok_t)
                            gel_toks[hc] = c.inc(nc.scalar.activation(
                                out=h_sb[:, hc, :], in_=ph, func=AF.Gelu,
                                bias=b1T[:, hc, e:e + 1]), "s")
                        if hs + 3 < 8:
                            w1_pend[hs + 3] = issue_w1(e, hs + 3)
                    for hs in range(8):
                        bi = hs % 3
                        dsem, tok_d = w2_pend.pop(hs)
                        if hs == 0:
                            c.wait(nc.tensor, "s", gel_toks[HC - 1])
                            c.wait(nc.tensor, "v")
                        c.wait(nc.tensor, dsem, tok_d)
                        for hh in range(4):
                            hc = hs * 4 + hh
                            for jt in range(JT):
                                for dh2 in range(2):
                                    ins = nc.tensor.matmul(
                                        psO[jt][:, dh2 * 512:(dh2 + 1) * 512],
                                        h_sb[:, hc, jt * 128:(jt + 1) * 128],
                                        w2sl[bi][:, hh, dh2 * 512:(dh2 + 1) * 512],
                                        start=(hc == 0), stop=(hc == HC - 1))
                        tok_pe = c.inc(ins, "t")
                        w2_buf[bi] = tok_pe
                        if hs + 3 < 8:
                            w2_pend[hs + 3] = issue_w2(e, hs + 3)
                    c.wait(nc.vector, "t")
                    c.wait(nc.vector, "v")
                    for jt in range(JT):
                        c.inc(nc.vector.tensor_copy(
                            out=out_g[:, jt, :], in_=psO[jt][:, 0:DM]), "v")
                    def sc_slot(sc_, dcc_):
                        if sc_ % 2 == 0:
                            return psO[dcc_ // 4][
                                :, (dcc_ % 4) * SCB:(dcc_ % 4) * SCB + SCB]
                        if dcc_ < 4:
                            return psO[2][:, dcc_ * SCB:(dcc_ + 1) * SCB] \
                                if SCB * 4 <= 1024 else psO[2][:, (dcc_ % 4) * SCB:]
                        u = (psU, psU2)[(dcc_ - 4) // 2]
                        return u[:, ((dcc_ - 4) % 2) * SCB:((dcc_ - 4) % 2) * SCB + SCB]
                    c.wait(nc.tensor, "v")
                    sc_add_tok = [0] * NSC
                    for sc in range(NSC):
                        if sc >= 2:
                            c.wait(nc.tensor, "v", sc_add_tok[sc - 2])
                        for dcc in range(DC):
                            for jc in range(JT):
                                ins = nc.tensor.matmul(
                                    sc_slot(sc, dcc),
                                    out_g[:, jc, dcc * 128:(dcc + 1) * 128],
                                    PwT[par][:, jc, sc * SCB:sc * SCB + SCB],
                                    start=(jc == 0), stop=(jc == JT - 1))
                        tok_t = c.inc(ins, "t")
                        c.wait(nc.vector, "t", tok_t)
                        c.wait(nc.vector, "v")
                        for dcc in range(DC):
                            c.inc(nc.vector.tensor_add(
                                acc[:, dcc, sc * SCB:sc * SCB + SCB],
                                acc[:, dcc, sc * SCB:sc * SCB + SCB],
                                sc_slot(sc, dcc)), "v")
                        sc_add_tok[sc] = c.counts["v"]
                    for _i in range(2):
                        psu_tok[_i] = max(psu_tok[_i], c.counts["v"])
                    pso2_tok[0] = max(pso2_tok[0], c.counts["v"])
                    c.wait(nc.tensor, "v")

                for tt2 in range(Q // 128):
                    pb = psO[tt2 % 2]
                    c.wait(nc.tensor, "v")
                    for dcc in range(DC):
                        ins = nc.tensor.transpose(
                            pb[:, dcc * 128:(dcc + 1) * 128],
                            acc[:, dcc, tt2 * 128:(tt2 + 1) * 128], eye32[:])
                    tok_t = c.inc(ins, "t")
                    c.wait(nc.vector, "t", tok_t)
                    c.wait(nc.vector, "v")
                    c.inc(nc.vector.bn_stats(bst[:, 0:6], pb[:, 0:512]), "v")
                    c.inc(nc.vector.bn_stats(bst[:, 6:12], pb[:, 512:1024]), "v")
                    c.wait(nc.vector, "v")
                    c.inc(nc.vector.bn_aggr(mv[:], bst[:]), "v")
                    c.wait(nc.scalar, "v")
                    c.inc(nc.scalar.activation(
                        out=sm0[:, 6:7], in_=mv[:, 1:2], func=AF.Ln,
                        bias=eps128[:]), "s")
                    c.wait(nc.scalar, "s")
                    c.inc(nc.scalar.activation(
                        out=sm0[:, 7:8], in_=sm0[:, 6:7], func=AF.Exp,
                        scale=-0.5), "s")
                    yb = ytile[0]
                    c.wait(nc.vector, "s")
                    c.wait(nc.vector, "dy0")
                    c.inc(nc.vector.tensor_scalar(
                        out=yb[:], in0=pb[:, 0:DM], scalar1=mv[:, 0:1],
                        scalar2=sm0[:, 7:8], op0=ALU.subtract, op1=ALU.mult), "v")
                    tok_v = c.counts["v"]
                    c.wait(nc.tensor, "v")
                    c.wait(nc.sync, "v", tok_v)
                    t0 = q0 + tt2 * 128
                    c.inc(nc.sync.dma_start(
                        out=y_h[t0:t0 + 128, :], in_=yb[:]), "dy0", 16)

                if "t_w" in tap_h:
                    c.wait(nc.sync, "v")
                    c.inc(nc.sync.dma_start(
                        out=tap_h["t_w"][:, q0:q0 + Q], in_=wT8[:]), "dct", 16)
                c.join(ALL_ENG)

        c.wait(nc.sync, "dy0")
        c.wait(nc.sync, "dy1")
    return nc


def kernel(L0_attn_x, L0_attn_Wq, L0_attn_Wk, L0_attn_Wv, L0_attn_Wo,
           L0_ffn_w_router, L0_ffn_w1, L0_ffn_b1, L0_ffn_w2, L0_ffn_b2):
    B, S, D = L0_attn_x.shape
    nc = build_nc(S=S)
    common = {
        "wq": np.ascontiguousarray(L0_attn_Wq, dtype=np.float32),
        "wk": np.ascontiguousarray(L0_attn_Wk, dtype=np.float32),
        "wv": np.ascontiguousarray(L0_attn_Wv, dtype=np.float32),
        "wo": np.ascontiguousarray(L0_attn_Wo, dtype=np.float32),
        "wr": np.ascontiguousarray(L0_ffn_w_router, dtype=np.float32),
        "w1": np.ascontiguousarray(L0_ffn_w1, dtype=np.float32),
        "b1": np.ascontiguousarray(L0_ffn_b1, dtype=np.float32),
        "w2": np.ascontiguousarray(L0_ffn_w2, dtype=np.float32),
        "b2": np.ascontiguousarray(L0_ffn_b2, dtype=np.float32),
    }
    in_maps = []
    for b in range(B):
        m = dict(common)
        m["x"] = np.ascontiguousarray(L0_attn_x[b], dtype=np.float32)
        in_maps.append(m)
    res = run_bass_kernel_spmd(nc, in_maps, core_ids=list(range(B)))
    return np.stack([res.results[b]["y"] for b in range(B)], axis=0)
